# version 04
# baseline (speedup 1.0000x reference)
from contextlib import ExitStack

import numpy as np

import concourse.bass as bass
import concourse.mybir as mybir
from concourse.bass_utils import run_bass_kernel_spmd

F32 = mybir.dt.float32
BF16 = mybir.dt.bfloat16
I16 = mybir.dt.int16
AF = mybir.ActivationFunctionType
ALU = mybir.AluOpType
AX = mybir.AxisListType

NCORES = 8
H = 2048
TC = 1024
HEADS = 8
CHUNK = 256
NCH = 4
MLP = 16384
EPS = 1e-5

_BETAS = np.linspace(-5.0, -9.0, 8).astype(np.float32).astype(np.float64)
GAMMA = 1.0 - np.exp2(_BETAS)
LNG = np.log(GAMMA)
CD = GAMMA ** CHUNK


class Ctr:
    __slots__ = ("v",)

    def __init__(self):
        self.v = 0

    def inc(self, d):
        self.v += d
        return self.v


class Prog:
    ENGINES = ("tensor", "vector", "scalar", "gpsimd", "sync")

    def __init__(self):
        self.ops = {e: [] for e in self.ENGINES}
        self.sem = {}
        self.ctr = {}

    def add_sem(self, name, handle):
        self.sem[name] = handle
        self.ctr[name] = Ctr()

    def emit(self, eng, fn):
        self.ops[eng].append(fn)

    def wait(self, eng, sem_name, val):
        if val is None or val <= 0:
            return
        s = self.sem[sem_name]
        self.emit(eng, lambda e, s=s, v=val: e.wait_ge(s, v))
        self.emit(eng, lambda e: e.nop(nofuse=True))

    def fence(self, eng, sem_name):
        self.wait(eng, sem_name, self.ctr[sem_name].v)

    def dma(self, eng, sem_name, out, in_, **kw):
        s = self.sem[sem_name]
        tok = self.ctr[sem_name].inc(16)
        self.emit(eng, lambda e, s=s, o=out, i=in_, kw=kw:
                  e.dma_start(out=o, in_=i, **kw).then_inc(s, 16))
        return tok

    def tag(self, eng, sem_name):
        s = self.sem[sem_name]
        tok = self.ctr[sem_name].inc(1)
        prev = self.ops[eng].pop()
        self.emit(eng, lambda e, p=prev, s=s: p(e).then_inc(s, 1))
        return tok


DBG = []
_LAST = {}


def build(dbg=()):
    nc = bass.Bass("TRN2", target_bir_lowering=False)

    def din(name, shape, dt):
        return nc.dram_tensor(name, list(shape), dt, kind="ExternalInput").ap()

    x_in = din("x", (TC, H), F32)
    xa_in = din("xa", (TC, H), F32)
    wq_in = din("wq", (H, H), F32)
    wk_in = din("wk", (H, H), F32)
    wv_in = din("wv", (H, H), F32)
    b_ins = {"q": din("bq", (1, H), F32), "k": din("bk", (1, H), F32),
             "v": din("bv", (1, H), F32), "ak": din("bak", (1, H), F32),
             "av": din("bav", (1, H), F32)}
    gp_in = din("gatep", (1024, 8192), I16)
    up_in = din("upp", (1024, 8192), I16)
    dp_in = din("downp", (8192, 1024), I16)
    gs_in = din("gates", (64, MLP), F32)
    us_in = din("ups", (64, MLP), F32)
    ds_in = din("downs", (512, H), F32)

    out_t = nc.dram_tensor("out", [TC, H], F32, kind="ExternalOutput").ap()

    def dint(name, shape, dt):
        return nc.dram_tensor(name, list(shape), dt).ap()

    qT_d = dint("qT_d", (128, 16 * TC), BF16)
    kT_d = dint("kT_d", (128, 16 * TC), BF16)
    kt_d = dint("kt_d", (128, 8 * H), BF16)
    v_d = dint("v_d", (128, 8 * H), BF16)
    akt_d = dint("akt_d", (128, 8 * H), BF16)
    av_d = dint("av_d", (128, 8 * H), BF16)
    s_d = dint("s_d", (128, 8 * H), F32)
    acc_d = dint("acc_d", (128, 8 * H), F32)
    Gg = dint("Gg", (H, MLP), BF16)
    Gu = dint("Gu", (H, MLP), BF16)
    Gd = dint("Gd", (MLP, H), BF16)
    gsh_d = dint("gsh_d", (64, MLP), BF16)
    ush_d = dint("ush_d", (64, MLP), BF16)
    dsh_d = dint("dsh_d", (512, H), BF16)

    with ExitStack() as ctx:
        sbt = lambda n, s, d: ctx.enter_context(nc.sbuf_tensor(n, list(s), d))
        pst = lambda n, s, d: ctx.enter_context(nc.psum_tensor(n, list(s), d))

        xnT = sbt("xnT", (128, 16 * TC), BF16)
        wbig = sbt("wbig", (128, 8 * 1024), BF16)
        scr = sbt("scr", (128, 4096), F32)
        scr2 = sbt("scr2", (128, 2048), F32)
        pk_sb = sbt("pk_sb", (128, 2 * 1024), I16)
        dreg = [sbt(f"dreg{i}", (128, 1024), I16) for i in range(8)]
        slab = sbt("slab_sb", (128, 2 * 1024), BF16)
        srep = sbt("srep_sb", (128, 2 * 1024), BF16)
        qe = sbt("qe", (128, 2 * 1024), BF16)
        Ssb = sbt("Ssb", (128, 16 * 256), F32)
        Sbf = sbt("Sbf", (128, 2 * 256), BF16)
        ksb = sbt("ksb", (128, 2 * 256), BF16)
        attnM = sbt("attnM", (128, 2 * 256), BF16)
        qs_sb = sbt("qs_sb", (128, 2 * 256), BF16)
        gu_sb = [sbt(f"gu{i}", (128, 16 * 512), BF16) for i in range(2)]
        itr = sbt("itr", (128, 16 * 1024), BF16)
        xanT = itr
        cons = sbt("cons", (128, 8), F32)
        csb = sbt("csb", (128, 8 * 256), F32)
        kvc = sbt("kvc", (128, 16), F32)
        dmsk = sbt("dmsk", (128, 2 * 256), F32)
        ident = sbt("ident", (128, 128), BF16)
        onesb = sbt("onesb", (128, 512), BF16)
        brow = sbt("brow", (128, H), BF16)
        iop = sbt("iop", (128, 2), F32)
        lst = sbt("lst", (128, 16), F32)
        fls = sbt("fls", (128, 2 * 512), F32)

        dtt = scr2[:, 0:512]
        ioc = scr2[:, 512:768]

        pA = [pst(f"pA{i}", (128, 512), F32) for i in range(4)]
        pB = [pst(f"pB{i}", (128, 1024), F32) for i in range(2)]
        p_upd = pB[0][:, 0:512]

        P = Prog()
        for s in ("sLx", "sLp0", "sLp1", "sGs0", "sGs1", "sLg0", "sLg1",
                  "sLw", "sGw0", "sGw1", "sGb", "sAq0", "sAq1", "sAs",
                  "sGa0", "sGa1", "sLo",
                  "sPE", "sDV", "sAC", "sGP"):
            P.add_sem(s, ctx.enter_context(nc.semaphore(s)))

        block = ctx.enter_context(nc.Block())
        E, W, D, T, FE = P.emit, P.wait, P.dma, P.tag, P.fence

        mask_free = [0]
        attn_free = [0]
        ks_free = [0]
        upd_free = [0]
        gelu_free = [0, 0]
        fls_dve = [0, 0, 0, 0]
        fls_gd = [0, 0]

        for col, val in {0: -4.0, 1: -16.0, 2: 1.0, 3: EPS}.items():
            E("gpsimd", lambda e, c=col, v=val: e.memset(cons[:, c:c + 1], v))
        E("gpsimd", lambda e: e.memset(onesb[0:1, :], 1.0))
        E("gpsimd", lambda e: e.memset(Ssb[:, :], 0.0))
        E("gpsimd", lambda e: e.iota(ioc, pattern=[[1, 256]], base=0,
                                     channel_multiplier=0,
                                     allow_small_or_imprecise_dtypes=True))
        E("gpsimd", lambda e: e.iota(iop[:, 0:1], pattern=[[0, 1]], base=0,
                                     channel_multiplier=1,
                                     allow_small_or_imprecise_dtypes=True))
        for jc in range(2):
            E("gpsimd", lambda e, jc=jc: e.iota(
                dmsk[:, jc * 256:(jc + 1) * 256], pattern=[[1, 256]],
                base=-128 * jc, channel_multiplier=-1,
                allow_small_or_imprecise_dtypes=True))
            t_io = T("gpsimd", "sGP")
            W("gpsimd", "sGP", t_io)
            E("gpsimd", lambda e, jc=jc: e.affine_select(
                dmsk[:, jc * 256:(jc + 1) * 256], dmsk[:, jc * 256:(jc + 1) * 256],
                pattern=[[1, 256]], compare_op=ALU.is_ge, fill=1.0e9,
                base=-128 * jc, channel_multiplier=-1))
        E("gpsimd", lambda e: e.iota(scr2[:, 768:896], pattern=[[1, 128]], base=0,
                                     channel_multiplier=-1,
                                     allow_small_or_imprecise_dtypes=True))
        tok_gp0 = T("gpsimd", "sGP")

        W("vector", "sGP", tok_gp0)
        E("vector", lambda e: e.tensor_scalar(scr2[:, 896:1024], scr2[:, 768:896],
                                              0.0, None, ALU.is_equal))
        t_ieq = T("vector", "sDV")
        W("vector", "sDV", t_ieq)
        E("vector", lambda e: e.tensor_copy(ident[:, :], scr2[:, 896:1024]))
        tok_ident = T("vector", "sDV")

        W("scalar", "sGP", tok_gp0)
        for h in range(HEADS):
            E("scalar", lambda e, h=h: e.activation(
                csb[:, h * 256:(h + 1) * 256], ioc, AF.Exp, scale=float(LNG[h])))
        for tsl in range(2):
            for h in range(HEADS):
                col = tsl * 8 + h
                E("scalar", lambda e, h=h, col=col: e.activation(
                    kvc[:, col:col + 1], iop[:, 0:1], AF.Exp, scale=-float(LNG[h])))
        tok_exp = T("scalar", "sAC")
        W("vector", "sAC", tok_exp)
        for h in range(HEADS):
            E("vector", lambda e, h=h: e.tensor_scalar(
                csb[:, h * 256:(h + 1) * 256], csb[:, h * 256:(h + 1) * 256],
                float(GAMMA[h]), None, ALU.mult))
            E("vector", lambda e: e.nop())
        for tsl in range(2):
            for h in range(HEADS):
                col = tsl * 8 + h
                mv = float(GAMMA[h] ** (255 - 128 * tsl))
                E("vector", lambda e: e.nop())
                E("vector", lambda e, col=col, mv=mv: e.tensor_scalar(
                    kvc[:, col:col + 1], kvc[:, col:col + 1], mv, None, ALU.mult))
        T("vector", "sDV")

        last_sc_st = 0
        for src, dst, rows, cols in ((gs_in, gsh_d, 64, MLP),
                                     (us_in, ush_d, 64, MLP),
                                     (ds_in, dsh_d, 512, H)):
            for r0 in range(0, rows, 64):
                rr = min(64, rows - r0)
                for c0 in range(0, cols, 4096):
                    cc = min(4096, cols - c0)
                    FE("sync", "sAs")
                    t_ld = D("sync", "sLx", scr[0:rr, 0:cc],
                             src[r0:r0 + rr, c0:c0 + cc])
                    FE("vector", "sLx")
                    E("vector", lambda e, rr=rr, cc=cc: e.tensor_scalar(
                        qe[0:rr, 0:min(cc, 2048)] if cc <= 2048 else
                        wbig[0:rr, 0:cc], scr[0:rr, 0:cc], 0.5, None, ALU.mult))
                    t_h = T("vector", "sDV")
                    E("vector", lambda e: e.nop())
                    W("scalar", "sDV", t_h)
                    buf = qe[0:rr, 0:cc] if cc <= 2048 else wbig[0:rr, 0:cc]
                    last_sc_st = D("scalar", "sAs", dst[r0:r0 + rr, c0:c0 + cc],
                                   buf)
        tok_scales_st = last_sc_st
        tok_scales_dve = P.ctr["sDV"].v

        def load_bias(name, wait_pe=None):
            if wait_pe:
                W("gpsimd", "sPE", wait_pe)
            return D("gpsimd", "sGb", brow[0:1, 0:H], b_ins[name])

        gsh_aps = {"g": gsh_d, "u": ush_d}
        pk_aps = {"g": gp_in, "u": up_in}
        G_aps = {"g": Gg, "u": Gu}
        stage_tok = {(m, g): [] for m in ("g", "u", "d") for g in range(8)}
        ds_pk_free = [0, 0]
        ds_srep_free = [0, 0]
        ds_slab_free = [0, 0]
        ds_slot = [0]
        FD = 1024

        def emit_chain_pair(planes, pk_ap):
            for j, (sh, _, _) in enumerate(planes):
                r = dreg[4 * j]
                if sh:
                    E("vector", lambda e, r=r, sh=sh: e.tensor_scalar(
                        r[:, :], pk_ap, sh, 15,
                        ALU.logical_shift_right, ALU.bitwise_and))
                else:
                    E("vector", lambda e, r=r: e.tensor_scalar(
                        r[:, :], pk_ap, 15, None, ALU.bitwise_and))
            for j in range(len(planes)):
                c0, m = dreg[4 * j], dreg[4 * j + 1]
                E("vector", lambda e, c0=c0, m=m: e.tensor_scalar(
                    m[:, :], c0[:, :], 7, None, ALU.bitwise_and))
            tok_m = T("vector", "sDV")
            W("scalar", "sDV", tok_m)
            for j in range(len(planes)):
                m, a, b = dreg[4 * j + 1], dreg[4 * j + 2], dreg[4 * j + 3]
                E("scalar", lambda e, m=m, a=a: e.activation(
                    a[:, :], m[:, :], AF.Identity, scale=2.0, bias=cons[:, 0:1]))
                E("scalar", lambda e, m=m, b=b: e.activation(
                    b[:, :], m[:, :], AF.Identity, scale=4.0, bias=cons[:, 1:2]))
            tok_ab = T("scalar", "sAC")
            W("vector", "sAC", tok_ab)
            for j in range(len(planes)):
                a, b = dreg[4 * j + 2], dreg[4 * j + 3]
                E("vector", lambda e, a=a, b=b: e.tensor_tensor(
                    a[:, :], a[:, :], b[:, :], ALU.max))
            for j in range(len(planes)):
                m, a = dreg[4 * j + 1], dreg[4 * j + 2]
                E("vector", lambda e, m=m, a=a: e.tensor_tensor(
                    a[:, :], m[:, :], a[:, :], ALU.max))
            for j in range(len(planes)):
                c0, b = dreg[4 * j], dreg[4 * j + 3]
                E("vector", lambda e, c0=c0, b=b: e.tensor_scalar(
                    b[:, :], c0[:, :], 3, 1,
                    ALU.logical_shift_right, ALU.bitwise_and))
            for j in range(len(planes)):
                b = dreg[4 * j + 3]
                E("vector", lambda e, b=b: e.tensor_scalar(
                    b[:, :], b[:, :], -2, 1, ALU.mult, ALU.add))
            for j in range(len(planes)):
                a, b = dreg[4 * j + 2], dreg[4 * j + 3]
                E("vector", lambda e, a=a, b=b: e.tensor_tensor(
                    a[:, :], a[:, :], b[:, :], ALU.mult))
            for j, (_, slab_sl, srep_sl) in enumerate(planes):
                a = dreg[4 * j + 2]
                E("vector", lambda e, a=a, o=slab_sl, sr=srep_sl:
                  e.tensor_tensor(o, a[:, :], sr, ALU.mult))
            return T("vector", "sDV")

        def emit_unit_gu(mat, g, c, ch):
            b = g // 4
            frq = 2048 * (g % 4) + 1024 * ch
            n0 = 2048 * g + 1024 * ch
            slot = ds_slot[0]
            ds_slot[0] ^= 1
            W("sync", "sDV", ds_pk_free[slot])
            pk_sl = pk_sb[:, slot * 1024: slot * 1024 + 1024]
            t_pk = D("sync", "sLp%d" % slot, pk_sl,
                     pk_aps[mat][c * 128:(c + 1) * 128, frq:frq + 1024])
            if ds_srep_free[slot] == 0:
                W("gpsimd", "sAs", tok_scales_st)
            W("gpsimd", "sDV", ds_srep_free[slot])
            sr_sl = srep[:, slot * 1024: slot * 1024 + 1024]
            t_sr = D("gpsimd", "sGs%d" % slot, sr_sl,
                     gsh_aps[mat][8 * c:8 * c + 8, n0:n0 + 1024]
                     .unsqueeze(1).broadcast_to([8, 16, 1024]))
            FE("vector", "sLp%d" % slot)
            FE("vector", "sGs%d" % slot)
            FE("vector", "sAs")
            planes = [(8 * b, slab[:, 0:1024], sr_sl),
                      (8 * b + 4, slab[:, 1024:2048], sr_sl)]
            tok_w = emit_chain_pair(planes, pk_sl)
            ds_pk_free[slot] = tok_w
            ds_srep_free[slot] = tok_w
            W("scalar", "sDV", tok_w)
            Gm = G_aps[mat]
            ds_slab_free[0] = D("scalar", "sAs",
                                Gm[256 * c: 256 * c + 256: 2, n0:n0 + 1024],
                                slab[:, 0:1024])
            ds_slab_free[1] = D("scalar", "sAs",
                                Gm[256 * c + 1: 256 * c + 256: 2, n0:n0 + 1024],
                                slab[:, 1024:2048])
            stage_tok[(mat, g)].append(ds_slab_free[1])

        def emit_unit_down(g, cr, pairi):
            cp = 8 * g + cr
            slot = ds_slot[0]
            ds_slot[0] ^= 1
            W("sync", "sDV", ds_pk_free[slot])
            pk_sl = pk_sb[:, slot * 1024: slot * 1024 + 1024]
            t_pk = D("sync", "sLp%d" % slot, pk_sl,
                     dp_in[cp * 128:(cp + 1) * 128, 0:1024])
            if ds_srep_free[slot] == 0:
                W("gpsimd", "sAs", tok_scales_st)
            W("gpsimd", "sDV", ds_srep_free[slot])
            sr_sl = srep[:, slot * 1024: slot * 1024 + 1024]
            t_sr = D("gpsimd", "sGs%d" % slot, sr_sl,
                     dsh_d[8 * cp:8 * cp + 8, pairi * 1024:(pairi + 1) * 1024]
                     .unsqueeze(1).broadcast_to([8, 16, 1024]))
            FE("vector", "sLp%d" % slot)
            FE("vector", "sGs%d" % slot)
            FE("vector", "sAs")
            planes = [(8 * pairi, slab[:, 0:1024], sr_sl),
                      (8 * pairi + 4, slab[:, 1024:2048], sr_sl)]
            tok_w = emit_chain_pair(planes, pk_sl)
            ds_pk_free[slot] = tok_w
            ds_srep_free[slot] = tok_w
            W("scalar", "sDV", tok_w)
            t_last = 0
            for nib in range(2):
                sl = slab[:, nib * 1024:(nib + 1) * 1024]
                t_last = D("scalar", "sAs",
                           Gd[256 * cp + nib: 256 * cp + 256: 2,
                              pairi * 1024: (pairi + 1) * 1024], sl)
            ds_slab_free[0] = t_last
            ds_slab_free[1] = t_last
            stage_tok[("d", g)].append(t_last)

        dec_units = []
        for g in range(8):
            for c in range(8):
                for ch in range(2):
                    dec_units.append(("g", g, c, ch))
                    dec_units.append(("u", g, c, ch))
            for cr in range(8):
                for pairi in range(2):
                    dec_units.append(("d", g, cr, pairi))
        dec_pos = [0]
        dec_on = [False]

        def pump(n):
            if not dec_on[0]:
                return
            for _ in range(n):
                if dec_pos[0] >= len(dec_units):
                    return
                kind, g, c, x2 = dec_units[dec_pos[0]]
                dec_pos[0] += 1
                if kind == "d":
                    emit_unit_down(g, c, x2)
                else:
                    emit_unit_gu(kind, g, c, x2)

        x_sl = scr[:, 0:2048]
        sq_sl = scr[:, 2048:4096]
        xnbf = wbig[:, 0:2048]
        slot_free = tok_scales_dve
        xnbf_pe_free = 0
        FE("vector", "sAs")
        W("scalar", "sDV", tok_scales_dve)
        FE("scalar", "sAs")
        tr_evac_tok = {0: 0, 1: 0}
        tpsum = [pB[0][:, 0:64].bitcast(BF16), pB[1][:, 0:64].bitcast(BF16)]

        for kind, tt in [("m", t) for t in range(8)] + [("a", t) for t in range(8)]:
            src = x_in if kind == "m" else xa_in
            W("sync", "sDV", slot_free)
            t_ld = D("sync", "sLx", x_sl, src[tt * 128:(tt + 1) * 128, :])
            FE("vector", "sLx")
            FE("scalar", "sLx")
            E("vector", lambda e: e.tensor_reduce(lst[:, 0:1], x_sl, AX.X, ALU.add))
            E("scalar", lambda e: e.activation(sq_sl, x_sl, AF.Square))
            t_sq = T("scalar", "sAC")
            W("vector", "sAC", t_sq)
            E("vector", lambda e: e.tensor_reduce(lst[:, 1:2], sq_sl, AX.X, ALU.add))
            E("vector", lambda e: e.tensor_scalar(lst[:, 2:3], lst[:, 0:1],
                                                  1.0 / H, None, ALU.mult))
            t_ms = T("vector", "sDV")
            W("scalar", "sDV", t_ms)
            E("scalar", lambda e: e.activation(lst[:, 3:4], lst[:, 1:2],
                                               AF.Identity, scale=1.0 / H))
            E("scalar", lambda e: e.activation(lst[:, 4:5], lst[:, 2:3],
                                               AF.Square))
            t_m2 = T("scalar", "sAC")
            W("vector", "sAC", t_m2)
            E("vector", lambda e: e.tensor_tensor(lst[:, 5:6], lst[:, 3:4],
                                                  lst[:, 4:5], ALU.subtract))
            t_var = T("vector", "sDV")
            W("scalar", "sDV", t_var)
            E("scalar", lambda e: e.activation(lst[:, 6:7], lst[:, 5:6],
                                               AF.Sqrt, bias=cons[:, 3:4]))
            t_std = T("scalar", "sAC")
            W("vector", "sAC", t_std)
            E("vector", lambda e: e.reciprocal(lst[:, 7:8], lst[:, 6:7]))
            t_rc = T("vector", "sDV")
            W("vector", "sDV", t_rc)
            W("vector", "sPE", xnbf_pe_free)
            E("vector", lambda e: e.tensor_scalar(xnbf, x_sl, lst[:, 2:3],
                                                  lst[:, 7:8], ALU.subtract,
                                                  ALU.mult))
            t_xn = T("vector", "sDV")
            slot_free = t_xn
            if kind == "m" and tt == 0 and "xnbf0" in dbg:
                dbg_x0 = nc.dram_tensor("dbg_x0", [128, 2048], F32,
                                        kind="ExternalOutput").ap()
                dbg_xb0 = nc.dram_tensor("dbg_xnbf0", [128, 2048], BF16,
                                         kind="ExternalOutput").ap()
                dbg_lst0 = nc.dram_tensor("dbg_lst0", [128, 16], F32,
                                          kind="ExternalOutput").ap()
                W("sync", "sDV", t_xn)
                tt_x0 = D("sync", "sLo", dbg_x0, x_sl)
                tt_xb = D("sync", "sLo", dbg_xb0, xnbf)
                tt_ls = D("sync", "sLo", dbg_lst0, lst[:, 0:16])
                FE("vector", "sLo")
                FE("tensor", "sLo")
            dstT = xnT if kind == "m" else xanT
            W("tensor", "sDV", t_xn)
            W("tensor", "sDV", tok_ident)
            for kc in range(16):
                pp = kc % 2
                W("tensor", "sDV", tr_evac_tok[pp])
                E("tensor", lambda e, kc=kc, pp=pp: e.transpose(
                    tpsum[pp], xnbf[:, kc * 128:(kc + 1) * 128], ident[:, :]))
                t_tr = T("tensor", "sPE")
                W("vector", "sPE", t_tr)
                E("vector", lambda e, dstT=dstT, kc=kc, tt=tt, pp=pp:
                  e.tensor_copy(
                      dstT[:, kc * TC + tt * 128: kc * TC + (tt + 1) * 128],
                      tpsum[pp]))
                tr_evac_tok[pp] = T("vector", "sDV")
            xnbf_pe_free = P.ctr["sPE"].v
        xnT_ready = max(tr_evac_tok[0], tr_evac_tok[1])

        if "xnT_e" in dbg:
            dbg_xe = nc.dram_tensor("dbg_xnT_e", [128, 16 * TC], BF16,
                                    kind="ExternalOutput").ap()
            W("sync", "sDV", xnT_ready)
            D("sync", "sLo", dbg_xe, xnT[:, :])
        FE("gpsimd", "sAs")
        dec_on[0] = True

        pB_free = [xnT_ready, xnT_ready]
        bias_tok = [0]
        FE("vector", "sAs")

        def emit_fm(w_ap, bias_slot, dst_d):
            wslot_free = [P.ctr["sPE"].v, P.ctr["sPE"].v]
            for ocg in range(8):
                ws = ocg % 2
                W("gpsimd", "sPE", wslot_free[ws])
                t_w = D("gpsimd", "sGw%d" % ws, gu_sb[ws][:, 0:16 * 256].rearrange(
                    "p (c n) -> p c n", n=256),
                    w_ap.rearrange("(c p) n -> p c n", p=128)
                    [:, :, ocg * 256:(ocg + 1) * 256])
                pump(1)
                for oc2 in range(2):
                    oc = 2 * ocg + oc2
                    FE("tensor", "sGw%d" % ws)
                    FE("tensor", "sGb")
                    W("tensor", "sDV", pB_free[oc2])
                    for tb in range(2):
                        for kc in range(16):
                            E("tensor", lambda e, oc2=oc2, tb=tb, kc=kc, ws=ws:
                              e.matmul(pB[oc2][:, tb * 512:(tb + 1) * 512],
                                       gu_sb[ws][:, kc * 256 + oc2 * 128:
                                                 kc * 256 + oc2 * 128 + 128],
                                       xnT[:, kc * TC + tb * 512:
                                           kc * TC + (tb + 1) * 512],
                                       start=(kc == 0), stop=False))
                        E("tensor", lambda e, oc2=oc2, tb=tb, oc=oc, bs=bias_slot:
                          e.matmul(pB[oc2][:, tb * 512:(tb + 1) * 512],
                                   brow[0:1, oc * 128: oc * 128 + 128],
                                   onesb[0:1, 0:512], start=False, stop=True))
                    t_pe = T("tensor", "sPE")
                    W("vector", "sPE", t_pe)
                    FE("vector", "sAq%d" % oc2)
                    E("vector", lambda e, oc2=oc2: e.tensor_copy(
                        qe[:, oc2 * 1024:(oc2 + 1) * 1024], pB[oc2][:, :]))
                    t_ev = T("vector", "sDV")
                    pB_free[oc2] = t_ev
                    W("scalar", "sDV", t_ev)
                    D("scalar", "sAq%d" % oc2,
                      dst_d[:, oc * 1024:(oc + 1) * 1024],
                      qe[:, oc2 * 1024:(oc2 + 1) * 1024])
                    wslot_free[ws] = t_pe

        bias_tok[0] = load_bias("q")
        emit_fm(wq_in, 0, qT_d)
        bias_tok[0] = load_bias("k", wait_pe=P.ctr["sPE"].v)
        emit_fm(wk_in, 0, kT_d)

        tm_wfree = [P.ctr["sPE"].v] * 16

        def emit_tm(w_ap, bias_main, bias_alien, main_dst, alien_dst):
            for dq in range(4):
                w_toks = []
                for kc in range(16):
                    W("gpsimd", "sPE", tm_wfree[kc])
                    w_toks.append(D("gpsimd", "sGw0",
                                    wbig[:, kc * 512:(kc + 1) * 512],
                                    w_ap.rearrange("(c p) n -> p c n", p=128)
                                    [:, kc, dq * 512:(dq + 1) * 512]))
                    if kc % 4 == 3:
                        pump(1)
                for srcT, bname, dst_d in ((xnT, bias_main, main_dst),
                                           (xanT, bias_alien, alien_dst)):
                    bias_tok[0] = load_bias(bname, wait_pe=P.ctr["sPE"].v)
                    for tsub in range(8):
                        o2 = tsub % 2
                        reg = pB[o2][:, 0:512]
                        W("tensor", "sDV", pB_free[o2])
                        FE("tensor", "sGb")
                        FE("tensor", "sGw0")
                        for kc in range(16):
                            E("tensor", lambda e, reg=reg, kc=kc, tsub=tsub,
                              srcT=srcT:
                              e.matmul(reg,
                                       srcT[:, kc * TC + tsub * 128:
                                            kc * TC + tsub * 128 + 128],
                                       wbig[:, kc * 512:(kc + 1) * 512],
                                       start=(kc == 0), stop=False))
                        E("tensor", lambda e, reg=reg, dq=dq:
                          e.matmul(reg, onesb[0:1, 0:128],
                                   brow[0:1, dq * 512:(dq + 1) * 512],
                                   start=False, stop=True))
                        t_pe = T("tensor", "sPE")
                        W("vector", "sPE", t_pe)
                        FE("vector", "sAq%d" % o2)
                        E("vector", lambda e, o2=o2, reg=reg: e.tensor_copy(
                            qe[:, o2 * 1024: o2 * 1024 + 512], reg))
                        t_ev = T("vector", "sDV")
                        pB_free[o2] = t_ev
                        W("scalar", "sDV", t_ev)
                        D("scalar", "sAq%d" % o2,
                          dst_d[:, tsub * 2048 + dq * 512:
                                tsub * 2048 + (dq + 1) * 512],
                          qe[:, o2 * 1024: o2 * 1024 + 512])
                    pump(2)
                for kc in range(16):
                    tm_wfree[kc] = P.ctr["sPE"].v

        emit_tm(wk_in, "k", "ak", kt_d, akt_d)
        emit_tm(wv_in, "v", "av", v_d, av_d)

        qT_v = qT_d.rearrange("p (c t) -> p c t", t=TC)
        kT_v = kT_d.rearrange("p (c t) -> p c t", t=TC)

        pA_free = [pB_free[0]] * 4
        upd_free[0] = pB_free[0]
        scan_pe_free = P.ctr["sPE"].v
        scan_dve_free = P.ctr["sDV"].v
        last_upd = P.ctr["sDV"].v

        for phase, src_kt, src_v in (("a", akt_d, av_d), ("m", kt_d, v_d)):
            for c in range(NCH):
                FE("sync", "sAq0")
                FE("sync", "sAq1")
                W("sync", "sPE", scan_pe_free)
                W("sync", "sDV", last_upd)
                t_kt = D("sync", "sLx", wbig[:, 0:4096],
                         src_kt[:, (2 * c) * 2048:(2 * c + 2) * 2048])
                t_v = D("sync", "sLx", wbig[:, 4096:8192],
                        src_v[:, (2 * c) * 2048:(2 * c + 2) * 2048])
                t_q = t_k2 = None
                if phase == "m":
                    t_q = D("sync", "sLx", gu_sb[0][:, 0:4096].rearrange(
                        "p (k t) -> p k t", t=256),
                        qT_v[:, :, c * 256:(c + 1) * 256])
                    t_k2 = D("sync", "sLx", gu_sb[0][:, 4096:8192].rearrange(
                        "p (k t) -> p k t", t=256),
                        kT_v[:, :, c * 256:(c + 1) * 256])
                pump(1)
                for h in range(HEADS):
                    hp = h % 2
                    cdh = float(CD[h])
                    if phase == "m":
                        W("vector", "sDV", last_upd)
                        E("vector", lambda e, h=h: e.tensor_copy(
                            Sbf[:, :], Ssb[:, h * 512:(h + 1) * 512]))
                        t_sbf = T("vector", "sDV")
                        W("scalar", "sDV", mask_free[0])
                        for jc in range(2):
                            E("scalar", lambda e, h=h, jc=jc: e.activation(
                                dtt[:, jc * 256:(jc + 1) * 256],
                                dmsk[:, jc * 256:(jc + 1) * 256], AF.Exp,
                                scale=float(LNG[h])))
                        t_dt = T("scalar", "sAC")
                        FE("tensor", "sLx")
                        W("tensor", "sDV", pA_free[hp])
                        for jc in range(2):
                            for dc in range(2):
                                E("tensor", lambda e, h=h, jc=jc, dc=dc, hp=hp:
                                  e.matmul(pA[hp][:, jc * 256:(jc + 1) * 256],
                                           gu_sb[0][:, 4096 + (2 * h + dc) * 256 +
                                                 jc * 128:
                                                 4096 + (2 * h + dc) * 256 +
                                                 jc * 128 + 128],
                                           gu_sb[0][:, (2 * h + dc) * 256:
                                                 (2 * h + dc) * 256 + 256],
                                           start=(dc == 0), stop=(dc == 1)))
                        t_at = T("tensor", "sPE")
                        W("vector", "sPE", t_at)
                        W("vector", "sPE", attn_free[0])
                        W("vector", "sAC", t_dt)
                        for jc in range(2):
                            E("vector", lambda e, jc=jc, hp=hp: e.tensor_tensor(
                                attnM[:, jc * 256:(jc + 1) * 256],
                                pA[hp][:, jc * 256:(jc + 1) * 256],
                                dtt[:, jc * 256:(jc + 1) * 256], ALU.mult))
                            E("vector", lambda e, h=h, jc=jc: e.tensor_tensor(
                                qs_sb[:, jc * 256:(jc + 1) * 256],
                                gu_sb[0][:, (2 * h + jc) * 256:
                                      (2 * h + jc) * 256 + 256],
                                csb[:, h * 256:(h + 1) * 256], ALU.mult))
                        t_mq = T("vector", "sDV")
                        pA_free[hp] = t_mq
                        mask_free[0] = t_mq
                        W("tensor", "sDV", t_mq)
                        W("tensor", "sDV", t_sbf)
                        W("tensor", "sDV", pA_free[2 + hp])
                        for ih in range(2):
                            for jc in range(2):
                                E("tensor", lambda e, ih=ih, jc=jc, h=h, hp=hp:
                                  e.matmul(pA[2 + hp][:, ih * 256:(ih + 1) * 256],
                                           attnM[:, jc * 256 + ih * 128:
                                                 jc * 256 + ih * 128 + 128],
                                           wbig[:, 4096 + jc * 2048 + h * 256:
                                                4096 + jc * 2048 + h * 256 + 256],
                                           start=(jc == 0), stop=False))
                            for dc in range(2):
                                E("tensor", lambda e, ih=ih, dc=dc, hp=hp:
                                  e.matmul(pA[2 + hp][:, ih * 256:(ih + 1) * 256],
                                           qs_sb[:, dc * 256 + ih * 128:
                                                 dc * 256 + ih * 128 + 128],
                                           Sbf[:, dc * 256:(dc + 1) * 256],
                                           start=False, stop=(dc == 1)))
                        t_out = T("tensor", "sPE")
                        attn_free[0] = t_out
                        W("vector", "sPE", t_out)
                        FE("vector", "sAs")
                        for ih in range(2):
                            E("vector", lambda e, ih=ih, h=h, hp=hp:
                              e.tensor_copy(
                                  scr[:, ih * 2048 + h * 256:
                                      ih * 2048 + h * 256 + 256],
                                  pA[2 + hp][:, ih * 256:(ih + 1) * 256]))
                        t_sev = T("vector", "sDV")
                        pA_free[2 + hp] = t_sev
                    FE("vector", "sLx")
                    W("vector", "sPE", ks_free[0])
                    for tsl in range(2):
                        E("vector", lambda e: e.nop())
                        E("vector", lambda e, tsl=tsl, h=h: e.tensor_scalar(
                            ksb[:, tsl * 256:(tsl + 1) * 256],
                            wbig[:, tsl * 2048 + h * 256:
                                 tsl * 2048 + h * 256 + 256],
                            kvc[:, (tsl * 8 + h):(tsl * 8 + h) + 1], None,
                            ALU.mult))
                    t_ks = T("vector", "sDV")
                    W("tensor", "sDV", t_ks)
                    W("tensor", "sDV", upd_free[0])
                    FE("tensor", "sLx")
                    for dh in range(2):
                        for tsl in range(2):
                            E("tensor", lambda e, dh=dh, tsl=tsl, h=h:
                              e.matmul(p_upd[:, dh * 256:(dh + 1) * 256],
                                       ksb[:, tsl * 256 + dh * 128:
                                           tsl * 256 + dh * 128 + 128],
                                       wbig[:, 4096 + tsl * 2048 + h * 256:
                                            4096 + tsl * 2048 + h * 256 + 256],
                                       start=(tsl == 0), stop=(tsl == 1)))
                    t_upd = T("tensor", "sPE")
                    ks_free[0] = t_upd
                    W("vector", "sPE", t_upd)
                    for dh in range(2):
                        E("vector", lambda e: e.nop())
                        E("vector", lambda e, dh=dh, h=h, cdh=cdh:
                          e.scalar_tensor_tensor(
                              Ssb[:, h * 512 + dh * 256:
                                  h * 512 + dh * 256 + 256],
                              Ssb[:, h * 512 + dh * 256:
                                  h * 512 + dh * 256 + 256],
                              cdh, p_upd[:, dh * 256:(dh + 1) * 256],
                              ALU.mult, ALU.add))
                    last_upd = T("vector", "sDV")
                    upd_free[0] = last_upd
                if phase == "m":
                    W("scalar", "sDV", last_upd)
                    D("scalar", "sAs",
                      s_d[:, (2 * c) * 2048:(2 * c + 2) * 2048],
                      scr[:, 0:4096])
                scan_pe_free = P.ctr["sPE"].v
                pump(2)

        Gg_v = Gg.rearrange("(c p) n -> p c n", p=128)
        Gu_v = Gu.rearrange("(c p) n -> p c n", p=128)
        chain_tok = {}
        gu_ld_free = [P.ctr["sPE"].v, P.ctr["sPE"].v]
        pGU_free = [P.ctr["sDV"].v] * 4
        for i in range(4):
            fls_dve[i] = P.ctr["sDV"].v
        gelu_free[0] = gelu_free[1] = P.ctr["sDV"].v
        wd_free = P.ctr["sPE"].v
        gu_tok = (0, 0)

        for g in range(8):
            assert len(stage_tok[("g", g)]) == 16, (g, len(stage_tok[("g", g)]))
            assert len(stage_tok[("u", g)]) == 16
            for i in range(16):
                pump(2)
                sl = (i // 2) % 2
                if i % 2 == 0:
                    FE("sync", "sAs")
                    W("sync", "sPE", gu_ld_free[sl])
                    ncp0 = (16 * g + i) * 128
                    t_gg = D("sync", "sLg%d" % sl, gu_sb[sl][:, 0:16 * 256].rearrange(
                        "p (c n) -> p c n", n=256), Gg_v[:, :, ncp0:ncp0 + 256])
                    t_gu = D("sync", "sLg%d" % sl,
                             gu_sb[sl][:, 16 * 256:32 * 256].rearrange(
                                 "p (c n) -> p c n", n=256),
                             Gu_v[:, :, ncp0:ncp0 + 256])
                    gu_tok = (t_gg, t_gu)
                io = i % 2
                for th in range(2):
                    FE("tensor", "sLg%d" % sl)
                    W("tensor", "sDV", pGU_free[th])
                    W("tensor", "sDV", pGU_free[2 + th])
                    for kc in range(16):
                        E("tensor", lambda e, sl=sl, kc=kc, io=io, th=th:
                          e.matmul(pA[th][:, 0:512],
                                   gu_sb[sl][:, kc * 256 + io * 128:
                                             kc * 256 + io * 128 + 128],
                                   xnT[:, kc * TC + th * 512:
                                       kc * TC + (th + 1) * 512],
                                   start=(kc == 0), stop=(kc == 15)))
                    t_g = T("tensor", "sPE")
                    for kc in range(16):
                        E("tensor", lambda e, sl=sl, kc=kc, io=io, th=th:
                          e.matmul(pA[2 + th][:, 0:512],
                                   gu_sb[sl][:, 16 * 256 + kc * 256 + io * 128:
                                             16 * 256 + kc * 256 + io * 128 + 128],
                                   xnT[:, kc * TC + th * 512:
                                       kc * TC + (th + 1) * 512],
                                   start=(kc == 0), stop=(kc == 15)))
                    t_u = T("tensor", "sPE")
                    W("scalar", "sPE", t_g)
                    W("scalar", "sDV", gelu_free[th])
                    E("scalar", lambda e, th=th: e.activation(
                        scr2[:, 1024 + th * 512: 1024 + (th + 1) * 512],
                        pA[th][:, 0:512], AF.Gelu_apprx_tanh))
                    t_gelu = T("scalar", "sAC")
                    W("vector", "sAC", t_gelu)
                    W("vector", "sPE", t_u)
                    E("vector", lambda e, i=i, th=th: e.tensor_tensor(
                        itr[:, i * 1024 + th * 512: i * 1024 + (th + 1) * 512],
                        scr2[:, 1024 + th * 512: 1024 + (th + 1) * 512],
                        pA[2 + th][:, 0:512], ALU.mult))
                    t_ch = T("vector", "sDV")
                    chain_tok[(i, th)] = t_ch
                    gelu_free[th] = t_ch
                    pGU_free[th] = t_ch
                    pGU_free[2 + th] = t_ch
                gu_ld_free[sl] = P.ctr["sPE"].v
            assert len(stage_tok[("d", g)]) == 16
            for q in range(4):
                pump(2)
                W("sync", "sPE", wd_free)
                FE("sync", "sAs")
                for i in range(16):
                    r0 = (16 * g + i) * 128
                    D("sync", "sLw", wbig[:, i * 512:(i + 1) * 512],
                      Gd[r0:r0 + 128, q * 512:(q + 1) * 512])
                for ts_ in range(8):
                    o4 = ts_ % 4
                    reg = pB[o4 // 2][:, (o4 % 2) * 512:(o4 % 2) * 512 + 512]
                    W("tensor", "sDV", fls_dve[o4])
                    if ts_ == 0:
                        FE("tensor", "sLw")
                    for i in range(16):
                        if q == 0 and ts_ == 0:
                            W("tensor", "sDV", chain_tok[(i, 0)])
                            W("tensor", "sDV", chain_tok[(i, 1)])
                        E("tensor", lambda e, reg=reg, i=i, ts_=ts_:
                          e.matmul(reg,
                                   itr[:, i * 1024 + ts_ * 128:
                                       i * 1024 + ts_ * 128 + 128],
                                   wbig[:, i * 512:(i + 1) * 512],
                                   start=(i == 0), stop=(i == 15)))
                    t_dn = T("tensor", "sPE")
                    W("vector", "sPE", t_dn)
                    o2 = ts_ % 2
                    FE("vector", "sGa%d" % o2)
                    E("vector", lambda e, o2=o2, reg=reg: e.tensor_copy(
                        fls[:, o2 * 512:(o2 + 1) * 512], reg))
                    t_fl = T("vector", "sDV")
                    fls_dve[o4] = t_fl
                    W("gpsimd", "sDV", t_fl)
                    kw = {} if g == 0 else {"accum_op": ALU.add}
                    D("gpsimd", "sGa%d" % o2,
                      acc_d[:, ts_ * 2048 + q * 512:
                            ts_ * 2048 + q * 512 + 512],
                      fls[:, o2 * 512:(o2 + 1) * 512], **kw)
                wd_free = P.ctr["sPE"].v

        W("sync", "sDV", P.ctr["sDV"].v)
        FE("sync", "sGa0")
        FE("sync", "sGa1")
        FE("sync", "sAs")
        cmb_free = 0
        last_store = 0
        for ts_ in range(8):
            W("sync", "sDV", cmb_free)
            FE("sync", "sLo")
            t_x = D("sync", "sLx", scr[:, 0:2048],
                    x_in[ts_ * 128:(ts_ + 1) * 128, :])
            t_s = D("sync", "sLx", scr[:, 2048:4096],
                    s_d[:, ts_ * 2048:(ts_ + 1) * 2048])
            atile = scr2[:, 0:2048]
            t_a = D("sync", "sLx", atile, acc_d[:, ts_ * 2048:(ts_ + 1) * 2048])
            FE("scalar", "sLx")
            W("scalar", "sDV", cmb_free)
            E("scalar", lambda e, a=atile: e.activation(a, a, AF.Silu))
            t_sl = T("scalar", "sAC")
            W("vector", "sAC", t_sl)
            FE("vector", "sLx")
            E("vector", lambda e, a=atile: e.tensor_tensor(
                scr[:, 2048:4096], scr[:, 2048:4096], a, ALU.mult))
            t_sg = T("vector", "sDV")
            W("vector", "sDV", t_sg)
            E("vector", lambda e: e.tensor_tensor(
                scr[:, 0:2048], scr[:, 0:2048], scr[:, 2048:4096], ALU.add))
            t_cb = T("vector", "sDV")
            cmb_free = t_cb
            W("sync", "sDV", t_cb)
            last_store = D("sync", "sLo", out_t[ts_ * 128:(ts_ + 1) * 128, :],
                           scr[:, 0:2048])

        dbg_reg = {"xnT_t": None, "qT_d": qT_d, "kT_d": kT_d, "kt_d": kt_d,
                   "v_d": v_d, "akt_d": akt_d, "av_d": av_d, "s_d": s_d,
                   "acc_d": acc_d, "Gg": Gg, "Gu": Gu, "Gd": Gd,
                   "gsh_d": gsh_d, "dsh_d": dsh_d}
        if dbg:
            FE("sync", "sAs")
            FE("sync", "sGa0")
            FE("sync", "sGa1")
            W("sync", "sDV", P.ctr["sDV"].v)
            for name in dbg:
                if name in ("xnT_e", "xnbf0"):
                    continue
                if name == "xnT_t":
                    o = nc.dram_tensor("dbg_xnT", [128, 16 * TC], BF16,
                                       kind="ExternalOutput").ap()
                    last_store = D("sync", "sLo", o, xnT[:, :])
                    continue
                src = dbg_reg[name]
                o = nc.dram_tensor("dbg_" + name, list(src.shape),
                                   src.dtype, kind="ExternalOutput").ap()
                last_store = D("sync", "sLo", o, src)
            FE("sync", "sLo")

        FE("sync", "sLo")
        FE("gpsimd", "sLo")

        assert dec_pos[0] == len(dec_units), (dec_pos[0], len(dec_units))

        @block.tensor
        def _(e):
            for f in P.ops["tensor"]:
                f(e)

        @block.vector
        def _(e):
            for f in P.ops["vector"]:
                f(e)

        @block.scalar
        def _(e):
            for f in P.ops["scalar"]:
                f(e)

        @block.gpsimd
        def _(e):
            for f in P.ops["gpsimd"]:
                f(e)

        @block.sync
        def _(e):
            for f in P.ops["sync"]:
                f(e)

    return nc


def _host_prepack(w_int32):
    b = np.ascontiguousarray(w_int32).view(np.uint8)
    b = b.reshape(w_int32.shape[0], w_int32.shape[1], 4)[:, :, 0]
    bT = np.ascontiguousarray(b.T)
    n = bT.shape[1]
    il = np.stack([np.arange(n // 2), np.arange(n // 2) + n // 2], 1).reshape(-1)
    bTi = np.ascontiguousarray(bT[:, il])
    return bTi.view(np.int16)


_CACHE = {}


def kernel(L0_attn_x, L0_attn_Wq, L0_attn_bq, L0_attn_Wk, L0_attn_bk,
           L0_attn_Wv, L0_attn_bv, L0_ffn_gate_W_packed, L0_ffn_gate_scales,
           L0_ffn_up_W_packed, L0_ffn_up_scales, L0_ffn_down_W_packed,
           L0_ffn_down_scales):
    x = np.ascontiguousarray(np.asarray(L0_attn_x, dtype=np.float32)).reshape(8192, H)
    wq = np.ascontiguousarray(np.asarray(L0_attn_Wq, dtype=np.float32))
    wk = np.ascontiguousarray(np.asarray(L0_attn_Wk, dtype=np.float32))
    wv = np.ascontiguousarray(np.asarray(L0_attn_Wv, dtype=np.float32))
    bq = np.ascontiguousarray(np.asarray(L0_attn_bq, dtype=np.float32)).reshape(1, H)
    bk = np.ascontiguousarray(np.asarray(L0_attn_bk, dtype=np.float32)).reshape(1, H)
    bv = np.ascontiguousarray(np.asarray(L0_attn_bv, dtype=np.float32)).reshape(1, H)
    gp = _host_prepack(np.ascontiguousarray(np.asarray(L0_ffn_gate_W_packed, dtype=np.int32)))
    up = _host_prepack(np.ascontiguousarray(np.asarray(L0_ffn_up_W_packed, dtype=np.int32)))
    dp = _host_prepack(np.ascontiguousarray(np.asarray(L0_ffn_down_W_packed, dtype=np.int32)))
    gs = np.ascontiguousarray(np.asarray(L0_ffn_gate_scales, dtype=np.float32).T)
    us = np.ascontiguousarray(np.asarray(L0_ffn_up_scales, dtype=np.float32).T)
    ds = np.ascontiguousarray(np.asarray(L0_ffn_down_scales, dtype=np.float32).T)

    z_x = np.zeros((TC, H), np.float32)
    z_b = np.zeros((1, H), np.float32)

    key = tuple(DBG)
    if key not in _CACHE:
        _CACHE[key] = build(dbg=key)
    nc = _CACHE[key]

    in_maps = []
    for r in range(NCORES):
        odd = r % 2 == 1
        in_maps.append({
            "x": x[r * TC:(r + 1) * TC],
            "xa": x[(r - 1) * TC: r * TC] if odd else z_x,
            "wq": wq, "wk": wk, "wv": wv,
            "bq": bq, "bk": bk, "bv": bv,
            "bak": bk if odd else z_b, "bav": bv if odd else z_b,
            "gatep": gp, "upp": up, "downp": dp,
            "gates": gs, "ups": us, "downs": ds,
        })

    res = run_bass_kernel_spmd(nc, in_maps, core_ids=list(range(NCORES)))
    _LAST.clear()
    _LAST.update({"results": res.results})
    out = np.concatenate([res.results[r]["out"] for r in range(NCORES)], axis=0)
    return out.reshape(4, 2048, H)


if __name__ == "__main__":
    nc = build()
    n = sum(1 for _ in nc.all_instructions())
    print("built ok; instructions:", n)


# version 05
# speedup vs baseline: 1.0045x; 1.0045x over previous
from contextlib import ExitStack

import numpy as np

import concourse.bass as bass
import concourse.mybir as mybir
from concourse.bass_utils import run_bass_kernel_spmd

F32 = mybir.dt.float32
BF16 = mybir.dt.bfloat16
I16 = mybir.dt.int16
AF = mybir.ActivationFunctionType
ALU = mybir.AluOpType
AX = mybir.AxisListType

NCORES = 8
H = 2048
TC = 1024
HEADS = 8
CHUNK = 256
NCH = 4
MLP = 16384
EPS = 1e-5

_BETAS = np.linspace(-5.0, -9.0, 8).astype(np.float32).astype(np.float64)
GAMMA = 1.0 - np.exp2(_BETAS)
LNG = np.log(GAMMA)
CD = GAMMA ** CHUNK


class Ctr:
    __slots__ = ("v",)

    def __init__(self):
        self.v = 0

    def inc(self, d):
        self.v += d
        return self.v


class Prog:
    ENGINES = ("tensor", "vector", "scalar", "gpsimd", "sync")

    def __init__(self):
        self.ops = {e: [] for e in self.ENGINES}
        self.sem = {}
        self.ctr = {}

    def add_sem(self, name, handle):
        self.sem[name] = handle
        self.ctr[name] = Ctr()

    def emit(self, eng, fn):
        self.ops[eng].append(fn)

    def wait(self, eng, sem_name, val):
        if val is None or val <= 0:
            return
        s = self.sem[sem_name]
        self.emit(eng, lambda e, s=s, v=val: e.wait_ge(s, v))
        self.emit(eng, lambda e: e.nop(nofuse=True))

    def fence(self, eng, sem_name):
        self.wait(eng, sem_name, self.ctr[sem_name].v)

    def dma(self, eng, sem_name, out, in_, **kw):
        s = self.sem[sem_name]
        tok = self.ctr[sem_name].inc(16)
        self.emit(eng, lambda e, s=s, o=out, i=in_, kw=kw:
                  e.dma_start(out=o, in_=i, **kw).then_inc(s, 16))
        return tok

    def tag(self, eng, sem_name):
        s = self.sem[sem_name]
        tok = self.ctr[sem_name].inc(1)
        prev = self.ops[eng].pop()
        self.emit(eng, lambda e, p=prev, s=s: p(e).then_inc(s, 1))
        return tok


DBG = []
_LAST = {}


def build(dbg=()):
    nc = bass.Bass("TRN2", target_bir_lowering=False)

    def din(name, shape, dt):
        return nc.dram_tensor(name, list(shape), dt, kind="ExternalInput").ap()

    x_in = din("x", (TC, H), F32)
    xa_in = din("xa", (TC, H), F32)
    wq_in = din("wq", (H, H), F32)
    wk_in = din("wk", (H, H), F32)
    wv_in = din("wv", (H, H), F32)
    b_ins = {"q": din("bq", (1, H), F32), "k": din("bk", (1, H), F32),
             "v": din("bv", (1, H), F32), "ak": din("bak", (1, H), F32),
             "av": din("bav", (1, H), F32)}
    gp_in = din("gatep", (1024, 8192), I16)
    up_in = din("upp", (1024, 8192), I16)
    dp_in = din("downp", (8192, 1024), I16)
    gs_in = din("gates", (64, MLP), F32)
    us_in = din("ups", (64, MLP), F32)
    ds_in = din("downs", (512, H), F32)

    out_t = nc.dram_tensor("out", [TC, H], F32, kind="ExternalOutput").ap()

    def dint(name, shape, dt):
        return nc.dram_tensor(name, list(shape), dt).ap()

    qT_d = dint("qT_d", (128, 16 * TC), BF16)
    kT_d = dint("kT_d", (128, 16 * TC), BF16)
    kt_d = dint("kt_d", (128, 8 * H), BF16)
    v_d = dint("v_d", (128, 8 * H), BF16)
    akt_d = dint("akt_d", (128, 8 * H), BF16)
    av_d = dint("av_d", (128, 8 * H), BF16)
    s_d = dint("s_d", (128, 8 * H), F32)
    acc_d = dint("acc_d", (128, 8 * H), F32)
    Gg = dint("Gg", (H, MLP), BF16)
    Gu = dint("Gu", (H, MLP), BF16)
    Gd = dint("Gd", (MLP, H), BF16)
    gsh_d = dint("gsh_d", (64, MLP), BF16)
    ush_d = dint("ush_d", (64, MLP), BF16)
    dsh_d = dint("dsh_d", (512, H), BF16)

    with ExitStack() as ctx:
        sbt = lambda n, s, d: ctx.enter_context(nc.sbuf_tensor(n, list(s), d))
        pst = lambda n, s, d: ctx.enter_context(nc.psum_tensor(n, list(s), d))

        xnT = sbt("xnT", (128, 16 * TC), BF16)
        wbig = sbt("wbig", (128, 8 * 1024), BF16)
        scr = sbt("scr", (128, 4096), F32)
        scr2 = sbt("scr2", (128, 2048), F32)
        pk_sb = sbt("pk_sb", (128, 2 * 1024), I16)
        dreg = [sbt(f"dreg{i}", (128, 1024), I16) for i in range(8)]
        slab = sbt("slab_sb", (128, 2 * 1024), BF16)
        srep = sbt("srep_sb", (128, 2 * 1024), BF16)
        qe = sbt("qe", (128, 2 * 1024), BF16)
        Ssb = sbt("Ssb", (128, 16 * 256), F32)
        Sbf = sbt("Sbf", (128, 2 * 256), BF16)
        ksb = sbt("ksb", (128, 2 * 256), BF16)
        attnM = sbt("attnM", (128, 2 * 256), BF16)
        qs_sb = sbt("qs_sb", (128, 2 * 256), BF16)
        gu_sb = [sbt(f"gu{i}", (128, 16 * 512), BF16) for i in range(2)]
        itr = sbt("itr", (128, 16 * 1024), BF16)
        xanT = itr
        cons = sbt("cons", (128, 8), F32)
        csb = sbt("csb", (128, 8 * 256), F32)
        kvc = sbt("kvc", (128, 16), F32)
        dmsk = sbt("dmsk", (128, 2 * 256), F32)
        ident = sbt("ident", (128, 128), BF16)
        onesb = sbt("onesb", (128, 512), BF16)
        brow = sbt("brow", (128, H), BF16)
        iop = sbt("iop", (128, 2), F32)
        lst = sbt("lst", (128, 16), F32)
        fls = sbt("fls", (128, 2 * 512), F32)

        dtt = scr2[:, 0:512]
        ioc = scr2[:, 512:768]

        pA = [pst(f"pA{i}", (128, 512), F32) for i in range(4)]
        pB = [pst(f"pB{i}", (128, 1024), F32) for i in range(2)]
        p_upd = pB[0][:, 0:512]

        P = Prog()
        for s in ("sLx", "sLp0", "sLp1", "sGs0", "sGs1", "sLg0", "sLg1",
                  "sLw", "sGw0", "sGw1", "sGb", "sAq0", "sAq1", "sAs",
                  "sGa0", "sGa1", "sLo",
                  "sPE", "sDV", "sAC", "sGP"):
            P.add_sem(s, ctx.enter_context(nc.semaphore(s)))

        block = ctx.enter_context(nc.Block())
        E, W, D, T, FE = P.emit, P.wait, P.dma, P.tag, P.fence

        mask_free = [0]
        attn_free = [0]
        ks_free = [0]
        upd_free = [0]
        gelu_free = [0, 0]
        fls_dve = [0, 0, 0, 0]
        fls_gd = [0, 0]

        for col, val in {0: -4.0, 1: -16.0, 2: 1.0, 3: EPS}.items():
            E("gpsimd", lambda e, c=col, v=val: e.memset(cons[:, c:c + 1], v))
        E("gpsimd", lambda e: e.memset(onesb[0:1, :], 1.0))
        E("gpsimd", lambda e: e.memset(Ssb[:, :], 0.0))
        E("gpsimd", lambda e: e.iota(ioc, pattern=[[1, 256]], base=0,
                                     channel_multiplier=0,
                                     allow_small_or_imprecise_dtypes=True))
        E("gpsimd", lambda e: e.iota(iop[:, 0:1], pattern=[[0, 1]], base=0,
                                     channel_multiplier=1,
                                     allow_small_or_imprecise_dtypes=True))
        for jc in range(2):
            E("gpsimd", lambda e, jc=jc: e.iota(
                dmsk[:, jc * 256:(jc + 1) * 256], pattern=[[1, 256]],
                base=-128 * jc, channel_multiplier=-1,
                allow_small_or_imprecise_dtypes=True))
            t_io = T("gpsimd", "sGP")
            W("gpsimd", "sGP", t_io)
            E("gpsimd", lambda e, jc=jc: e.affine_select(
                dmsk[:, jc * 256:(jc + 1) * 256], dmsk[:, jc * 256:(jc + 1) * 256],
                pattern=[[1, 256]], compare_op=ALU.is_ge, fill=1.0e9,
                base=-128 * jc, channel_multiplier=-1))
        E("gpsimd", lambda e: e.iota(scr2[:, 768:896], pattern=[[1, 128]], base=0,
                                     channel_multiplier=-1,
                                     allow_small_or_imprecise_dtypes=True))
        tok_gp0 = T("gpsimd", "sGP")

        W("vector", "sGP", tok_gp0)
        E("vector", lambda e: e.tensor_scalar(scr2[:, 896:1024], scr2[:, 768:896],
                                              0.0, None, ALU.is_equal))
        t_ieq = T("vector", "sDV")
        W("vector", "sDV", t_ieq)
        E("vector", lambda e: e.tensor_copy(ident[:, :], scr2[:, 896:1024]))
        tok_ident = T("vector", "sDV")

        W("scalar", "sGP", tok_gp0)
        for h in range(HEADS):
            E("scalar", lambda e, h=h: e.activation(
                csb[:, h * 256:(h + 1) * 256], ioc, AF.Exp, scale=float(LNG[h])))
        for tsl in range(2):
            for h in range(HEADS):
                col = tsl * 8 + h
                E("scalar", lambda e, h=h, col=col: e.activation(
                    kvc[:, col:col + 1], iop[:, 0:1], AF.Exp, scale=-float(LNG[h])))
        tok_exp = T("scalar", "sAC")
        W("vector", "sAC", tok_exp)
        for h in range(HEADS):
            E("vector", lambda e, h=h: e.tensor_scalar(
                csb[:, h * 256:(h + 1) * 256], csb[:, h * 256:(h + 1) * 256],
                float(GAMMA[h]), None, ALU.mult))
            E("vector", lambda e: e.nop())
        for tsl in range(2):
            for h in range(HEADS):
                col = tsl * 8 + h
                mv = float(GAMMA[h] ** (255 - 128 * tsl))
                E("vector", lambda e: e.nop())
                E("vector", lambda e, col=col, mv=mv: e.tensor_scalar(
                    kvc[:, col:col + 1], kvc[:, col:col + 1], mv, None, ALU.mult))
        T("vector", "sDV")

        last_sc_st = 0
        for src, dst, rows, cols in ((gs_in, gsh_d, 64, MLP),
                                     (us_in, ush_d, 64, MLP),
                                     (ds_in, dsh_d, 512, H)):
            for r0 in range(0, rows, 64):
                rr = min(64, rows - r0)
                for c0 in range(0, cols, 4096):
                    cc = min(4096, cols - c0)
                    FE("sync", "sAs")
                    t_ld = D("sync", "sLx", scr[0:rr, 0:cc],
                             src[r0:r0 + rr, c0:c0 + cc])
                    FE("vector", "sLx")
                    E("vector", lambda e, rr=rr, cc=cc: e.tensor_scalar(
                        qe[0:rr, 0:min(cc, 2048)] if cc <= 2048 else
                        wbig[0:rr, 0:cc], scr[0:rr, 0:cc], 0.5, None, ALU.mult))
                    t_h = T("vector", "sDV")
                    E("vector", lambda e: e.nop())
                    W("scalar", "sDV", t_h)
                    buf = qe[0:rr, 0:cc] if cc <= 2048 else wbig[0:rr, 0:cc]
                    last_sc_st = D("scalar", "sAs", dst[r0:r0 + rr, c0:c0 + cc],
                                   buf)
        tok_scales_st = last_sc_st
        tok_scales_dve = P.ctr["sDV"].v

        def load_bias(name, wait_pe=None):
            if wait_pe:
                W("gpsimd", "sPE", wait_pe)
            return D("gpsimd", "sGb", brow[0:1, 0:H], b_ins[name])

        gsh_aps = {"g": gsh_d, "u": ush_d}
        pk_aps = {"g": gp_in, "u": up_in}
        G_aps = {"g": Gg, "u": Gu}
        stage_tok = {(m, g): [] for m in ("g", "u", "d") for g in range(8)}
        ds_pk_free = [0, 0]
        ds_srep_free = [0, 0]
        ds_slab_free = [0, 0]
        ds_slot = [0]
        FD = 1024

        def emit_chain_pair(planes, pk_ap):
            for j, (sh, _, _) in enumerate(planes):
                r = dreg[4 * j]
                if sh:
                    E("vector", lambda e, r=r, sh=sh: e.tensor_scalar(
                        r[:, :], pk_ap, sh, 15,
                        ALU.logical_shift_right, ALU.bitwise_and))
                else:
                    E("vector", lambda e, r=r: e.tensor_scalar(
                        r[:, :], pk_ap, 15, None, ALU.bitwise_and))
            for j in range(len(planes)):
                c0, m = dreg[4 * j], dreg[4 * j + 1]
                E("vector", lambda e, c0=c0, m=m: e.tensor_scalar(
                    m[:, :], c0[:, :], 7, None, ALU.bitwise_and))
            tok_m = T("vector", "sDV")
            W("scalar", "sDV", tok_m)
            for j in range(len(planes)):
                m, a, b = dreg[4 * j + 1], dreg[4 * j + 2], dreg[4 * j + 3]
                E("scalar", lambda e, m=m, a=a: e.activation(
                    a[:, :], m[:, :], AF.Identity, scale=2.0, bias=cons[:, 0:1]))
                E("scalar", lambda e, m=m, b=b: e.activation(
                    b[:, :], m[:, :], AF.Identity, scale=4.0, bias=cons[:, 1:2]))
            tok_ab = T("scalar", "sAC")
            W("vector", "sAC", tok_ab)
            for j in range(len(planes)):
                a, b = dreg[4 * j + 2], dreg[4 * j + 3]
                E("vector", lambda e, a=a, b=b: e.tensor_tensor(
                    a[:, :], a[:, :], b[:, :], ALU.max))
            for j in range(len(planes)):
                m, a = dreg[4 * j + 1], dreg[4 * j + 2]
                E("vector", lambda e, m=m, a=a: e.tensor_tensor(
                    a[:, :], m[:, :], a[:, :], ALU.max))
            for j in range(len(planes)):
                c0, b = dreg[4 * j], dreg[4 * j + 3]
                E("vector", lambda e, c0=c0, b=b: e.tensor_scalar(
                    b[:, :], c0[:, :], 3, 1,
                    ALU.logical_shift_right, ALU.bitwise_and))
            for j in range(len(planes)):
                b = dreg[4 * j + 3]
                E("vector", lambda e, b=b: e.tensor_scalar(
                    b[:, :], b[:, :], -2, 1, ALU.mult, ALU.add))
            for j in range(len(planes)):
                a, b = dreg[4 * j + 2], dreg[4 * j + 3]
                E("vector", lambda e, a=a, b=b: e.tensor_tensor(
                    a[:, :], a[:, :], b[:, :], ALU.mult))
            for j, (_, slab_sl, srep_sl) in enumerate(planes):
                a = dreg[4 * j + 2]
                E("vector", lambda e, a=a, o=slab_sl, sr=srep_sl:
                  e.tensor_tensor(o, a[:, :], sr, ALU.mult))
            return T("vector", "sDV")

        def emit_unit_gu(mat, g, c, ch):
            b = g // 4
            frq = 2048 * (g % 4) + 1024 * ch
            n0 = 2048 * g + 1024 * ch
            slot = ds_slot[0]
            ds_slot[0] ^= 1
            W("sync", "sDV", ds_pk_free[slot])
            pk_sl = pk_sb[:, slot * 1024: slot * 1024 + 1024]
            t_pk = D("sync", "sLp%d" % slot, pk_sl,
                     pk_aps[mat][c * 128:(c + 1) * 128, frq:frq + 1024])
            if ds_srep_free[slot] == 0:
                W("gpsimd", "sAs", tok_scales_st)
            W("gpsimd", "sDV", ds_srep_free[slot])
            sr_sl = srep[:, slot * 1024: slot * 1024 + 1024]
            t_sr = D("gpsimd", "sGs%d" % slot, sr_sl,
                     gsh_aps[mat][8 * c:8 * c + 8, n0:n0 + 1024]
                     .unsqueeze(1).broadcast_to([8, 16, 1024]))
            FE("vector", "sLp%d" % slot)
            FE("vector", "sGs%d" % slot)
            FE("vector", "sAs")
            planes = [(8 * b, slab[:, 0:1024], sr_sl),
                      (8 * b + 4, slab[:, 1024:2048], sr_sl)]
            tok_w = emit_chain_pair(planes, pk_sl)
            ds_pk_free[slot] = tok_w
            ds_srep_free[slot] = tok_w
            W("scalar", "sDV", tok_w)
            Gm = G_aps[mat]
            ds_slab_free[0] = D("scalar", "sAs",
                                Gm[256 * c: 256 * c + 256: 2, n0:n0 + 1024],
                                slab[:, 0:1024])
            ds_slab_free[1] = D("scalar", "sAs",
                                Gm[256 * c + 1: 256 * c + 256: 2, n0:n0 + 1024],
                                slab[:, 1024:2048])
            stage_tok[(mat, g)].append(ds_slab_free[1])

        def emit_unit_down(g, cr, pairi):
            cp = 8 * g + cr
            slot = ds_slot[0]
            ds_slot[0] ^= 1
            W("sync", "sDV", ds_pk_free[slot])
            pk_sl = pk_sb[:, slot * 1024: slot * 1024 + 1024]
            t_pk = D("sync", "sLp%d" % slot, pk_sl,
                     dp_in[cp * 128:(cp + 1) * 128, 0:1024])
            if ds_srep_free[slot] == 0:
                W("gpsimd", "sAs", tok_scales_st)
            W("gpsimd", "sDV", ds_srep_free[slot])
            sr_sl = srep[:, slot * 1024: slot * 1024 + 1024]
            t_sr = D("gpsimd", "sGs%d" % slot, sr_sl,
                     dsh_d[8 * cp:8 * cp + 8, pairi * 1024:(pairi + 1) * 1024]
                     .unsqueeze(1).broadcast_to([8, 16, 1024]))
            FE("vector", "sLp%d" % slot)
            FE("vector", "sGs%d" % slot)
            FE("vector", "sAs")
            planes = [(8 * pairi, slab[:, 0:1024], sr_sl),
                      (8 * pairi + 4, slab[:, 1024:2048], sr_sl)]
            tok_w = emit_chain_pair(planes, pk_sl)
            ds_pk_free[slot] = tok_w
            ds_srep_free[slot] = tok_w
            W("scalar", "sDV", tok_w)
            t_last = 0
            for nib in range(2):
                sl = slab[:, nib * 1024:(nib + 1) * 1024]
                t_last = D("scalar", "sAs",
                           Gd[256 * cp + nib: 256 * cp + 256: 2,
                              pairi * 1024: (pairi + 1) * 1024], sl)
            ds_slab_free[0] = t_last
            ds_slab_free[1] = t_last
            stage_tok[("d", g)].append(t_last)

        dec_units = []
        for g in range(8):
            for c in range(8):
                for ch in range(2):
                    dec_units.append(("g", g, c, ch))
                    dec_units.append(("u", g, c, ch))
            for cr in range(8):
                for pairi in range(2):
                    dec_units.append(("d", g, cr, pairi))
        dec_pos = [0]
        dec_on = [False]

        def pump(n):
            if not dec_on[0]:
                return
            for _ in range(n):
                if dec_pos[0] >= len(dec_units):
                    return
                kind, g, c, x2 = dec_units[dec_pos[0]]
                dec_pos[0] += 1
                if kind == "d":
                    emit_unit_down(g, c, x2)
                else:
                    emit_unit_gu(kind, g, c, x2)

        x_sl = scr[:, 0:2048]
        sq_sl = scr[:, 2048:4096]
        xnbf = wbig[:, 0:2048]
        slot_free = tok_scales_dve
        xnbf_pe_free = 0
        FE("vector", "sAs")
        W("scalar", "sDV", tok_scales_dve)
        FE("scalar", "sAs")
        tr_evac_tok = {0: 0, 1: 0}
        tpsum = [pB[0][:, 0:64].bitcast(BF16), pB[1][:, 0:64].bitcast(BF16)]

        for kind, tt in [("m", t) for t in range(8)] + [("a", t) for t in range(8)]:
            src = x_in if kind == "m" else xa_in
            W("sync", "sDV", slot_free)
            t_ld = D("sync", "sLx", x_sl, src[tt * 128:(tt + 1) * 128, :])
            FE("vector", "sLx")
            FE("scalar", "sLx")
            E("vector", lambda e: e.tensor_reduce(lst[:, 0:1], x_sl, AX.X, ALU.add))
            t_s0 = T("vector", "sDV")
            E("scalar", lambda e: e.activation(sq_sl, x_sl, AF.Square))
            t_sq = T("scalar", "sAC")
            W("vector", "sAC", t_sq)
            E("vector", lambda e: e.tensor_reduce(lst[:, 1:2], sq_sl, AX.X, ALU.add))
            W("vector", "sDV", t_s0)
            E("vector", lambda e: e.tensor_scalar(lst[:, 2:3], lst[:, 0:1],
                                                  1.0 / H, None, ALU.mult))
            t_ms = T("vector", "sDV")
            W("scalar", "sDV", t_ms)
            E("scalar", lambda e: e.activation(lst[:, 3:4], lst[:, 1:2],
                                               AF.Identity, scale=1.0 / H))
            E("scalar", lambda e: e.activation(lst[:, 4:5], lst[:, 2:3],
                                               AF.Square))
            t_m2 = T("scalar", "sAC")
            W("vector", "sAC", t_m2)
            E("vector", lambda e: e.tensor_tensor(lst[:, 5:6], lst[:, 3:4],
                                                  lst[:, 4:5], ALU.subtract))
            t_var = T("vector", "sDV")
            W("scalar", "sDV", t_var)
            E("scalar", lambda e: e.activation(lst[:, 6:7], lst[:, 5:6],
                                               AF.Sqrt, bias=cons[:, 3:4]))
            t_std = T("scalar", "sAC")
            W("vector", "sAC", t_std)
            E("vector", lambda e: e.reciprocal(lst[:, 7:8], lst[:, 6:7]))
            t_rc = T("vector", "sDV")
            W("vector", "sDV", t_rc)
            W("vector", "sPE", xnbf_pe_free)
            E("vector", lambda e: e.tensor_scalar(xnbf, x_sl, lst[:, 2:3],
                                                  lst[:, 7:8], ALU.subtract,
                                                  ALU.mult))
            t_xn = T("vector", "sDV")
            slot_free = t_xn
            if kind == "m" and tt == 0 and "xnbf0" in dbg:
                dbg_x0 = nc.dram_tensor("dbg_x0", [128, 2048], F32,
                                        kind="ExternalOutput").ap()
                dbg_xb0 = nc.dram_tensor("dbg_xnbf0", [128, 2048], BF16,
                                         kind="ExternalOutput").ap()
                dbg_lst0 = nc.dram_tensor("dbg_lst0", [128, 16], F32,
                                          kind="ExternalOutput").ap()
                W("sync", "sDV", t_xn)
                tt_x0 = D("sync", "sLo", dbg_x0, x_sl)
                tt_xb = D("sync", "sLo", dbg_xb0, xnbf)
                tt_ls = D("sync", "sLo", dbg_lst0, lst[:, 0:16])
                FE("vector", "sLo")
                FE("tensor", "sLo")
            dstT = xnT if kind == "m" else xanT
            W("tensor", "sDV", t_xn)
            W("tensor", "sDV", tok_ident)
            for kc in range(16):
                pp = kc % 2
                W("tensor", "sDV", tr_evac_tok[pp])
                E("tensor", lambda e, kc=kc, pp=pp: e.transpose(
                    tpsum[pp], xnbf[:, kc * 128:(kc + 1) * 128], ident[:, :]))
                t_tr = T("tensor", "sPE")
                W("vector", "sPE", t_tr)
                E("vector", lambda e, dstT=dstT, kc=kc, tt=tt, pp=pp:
                  e.tensor_copy(
                      dstT[:, kc * TC + tt * 128: kc * TC + (tt + 1) * 128],
                      tpsum[pp]))
                tr_evac_tok[pp] = T("vector", "sDV")
            xnbf_pe_free = P.ctr["sPE"].v
        xnT_ready = max(tr_evac_tok[0], tr_evac_tok[1])

        if "xnT_e" in dbg:
            dbg_xe = nc.dram_tensor("dbg_xnT_e", [128, 16 * TC], BF16,
                                    kind="ExternalOutput").ap()
            W("sync", "sDV", xnT_ready)
            D("sync", "sLo", dbg_xe, xnT[:, :])
        FE("gpsimd", "sAs")
        dec_on[0] = True

        pB_free = [xnT_ready, xnT_ready]
        bias_tok = [0]
        FE("vector", "sAs")

        def emit_fm(w_ap, bias_slot, dst_d):
            wslot_free = [P.ctr["sPE"].v, P.ctr["sPE"].v]
            for ocg in range(8):
                ws = ocg % 2
                W("gpsimd", "sPE", wslot_free[ws])
                t_w = D("gpsimd", "sGw%d" % ws, gu_sb[ws][:, 0:16 * 256].rearrange(
                    "p (c n) -> p c n", n=256),
                    w_ap.rearrange("(c p) n -> p c n", p=128)
                    [:, :, ocg * 256:(ocg + 1) * 256])
                pump(1)
                for oc2 in range(2):
                    oc = 2 * ocg + oc2
                    FE("tensor", "sGw%d" % ws)
                    FE("tensor", "sGb")
                    W("tensor", "sDV", pB_free[oc2])
                    for tb in range(2):
                        for kc in range(16):
                            E("tensor", lambda e, oc2=oc2, tb=tb, kc=kc, ws=ws:
                              e.matmul(pB[oc2][:, tb * 512:(tb + 1) * 512],
                                       gu_sb[ws][:, kc * 256 + oc2 * 128:
                                                 kc * 256 + oc2 * 128 + 128],
                                       xnT[:, kc * TC + tb * 512:
                                           kc * TC + (tb + 1) * 512],
                                       start=(kc == 0), stop=False))
                        E("tensor", lambda e, oc2=oc2, tb=tb, oc=oc, bs=bias_slot:
                          e.matmul(pB[oc2][:, tb * 512:(tb + 1) * 512],
                                   brow[0:1, oc * 128: oc * 128 + 128],
                                   onesb[0:1, 0:512], start=False, stop=True))
                    t_pe = T("tensor", "sPE")
                    W("vector", "sPE", t_pe)
                    FE("vector", "sAq%d" % oc2)
                    E("vector", lambda e, oc2=oc2: e.tensor_copy(
                        qe[:, oc2 * 1024:(oc2 + 1) * 1024], pB[oc2][:, :]))
                    t_ev = T("vector", "sDV")
                    pB_free[oc2] = t_ev
                    W("scalar", "sDV", t_ev)
                    D("scalar", "sAq%d" % oc2,
                      dst_d[:, oc * 1024:(oc + 1) * 1024],
                      qe[:, oc2 * 1024:(oc2 + 1) * 1024])
                    wslot_free[ws] = t_pe

        bias_tok[0] = load_bias("q")
        emit_fm(wq_in, 0, qT_d)
        bias_tok[0] = load_bias("k", wait_pe=P.ctr["sPE"].v)
        emit_fm(wk_in, 0, kT_d)

        tm_wfree = [P.ctr["sPE"].v] * 16

        def emit_tm(w_ap, bias_main, bias_alien, main_dst, alien_dst):
            for dq in range(4):
                w_toks = []
                for kc in range(16):
                    W("gpsimd", "sPE", tm_wfree[kc])
                    w_toks.append(D("gpsimd", "sGw0",
                                    wbig[:, kc * 512:(kc + 1) * 512],
                                    w_ap.rearrange("(c p) n -> p c n", p=128)
                                    [:, kc, dq * 512:(dq + 1) * 512]))
                    if kc % 4 == 3:
                        pump(1)
                for srcT, bname, dst_d in ((xnT, bias_main, main_dst),
                                           (xanT, bias_alien, alien_dst)):
                    bias_tok[0] = load_bias(bname, wait_pe=P.ctr["sPE"].v)
                    for tsub in range(8):
                        o2 = tsub % 2
                        reg = pB[o2][:, 0:512]
                        W("tensor", "sDV", pB_free[o2])
                        FE("tensor", "sGb")
                        FE("tensor", "sGw0")
                        for kc in range(16):
                            E("tensor", lambda e, reg=reg, kc=kc, tsub=tsub,
                              srcT=srcT:
                              e.matmul(reg,
                                       srcT[:, kc * TC + tsub * 128:
                                            kc * TC + tsub * 128 + 128],
                                       wbig[:, kc * 512:(kc + 1) * 512],
                                       start=(kc == 0), stop=False))
                        E("tensor", lambda e, reg=reg, dq=dq:
                          e.matmul(reg, onesb[0:1, 0:128],
                                   brow[0:1, dq * 512:(dq + 1) * 512],
                                   start=False, stop=True))
                        t_pe = T("tensor", "sPE")
                        W("vector", "sPE", t_pe)
                        FE("vector", "sAq%d" % o2)
                        E("vector", lambda e, o2=o2, reg=reg: e.tensor_copy(
                            qe[:, o2 * 1024: o2 * 1024 + 512], reg))
                        t_ev = T("vector", "sDV")
                        pB_free[o2] = t_ev
                        W("scalar", "sDV", t_ev)
                        D("scalar", "sAq%d" % o2,
                          dst_d[:, tsub * 2048 + dq * 512:
                                tsub * 2048 + (dq + 1) * 512],
                          qe[:, o2 * 1024: o2 * 1024 + 512])
                    pump(2)
                for kc in range(16):
                    tm_wfree[kc] = P.ctr["sPE"].v

        emit_tm(wk_in, "k", "ak", kt_d, akt_d)
        emit_tm(wv_in, "v", "av", v_d, av_d)

        qT_v = qT_d.rearrange("p (c t) -> p c t", t=TC)
        kT_v = kT_d.rearrange("p (c t) -> p c t", t=TC)

        pA_free = [pB_free[0]] * 4
        upd_free[0] = pB_free[0]
        scan_pe_free = P.ctr["sPE"].v
        scan_dve_free = P.ctr["sDV"].v
        last_upd = P.ctr["sDV"].v

        for phase, src_kt, src_v in (("a", akt_d, av_d), ("m", kt_d, v_d)):
            for c in range(NCH):
                FE("sync", "sAq0")
                FE("sync", "sAq1")
                W("sync", "sPE", scan_pe_free)
                W("sync", "sDV", last_upd)
                t_kt = D("sync", "sLx", wbig[:, 0:4096],
                         src_kt[:, (2 * c) * 2048:(2 * c + 2) * 2048])
                t_v = D("sync", "sLx", wbig[:, 4096:8192],
                        src_v[:, (2 * c) * 2048:(2 * c + 2) * 2048])
                t_q = t_k2 = None
                if phase == "m":
                    t_q = D("sync", "sLx", gu_sb[0][:, 0:4096].rearrange(
                        "p (k t) -> p k t", t=256),
                        qT_v[:, :, c * 256:(c + 1) * 256])
                    t_k2 = D("sync", "sLx", gu_sb[0][:, 4096:8192].rearrange(
                        "p (k t) -> p k t", t=256),
                        kT_v[:, :, c * 256:(c + 1) * 256])
                pump(1)
                for h in range(HEADS):
                    hp = h % 2
                    cdh = float(CD[h])
                    if phase == "m":
                        W("vector", "sDV", last_upd)
                        E("vector", lambda e, h=h: e.tensor_copy(
                            Sbf[:, :], Ssb[:, h * 512:(h + 1) * 512]))
                        t_sbf = T("vector", "sDV")
                        W("scalar", "sDV", mask_free[0])
                        for jc in range(2):
                            E("scalar", lambda e, h=h, jc=jc: e.activation(
                                dtt[:, jc * 256:(jc + 1) * 256],
                                dmsk[:, jc * 256:(jc + 1) * 256], AF.Exp,
                                scale=float(LNG[h])))
                        t_dt = T("scalar", "sAC")
                        FE("tensor", "sLx")
                        W("tensor", "sDV", pA_free[hp])
                        for jc in range(2):
                            for dc in range(2):
                                E("tensor", lambda e, h=h, jc=jc, dc=dc, hp=hp:
                                  e.matmul(pA[hp][:, jc * 256:(jc + 1) * 256],
                                           gu_sb[0][:, 4096 + (2 * h + dc) * 256 +
                                                 jc * 128:
                                                 4096 + (2 * h + dc) * 256 +
                                                 jc * 128 + 128],
                                           gu_sb[0][:, (2 * h + dc) * 256:
                                                 (2 * h + dc) * 256 + 256],
                                           start=(dc == 0), stop=(dc == 1)))
                        t_at = T("tensor", "sPE")
                        W("vector", "sPE", t_at)
                        W("vector", "sPE", attn_free[0])
                        W("vector", "sAC", t_dt)
                        for jc in range(2):
                            E("vector", lambda e, jc=jc, hp=hp: e.tensor_tensor(
                                attnM[:, jc * 256:(jc + 1) * 256],
                                pA[hp][:, jc * 256:(jc + 1) * 256],
                                dtt[:, jc * 256:(jc + 1) * 256], ALU.mult))
                            E("vector", lambda e, h=h, jc=jc: e.tensor_tensor(
                                qs_sb[:, jc * 256:(jc + 1) * 256],
                                gu_sb[0][:, (2 * h + jc) * 256:
                                      (2 * h + jc) * 256 + 256],
                                csb[:, h * 256:(h + 1) * 256], ALU.mult))
                        t_mq = T("vector", "sDV")
                        pA_free[hp] = t_mq
                        mask_free[0] = t_mq
                        W("tensor", "sDV", t_mq)
                        W("tensor", "sDV", t_sbf)
                        W("tensor", "sDV", pA_free[2 + hp])
                        for ih in range(2):
                            for jc in range(2):
                                E("tensor", lambda e, ih=ih, jc=jc, h=h, hp=hp:
                                  e.matmul(pA[2 + hp][:, ih * 256:(ih + 1) * 256],
                                           attnM[:, jc * 256 + ih * 128:
                                                 jc * 256 + ih * 128 + 128],
                                           wbig[:, 4096 + jc * 2048 + h * 256:
                                                4096 + jc * 2048 + h * 256 + 256],
                                           start=(jc == 0), stop=False))
                            for dc in range(2):
                                E("tensor", lambda e, ih=ih, dc=dc, hp=hp:
                                  e.matmul(pA[2 + hp][:, ih * 256:(ih + 1) * 256],
                                           qs_sb[:, dc * 256 + ih * 128:
                                                 dc * 256 + ih * 128 + 128],
                                           Sbf[:, dc * 256:(dc + 1) * 256],
                                           start=False, stop=(dc == 1)))
                        t_out = T("tensor", "sPE")
                        attn_free[0] = t_out
                        W("vector", "sPE", t_out)
                        FE("vector", "sAs")
                        for ih in range(2):
                            E("vector", lambda e, ih=ih, h=h, hp=hp:
                              e.tensor_copy(
                                  scr[:, ih * 2048 + h * 256:
                                      ih * 2048 + h * 256 + 256],
                                  pA[2 + hp][:, ih * 256:(ih + 1) * 256]))
                        t_sev = T("vector", "sDV")
                        pA_free[2 + hp] = t_sev
                    FE("vector", "sLx")
                    W("vector", "sPE", ks_free[0])
                    for tsl in range(2):
                        E("vector", lambda e: e.nop())
                        E("vector", lambda e, tsl=tsl, h=h: e.tensor_scalar(
                            ksb[:, tsl * 256:(tsl + 1) * 256],
                            wbig[:, tsl * 2048 + h * 256:
                                 tsl * 2048 + h * 256 + 256],
                            kvc[:, (tsl * 8 + h):(tsl * 8 + h) + 1], None,
                            ALU.mult))
                    t_ks = T("vector", "sDV")
                    W("tensor", "sDV", t_ks)
                    W("tensor", "sDV", upd_free[0])
                    FE("tensor", "sLx")
                    for dh in range(2):
                        for tsl in range(2):
                            E("tensor", lambda e, dh=dh, tsl=tsl, h=h:
                              e.matmul(p_upd[:, dh * 256:(dh + 1) * 256],
                                       ksb[:, tsl * 256 + dh * 128:
                                           tsl * 256 + dh * 128 + 128],
                                       wbig[:, 4096 + tsl * 2048 + h * 256:
                                            4096 + tsl * 2048 + h * 256 + 256],
                                       start=(tsl == 0), stop=(tsl == 1)))
                    t_upd = T("tensor", "sPE")
                    ks_free[0] = t_upd
                    W("vector", "sPE", t_upd)
                    for dh in range(2):
                        E("vector", lambda e: e.nop())
                        E("vector", lambda e, dh=dh, h=h, cdh=cdh:
                          e.scalar_tensor_tensor(
                              Ssb[:, h * 512 + dh * 256:
                                  h * 512 + dh * 256 + 256],
                              Ssb[:, h * 512 + dh * 256:
                                  h * 512 + dh * 256 + 256],
                              cdh, p_upd[:, dh * 256:(dh + 1) * 256],
                              ALU.mult, ALU.add))
                    last_upd = T("vector", "sDV")
                    upd_free[0] = last_upd
                if phase == "m":
                    W("scalar", "sDV", last_upd)
                    D("scalar", "sAs",
                      s_d[:, (2 * c) * 2048:(2 * c + 2) * 2048],
                      scr[:, 0:4096])
                scan_pe_free = P.ctr["sPE"].v
                pump(2)

        Gg_v = Gg.rearrange("(c p) n -> p c n", p=128)
        Gu_v = Gu.rearrange("(c p) n -> p c n", p=128)
        chain_tok = {}
        gu_ld_free = [P.ctr["sPE"].v, P.ctr["sPE"].v]
        pGU_free = [P.ctr["sDV"].v] * 4
        for i in range(4):
            fls_dve[i] = P.ctr["sDV"].v
        gelu_free[0] = gelu_free[1] = P.ctr["sDV"].v
        wd_free = P.ctr["sPE"].v
        gu_tok = (0, 0)

        for g in range(8):
            assert len(stage_tok[("g", g)]) == 16, (g, len(stage_tok[("g", g)]))
            assert len(stage_tok[("u", g)]) == 16
            for i in range(16):
                pump(2)
                sl = (i // 2) % 2
                if i % 2 == 0:
                    FE("sync", "sAs")
                    W("sync", "sPE", gu_ld_free[sl])
                    ncp0 = (16 * g + i) * 128
                    t_gg = D("sync", "sLg%d" % sl, gu_sb[sl][:, 0:16 * 256].rearrange(
                        "p (c n) -> p c n", n=256), Gg_v[:, :, ncp0:ncp0 + 256])
                    t_gu = D("sync", "sLg%d" % sl,
                             gu_sb[sl][:, 16 * 256:32 * 256].rearrange(
                                 "p (c n) -> p c n", n=256),
                             Gu_v[:, :, ncp0:ncp0 + 256])
                    gu_tok = (t_gg, t_gu)
                io = i % 2
                for th in range(2):
                    FE("tensor", "sLg%d" % sl)
                    W("tensor", "sDV", pGU_free[th])
                    W("tensor", "sDV", pGU_free[2 + th])
                    for kc in range(16):
                        E("tensor", lambda e, sl=sl, kc=kc, io=io, th=th:
                          e.matmul(pA[th][:, 0:512],
                                   gu_sb[sl][:, kc * 256 + io * 128:
                                             kc * 256 + io * 128 + 128],
                                   xnT[:, kc * TC + th * 512:
                                       kc * TC + (th + 1) * 512],
                                   start=(kc == 0), stop=(kc == 15)))
                    t_g = T("tensor", "sPE")
                    for kc in range(16):
                        E("tensor", lambda e, sl=sl, kc=kc, io=io, th=th:
                          e.matmul(pA[2 + th][:, 0:512],
                                   gu_sb[sl][:, 16 * 256 + kc * 256 + io * 128:
                                             16 * 256 + kc * 256 + io * 128 + 128],
                                   xnT[:, kc * TC + th * 512:
                                       kc * TC + (th + 1) * 512],
                                   start=(kc == 0), stop=(kc == 15)))
                    t_u = T("tensor", "sPE")
                    W("scalar", "sPE", t_g)
                    W("scalar", "sDV", gelu_free[th])
                    E("scalar", lambda e, th=th: e.activation(
                        scr2[:, 1024 + th * 512: 1024 + (th + 1) * 512],
                        pA[th][:, 0:512], AF.Gelu_apprx_tanh))
                    t_gelu = T("scalar", "sAC")
                    W("vector", "sAC", t_gelu)
                    W("vector", "sPE", t_u)
                    E("vector", lambda e, i=i, th=th: e.tensor_tensor(
                        itr[:, i * 1024 + th * 512: i * 1024 + (th + 1) * 512],
                        scr2[:, 1024 + th * 512: 1024 + (th + 1) * 512],
                        pA[2 + th][:, 0:512], ALU.mult))
                    t_ch = T("vector", "sDV")
                    chain_tok[(i, th)] = t_ch
                    gelu_free[th] = t_ch
                    pGU_free[th] = t_ch
                    pGU_free[2 + th] = t_ch
                gu_ld_free[sl] = P.ctr["sPE"].v
            assert len(stage_tok[("d", g)]) == 16
            for q in range(4):
                pump(2)
                W("sync", "sPE", wd_free)
                FE("sync", "sAs")
                for i in range(16):
                    r0 = (16 * g + i) * 128
                    D("sync", "sLw", wbig[:, i * 512:(i + 1) * 512],
                      Gd[r0:r0 + 128, q * 512:(q + 1) * 512])
                for ts_ in range(8):
                    o4 = ts_ % 4
                    reg = pB[o4 // 2][:, (o4 % 2) * 512:(o4 % 2) * 512 + 512]
                    W("tensor", "sDV", fls_dve[o4])
                    if ts_ == 0:
                        FE("tensor", "sLw")
                    for i in range(16):
                        if q == 0 and ts_ == 0:
                            W("tensor", "sDV", chain_tok[(i, 0)])
                            W("tensor", "sDV", chain_tok[(i, 1)])
                        E("tensor", lambda e, reg=reg, i=i, ts_=ts_:
                          e.matmul(reg,
                                   itr[:, i * 1024 + ts_ * 128:
                                       i * 1024 + ts_ * 128 + 128],
                                   wbig[:, i * 512:(i + 1) * 512],
                                   start=(i == 0), stop=(i == 15)))
                    t_dn = T("tensor", "sPE")
                    W("vector", "sPE", t_dn)
                    o2 = ts_ % 2
                    FE("vector", "sGa%d" % o2)
                    E("vector", lambda e, o2=o2, reg=reg: e.tensor_copy(
                        fls[:, o2 * 512:(o2 + 1) * 512], reg))
                    t_fl = T("vector", "sDV")
                    fls_dve[o4] = t_fl
                    W("gpsimd", "sDV", t_fl)
                    kw = {} if g == 0 else {"accum_op": ALU.add}
                    D("gpsimd", "sGa%d" % o2,
                      acc_d[:, ts_ * 2048 + q * 512:
                            ts_ * 2048 + q * 512 + 512],
                      fls[:, o2 * 512:(o2 + 1) * 512], **kw)
                wd_free = P.ctr["sPE"].v

        W("sync", "sDV", P.ctr["sDV"].v)
        FE("sync", "sGa0")
        FE("sync", "sGa1")
        FE("sync", "sAs")
        cmb_free = 0
        last_store = 0
        for ts_ in range(8):
            W("sync", "sDV", cmb_free)
            FE("sync", "sLo")
            t_x = D("sync", "sLx", scr[:, 0:2048],
                    x_in[ts_ * 128:(ts_ + 1) * 128, :])
            t_s = D("sync", "sLx", scr[:, 2048:4096],
                    s_d[:, ts_ * 2048:(ts_ + 1) * 2048])
            atile = scr2[:, 0:2048]
            t_a = D("sync", "sLx", atile, acc_d[:, ts_ * 2048:(ts_ + 1) * 2048])
            FE("scalar", "sLx")
            W("scalar", "sDV", cmb_free)
            E("scalar", lambda e, a=atile: e.activation(a, a, AF.Silu))
            t_sl = T("scalar", "sAC")
            W("vector", "sAC", t_sl)
            FE("vector", "sLx")
            E("vector", lambda e, a=atile: e.tensor_tensor(
                scr[:, 2048:4096], scr[:, 2048:4096], a, ALU.mult))
            t_sg = T("vector", "sDV")
            W("vector", "sDV", t_sg)
            E("vector", lambda e: e.tensor_tensor(
                scr[:, 0:2048], scr[:, 0:2048], scr[:, 2048:4096], ALU.add))
            t_cb = T("vector", "sDV")
            cmb_free = t_cb
            W("sync", "sDV", t_cb)
            last_store = D("sync", "sLo", out_t[ts_ * 128:(ts_ + 1) * 128, :],
                           scr[:, 0:2048])

        dbg_reg = {"xnT_t": None, "qT_d": qT_d, "kT_d": kT_d, "kt_d": kt_d,
                   "v_d": v_d, "akt_d": akt_d, "av_d": av_d, "s_d": s_d,
                   "acc_d": acc_d, "Gg": Gg, "Gu": Gu, "Gd": Gd,
                   "gsh_d": gsh_d, "dsh_d": dsh_d}
        if dbg:
            FE("sync", "sAs")
            FE("sync", "sGa0")
            FE("sync", "sGa1")
            W("sync", "sDV", P.ctr["sDV"].v)
            for name in dbg:
                if name in ("xnT_e", "xnbf0"):
                    continue
                if name == "xnT_t":
                    o = nc.dram_tensor("dbg_xnT", [128, 16 * TC], BF16,
                                       kind="ExternalOutput").ap()
                    last_store = D("sync", "sLo", o, xnT[:, :])
                    continue
                src = dbg_reg[name]
                o = nc.dram_tensor("dbg_" + name, list(src.shape),
                                   src.dtype, kind="ExternalOutput").ap()
                last_store = D("sync", "sLo", o, src)
            FE("sync", "sLo")

        FE("sync", "sLo")
        FE("gpsimd", "sLo")

        assert dec_pos[0] == len(dec_units), (dec_pos[0], len(dec_units))

        @block.tensor
        def _(e):
            for f in P.ops["tensor"]:
                f(e)

        @block.vector
        def _(e):
            for f in P.ops["vector"]:
                f(e)

        @block.scalar
        def _(e):
            for f in P.ops["scalar"]:
                f(e)

        @block.gpsimd
        def _(e):
            for f in P.ops["gpsimd"]:
                f(e)

        @block.sync
        def _(e):
            for f in P.ops["sync"]:
                f(e)

    return nc


def _host_prepack(w_int32):
    b = np.ascontiguousarray(w_int32).view(np.uint8)
    b = b.reshape(w_int32.shape[0], w_int32.shape[1], 4)[:, :, 0]
    bT = np.ascontiguousarray(b.T)
    n = bT.shape[1]
    il = np.stack([np.arange(n // 2), np.arange(n // 2) + n // 2], 1).reshape(-1)
    bTi = np.ascontiguousarray(bT[:, il])
    return bTi.view(np.int16)


_CACHE = {}


def kernel(L0_attn_x, L0_attn_Wq, L0_attn_bq, L0_attn_Wk, L0_attn_bk,
           L0_attn_Wv, L0_attn_bv, L0_ffn_gate_W_packed, L0_ffn_gate_scales,
           L0_ffn_up_W_packed, L0_ffn_up_scales, L0_ffn_down_W_packed,
           L0_ffn_down_scales):
    x = np.ascontiguousarray(np.asarray(L0_attn_x, dtype=np.float32)).reshape(8192, H)
    wq = np.ascontiguousarray(np.asarray(L0_attn_Wq, dtype=np.float32))
    wk = np.ascontiguousarray(np.asarray(L0_attn_Wk, dtype=np.float32))
    wv = np.ascontiguousarray(np.asarray(L0_attn_Wv, dtype=np.float32))
    bq = np.ascontiguousarray(np.asarray(L0_attn_bq, dtype=np.float32)).reshape(1, H)
    bk = np.ascontiguousarray(np.asarray(L0_attn_bk, dtype=np.float32)).reshape(1, H)
    bv = np.ascontiguousarray(np.asarray(L0_attn_bv, dtype=np.float32)).reshape(1, H)
    gp = _host_prepack(np.ascontiguousarray(np.asarray(L0_ffn_gate_W_packed, dtype=np.int32)))
    up = _host_prepack(np.ascontiguousarray(np.asarray(L0_ffn_up_W_packed, dtype=np.int32)))
    dp = _host_prepack(np.ascontiguousarray(np.asarray(L0_ffn_down_W_packed, dtype=np.int32)))
    gs = np.ascontiguousarray(np.asarray(L0_ffn_gate_scales, dtype=np.float32).T)
    us = np.ascontiguousarray(np.asarray(L0_ffn_up_scales, dtype=np.float32).T)
    ds = np.ascontiguousarray(np.asarray(L0_ffn_down_scales, dtype=np.float32).T)

    z_x = np.zeros((TC, H), np.float32)
    z_b = np.zeros((1, H), np.float32)

    key = tuple(DBG)
    if key not in _CACHE:
        _CACHE[key] = build(dbg=key)
    nc = _CACHE[key]

    in_maps = []
    for r in range(NCORES):
        odd = r % 2 == 1
        in_maps.append({
            "x": x[r * TC:(r + 1) * TC],
            "xa": x[(r - 1) * TC: r * TC] if odd else z_x,
            "wq": wq, "wk": wk, "wv": wv,
            "bq": bq, "bk": bk, "bv": bv,
            "bak": bk if odd else z_b, "bav": bv if odd else z_b,
            "gatep": gp, "upp": up, "downp": dp,
            "gates": gs, "ups": us, "downs": ds,
        })

    res = run_bass_kernel_spmd(nc, in_maps, core_ids=list(range(NCORES)))
    _LAST.clear()
    _LAST.update({"results": res.results})
    out = np.concatenate([res.results[r]["out"] for r in range(NCORES)], axis=0)
    return out.reshape(4, 2048, H)


if __name__ == "__main__":
    nc = build()
    n = sum(1 for _ in nc.all_instructions())
    print("built ok; instructions:", n)


# version 06
# speedup vs baseline: 1.0201x; 1.0156x over previous
from contextlib import ExitStack

import numpy as np

import concourse.bass as bass
import concourse.mybir as mybir
from concourse.bass_utils import run_bass_kernel_spmd

F32 = mybir.dt.float32
BF16 = mybir.dt.bfloat16
I16 = mybir.dt.int16
AF = mybir.ActivationFunctionType
ALU = mybir.AluOpType
AX = mybir.AxisListType

NCORES = 8
H = 2048
TC = 1024
HEADS = 8
CHUNK = 256
NCH = 4
MLP = 16384
EPS = 1e-5

_BETAS = np.linspace(-5.0, -9.0, 8).astype(np.float32).astype(np.float64)
GAMMA = 1.0 - np.exp2(_BETAS)
LNG = np.log(GAMMA)
CD = GAMMA ** CHUNK


class Ctr:
    __slots__ = ("v",)

    def __init__(self):
        self.v = 0

    def inc(self, d):
        self.v += d
        return self.v


class Prog:
    ENGINES = ("tensor", "vector", "scalar", "gpsimd", "sync")

    def __init__(self):
        self.ops = {e: [] for e in self.ENGINES}
        self.sem = {}
        self.ctr = {}

    def add_sem(self, name, handle):
        self.sem[name] = handle
        self.ctr[name] = Ctr()

    def emit(self, eng, fn):
        self.ops[eng].append(fn)

    def wait(self, eng, sem_name, val):
        if val is None or val <= 0:
            return
        s = self.sem[sem_name]
        self.emit(eng, lambda e, s=s, v=val: e.wait_ge(s, v))
        self.emit(eng, lambda e: e.nop(nofuse=True))

    def fence(self, eng, sem_name):
        self.wait(eng, sem_name, self.ctr[sem_name].v)

    def dma(self, eng, sem_name, out, in_, **kw):
        s = self.sem[sem_name]
        tok = self.ctr[sem_name].inc(16)
        self.emit(eng, lambda e, s=s, o=out, i=in_, kw=kw:
                  e.dma_start(out=o, in_=i, **kw).then_inc(s, 16))
        return tok

    def tag(self, eng, sem_name):
        s = self.sem[sem_name]
        tok = self.ctr[sem_name].inc(1)
        prev = self.ops[eng].pop()
        self.emit(eng, lambda e, p=prev, s=s: p(e).then_inc(s, 1))
        return tok


DBG = []
_LAST = {}


def build(dbg=()):
    nc = bass.Bass("TRN2", target_bir_lowering=False)

    def din(name, shape, dt):
        return nc.dram_tensor(name, list(shape), dt, kind="ExternalInput").ap()

    x_in = din("x", (TC, H), F32)
    xa_in = din("xa", (TC, H), F32)
    wq_in = din("wq", (H, H), F32)
    wk_in = din("wk", (H, H), F32)
    wv_in = din("wv", (H, H), F32)
    b_ins = {"q": din("bq", (1, H), F32), "k": din("bk", (1, H), F32),
             "v": din("bv", (1, H), F32), "ak": din("bak", (1, H), F32),
             "av": din("bav", (1, H), F32)}
    gp_in = din("gatep", (1024, 8192), I16)
    up_in = din("upp", (1024, 8192), I16)
    dp_in = din("downp", (8192, 1024), I16)
    gs_in = din("gates", (64, MLP), F32)
    us_in = din("ups", (64, MLP), F32)
    ds_in = din("downs", (512, H), F32)

    out_t = nc.dram_tensor("out", [TC, H], F32, kind="ExternalOutput").ap()

    def dint(name, shape, dt):
        return nc.dram_tensor(name, list(shape), dt).ap()

    qT_d = dint("qT_d", (128, 16 * TC), BF16)
    kT_d = dint("kT_d", (128, 16 * TC), BF16)
    kt_d = dint("kt_d", (128, 8 * H), BF16)
    v_d = dint("v_d", (128, 8 * H), BF16)
    akt_d = dint("akt_d", (128, 8 * H), BF16)
    av_d = dint("av_d", (128, 8 * H), BF16)
    s_d = dint("s_d", (128, 8 * H), F32)
    acc_d = dint("acc_d", (128, 8 * H), F32)
    Gg = dint("Gg", (H, MLP), BF16)
    Gu = dint("Gu", (H, MLP), BF16)
    Gd = dint("Gd", (MLP, H), BF16)
    gsh_d = dint("gsh_d", (64, MLP), BF16)
    ush_d = dint("ush_d", (64, MLP), BF16)
    dsh_d = dint("dsh_d", (512, H), BF16)

    with ExitStack() as ctx:
        sbt = lambda n, s, d: ctx.enter_context(nc.sbuf_tensor(n, list(s), d))
        pst = lambda n, s, d: ctx.enter_context(nc.psum_tensor(n, list(s), d))

        xnT = sbt("xnT", (128, 16 * TC), BF16)
        wbig = sbt("wbig", (128, 8 * 1024), BF16)
        scr = sbt("scr", (128, 4096), F32)
        scr2 = sbt("scr2", (128, 2048), F32)
        pk_sb = sbt("pk_sb", (128, 2 * 1024), I16)
        dreg = [sbt(f"dreg{i}", (128, 1024), I16) for i in range(8)]
        slab = sbt("slab_sb", (128, 2 * 1024), BF16)
        srep = sbt("srep_sb", (128, 2 * 1024), BF16)
        qe = sbt("qe", (128, 2 * 1024), BF16)
        Ssb = sbt("Ssb", (128, 16 * 256), F32)
        Sbf = sbt("Sbf", (128, 2 * 256), BF16)
        ksb = sbt("ksb", (128, 2 * 256), BF16)
        attnM = sbt("attnM", (128, 2 * 256), BF16)
        qs_sb = sbt("qs_sb", (128, 2 * 256), BF16)
        gu_sb = [sbt(f"gu{i}", (128, 16 * 512), BF16) for i in range(2)]
        itr = sbt("itr", (128, 16 * 1024), BF16)
        xanT = itr
        cons = sbt("cons", (128, 8), F32)
        csb = sbt("csb", (128, 8 * 256), F32)
        kvc = sbt("kvc", (128, 16), F32)
        dmsk = sbt("dmsk", (128, 2 * 256), F32)
        ident = sbt("ident", (128, 128), BF16)
        onesb = sbt("onesb", (128, 512), BF16)
        brow = sbt("brow", (128, H), BF16)
        iop = sbt("iop", (128, 2), F32)
        lst = sbt("lst", (128, 16), F32)
        fls = sbt("fls", (128, 2 * 512), F32)

        dtt = scr2[:, 0:512]
        ioc = scr2[:, 512:768]

        pA = [pst(f"pA{i}", (128, 512), F32) for i in range(4)]
        pB = [pst(f"pB{i}", (128, 1024), F32) for i in range(2)]
        p_upd = pB[0][:, 0:512]

        P = Prog()
        for s in ("sLx", "sLp0", "sLp1", "sGs0", "sGs1", "sLg0", "sLg1",
                  "sLw", "sGw0", "sGw1", "sGb", "sAq0", "sAq1", "sAs",
                  "sGa0", "sGa1", "sLo",
                  "sPE", "sDV", "sAC", "sGP"):
            P.add_sem(s, ctx.enter_context(nc.semaphore(s)))

        block = ctx.enter_context(nc.Block())
        E, W, D, T, FE = P.emit, P.wait, P.dma, P.tag, P.fence

        mask_free = [0]
        attn_free = [0]
        ks_free = [0]
        upd_free = [0]
        gelu_free = [0, 0]
        fls_dve = [0, 0, 0, 0]
        fls_gd = [0, 0]

        for col, val in {0: -4.0, 1: -16.0, 2: 1.0, 3: EPS}.items():
            E("gpsimd", lambda e, c=col, v=val: e.memset(cons[:, c:c + 1], v))
        E("gpsimd", lambda e: e.memset(onesb[0:1, :], 1.0))
        E("gpsimd", lambda e: e.memset(Ssb[:, :], 0.0))
        E("gpsimd", lambda e: e.iota(ioc, pattern=[[1, 256]], base=0,
                                     channel_multiplier=0,
                                     allow_small_or_imprecise_dtypes=True))
        E("gpsimd", lambda e: e.iota(iop[:, 0:1], pattern=[[0, 1]], base=0,
                                     channel_multiplier=1,
                                     allow_small_or_imprecise_dtypes=True))
        for jc in range(2):
            E("gpsimd", lambda e, jc=jc: e.iota(
                dmsk[:, jc * 256:(jc + 1) * 256], pattern=[[1, 256]],
                base=-128 * jc, channel_multiplier=-1,
                allow_small_or_imprecise_dtypes=True))
            t_io = T("gpsimd", "sGP")
            W("gpsimd", "sGP", t_io)
            E("gpsimd", lambda e, jc=jc: e.affine_select(
                dmsk[:, jc * 256:(jc + 1) * 256], dmsk[:, jc * 256:(jc + 1) * 256],
                pattern=[[1, 256]], compare_op=ALU.is_ge, fill=1.0e9,
                base=-128 * jc, channel_multiplier=-1))
        E("gpsimd", lambda e: e.iota(scr2[:, 768:896], pattern=[[1, 128]], base=0,
                                     channel_multiplier=-1,
                                     allow_small_or_imprecise_dtypes=True))
        tok_gp0 = T("gpsimd", "sGP")

        W("vector", "sGP", tok_gp0)
        E("vector", lambda e: e.tensor_scalar(scr2[:, 896:1024], scr2[:, 768:896],
                                              0.0, None, ALU.is_equal))
        t_ieq = T("vector", "sDV")
        W("vector", "sDV", t_ieq)
        E("vector", lambda e: e.tensor_copy(ident[:, :], scr2[:, 896:1024]))
        tok_ident = T("vector", "sDV")

        W("scalar", "sGP", tok_gp0)
        for h in range(HEADS):
            E("scalar", lambda e, h=h: e.activation(
                csb[:, h * 256:(h + 1) * 256], ioc, AF.Exp, scale=float(LNG[h])))
        for tsl in range(2):
            for h in range(HEADS):
                col = tsl * 8 + h
                E("scalar", lambda e, h=h, col=col: e.activation(
                    kvc[:, col:col + 1], iop[:, 0:1], AF.Exp, scale=-float(LNG[h])))
        tok_exp = T("scalar", "sAC")
        W("vector", "sAC", tok_exp)
        for h in range(HEADS):
            E("vector", lambda e, h=h: e.tensor_scalar(
                csb[:, h * 256:(h + 1) * 256], csb[:, h * 256:(h + 1) * 256],
                float(GAMMA[h]), None, ALU.mult))
            E("vector", lambda e: e.nop())
        for tsl in range(2):
            for h in range(HEADS):
                col = tsl * 8 + h
                mv = float(GAMMA[h] ** (255 - 128 * tsl))
                E("vector", lambda e: e.nop())
                E("vector", lambda e, col=col, mv=mv: e.tensor_scalar(
                    kvc[:, col:col + 1], kvc[:, col:col + 1], mv, None, ALU.mult))
        T("vector", "sDV")

        last_sc_st = 0
        sc_chunks = []
        for src, dst, rows, cols in ((gs_in, gsh_d, 64, MLP),
                                     (us_in, ush_d, 64, MLP),
                                     (ds_in, dsh_d, 512, H)):
            for r0 in range(0, rows, 64):
                rr = min(64, rows - r0)
                for c0 in range(0, cols, 4096):
                    cc = min(4096, cols - c0)
                    sc_chunks.append((src, dst, r0, rr, c0, cc))
        sc_st_tok = [0, 0]
        sc_ld_tok = [0, 0]
        for ci, (src, dst, r0, rr, c0, cc) in enumerate(sc_chunks):
            pp = ci % 2
            rb = pp * 64
            W("sync", "sDV", sc_st_tok[pp])
            t_ld = D("sync", "sLx", scr[rb:rb + rr, 0:cc],
                     src[r0:r0 + rr, c0:c0 + cc])
            W("vector", "sLx", t_ld)
            W("vector", "sAs", sc_ld_tok[pp])
            obuf = (qe[rb:rb + rr, 0:cc] if cc <= 2048 else
                    wbig[0:rr, pp * 4096: pp * 4096 + cc])
            E("vector", lambda e, rr=rr, cc=cc, rb=rb, obuf=obuf: e.tensor_scalar(
                obuf, scr[rb:rb + rr, 0:cc], 0.5, None, ALU.mult))
            t_h = T("vector", "sDV")
            sc_st_tok[pp] = t_h
            E("vector", lambda e: e.nop())
            W("scalar", "sDV", t_h)
            last_sc_st = D("scalar", "sAs", dst[r0:r0 + rr, c0:c0 + cc], obuf)
            sc_ld_tok[pp] = last_sc_st
        tok_scales_st = last_sc_st
        tok_scales_dve = P.ctr["sDV"].v

        def load_bias(name, wait_pe=None):
            if wait_pe:
                W("gpsimd", "sPE", wait_pe)
            return D("gpsimd", "sGb", brow[0:1, 0:H], b_ins[name])

        gsh_aps = {"g": gsh_d, "u": ush_d}
        pk_aps = {"g": gp_in, "u": up_in}
        G_aps = {"g": Gg, "u": Gu}
        stage_tok = {(m, g): [] for m in ("g", "u", "d") for g in range(8)}
        ds_pk_free = [0, 0]
        ds_srep_free = [0, 0]
        ds_slab_free = [0, 0]
        ds_slot = [0]
        FD = 1024

        def emit_chain_pair(planes, pk_ap):
            for j, (sh, _, _) in enumerate(planes):
                r = dreg[4 * j]
                if sh:
                    E("vector", lambda e, r=r, sh=sh: e.tensor_scalar(
                        r[:, :], pk_ap, sh, 15,
                        ALU.logical_shift_right, ALU.bitwise_and))
                else:
                    E("vector", lambda e, r=r: e.tensor_scalar(
                        r[:, :], pk_ap, 15, None, ALU.bitwise_and))
            for j in range(len(planes)):
                c0, m = dreg[4 * j], dreg[4 * j + 1]
                E("vector", lambda e, c0=c0, m=m: e.tensor_scalar(
                    m[:, :], c0[:, :], 7, None, ALU.bitwise_and))
            tok_m = T("vector", "sDV")
            W("scalar", "sDV", tok_m)
            for j in range(len(planes)):
                m, a, b = dreg[4 * j + 1], dreg[4 * j + 2], dreg[4 * j + 3]
                E("scalar", lambda e, m=m, a=a: e.activation(
                    a[:, :], m[:, :], AF.Identity, scale=2.0, bias=cons[:, 0:1]))
                E("scalar", lambda e, m=m, b=b: e.activation(
                    b[:, :], m[:, :], AF.Identity, scale=4.0, bias=cons[:, 1:2]))
            tok_ab = T("scalar", "sAC")
            W("vector", "sAC", tok_ab)
            for j in range(len(planes)):
                a, b = dreg[4 * j + 2], dreg[4 * j + 3]
                E("vector", lambda e, a=a, b=b: e.tensor_tensor(
                    a[:, :], a[:, :], b[:, :], ALU.max))
            for j in range(len(planes)):
                m, a = dreg[4 * j + 1], dreg[4 * j + 2]
                E("vector", lambda e, m=m, a=a: e.tensor_tensor(
                    a[:, :], m[:, :], a[:, :], ALU.max))
            for j in range(len(planes)):
                c0, b = dreg[4 * j], dreg[4 * j + 3]
                E("vector", lambda e, c0=c0, b=b: e.tensor_scalar(
                    b[:, :], c0[:, :], 3, 1,
                    ALU.logical_shift_right, ALU.bitwise_and))
            for j in range(len(planes)):
                b = dreg[4 * j + 3]
                E("vector", lambda e, b=b: e.tensor_scalar(
                    b[:, :], b[:, :], -2, 1, ALU.mult, ALU.add))
            for j in range(len(planes)):
                a, b = dreg[4 * j + 2], dreg[4 * j + 3]
                E("vector", lambda e, a=a, b=b: e.tensor_tensor(
                    a[:, :], a[:, :], b[:, :], ALU.mult))
            for j, (_, slab_sl, srep_sl) in enumerate(planes):
                a = dreg[4 * j + 2]
                E("vector", lambda e, a=a, o=slab_sl, sr=srep_sl:
                  e.tensor_tensor(o, a[:, :], sr, ALU.mult))
            return T("vector", "sDV")

        def emit_unit_gu(mat, g, c, ch):
            b = g // 4
            frq = 2048 * (g % 4) + 1024 * ch
            n0 = 2048 * g + 1024 * ch
            slot = ds_slot[0]
            ds_slot[0] ^= 1
            W("sync", "sDV", ds_pk_free[slot])
            pk_sl = pk_sb[:, slot * 1024: slot * 1024 + 1024]
            t_pk = D("sync", "sLp%d" % slot, pk_sl,
                     pk_aps[mat][c * 128:(c + 1) * 128, frq:frq + 1024])
            if ds_srep_free[slot] == 0:
                W("gpsimd", "sAs", tok_scales_st)
            W("gpsimd", "sDV", ds_srep_free[slot])
            sr_sl = srep[:, slot * 1024: slot * 1024 + 1024]
            t_sr = D("gpsimd", "sGs%d" % slot, sr_sl,
                     gsh_aps[mat][8 * c:8 * c + 8, n0:n0 + 1024]
                     .unsqueeze(1).broadcast_to([8, 16, 1024]))
            FE("vector", "sLp%d" % slot)
            FE("vector", "sGs%d" % slot)
            FE("vector", "sAs")
            planes = [(8 * b, slab[:, 0:1024], sr_sl),
                      (8 * b + 4, slab[:, 1024:2048], sr_sl)]
            tok_w = emit_chain_pair(planes, pk_sl)
            ds_pk_free[slot] = tok_w
            ds_srep_free[slot] = tok_w
            W("scalar", "sDV", tok_w)
            Gm = G_aps[mat]
            ds_slab_free[0] = D("scalar", "sAs",
                                Gm[256 * c: 256 * c + 256: 2, n0:n0 + 1024],
                                slab[:, 0:1024])
            ds_slab_free[1] = D("scalar", "sAs",
                                Gm[256 * c + 1: 256 * c + 256: 2, n0:n0 + 1024],
                                slab[:, 1024:2048])
            stage_tok[(mat, g)].append(ds_slab_free[1])

        def emit_unit_down(g, cr, pairi):
            cp = 8 * g + cr
            slot = ds_slot[0]
            ds_slot[0] ^= 1
            W("sync", "sDV", ds_pk_free[slot])
            pk_sl = pk_sb[:, slot * 1024: slot * 1024 + 1024]
            t_pk = D("sync", "sLp%d" % slot, pk_sl,
                     dp_in[cp * 128:(cp + 1) * 128, 0:1024])
            if ds_srep_free[slot] == 0:
                W("gpsimd", "sAs", tok_scales_st)
            W("gpsimd", "sDV", ds_srep_free[slot])
            sr_sl = srep[:, slot * 1024: slot * 1024 + 1024]
            t_sr = D("gpsimd", "sGs%d" % slot, sr_sl,
                     dsh_d[8 * cp:8 * cp + 8, pairi * 1024:(pairi + 1) * 1024]
                     .unsqueeze(1).broadcast_to([8, 16, 1024]))
            FE("vector", "sLp%d" % slot)
            FE("vector", "sGs%d" % slot)
            FE("vector", "sAs")
            planes = [(8 * pairi, slab[:, 0:1024], sr_sl),
                      (8 * pairi + 4, slab[:, 1024:2048], sr_sl)]
            tok_w = emit_chain_pair(planes, pk_sl)
            ds_pk_free[slot] = tok_w
            ds_srep_free[slot] = tok_w
            W("scalar", "sDV", tok_w)
            t_last = 0
            for nib in range(2):
                sl = slab[:, nib * 1024:(nib + 1) * 1024]
                t_last = D("scalar", "sAs",
                           Gd[256 * cp + nib: 256 * cp + 256: 2,
                              pairi * 1024: (pairi + 1) * 1024], sl)
            ds_slab_free[0] = t_last
            ds_slab_free[1] = t_last
            stage_tok[("d", g)].append(t_last)

        dec_units = []
        for g in range(8):
            for c in range(8):
                for ch in range(2):
                    dec_units.append(("g", g, c, ch))
                    dec_units.append(("u", g, c, ch))
            for cr in range(8):
                for pairi in range(2):
                    dec_units.append(("d", g, cr, pairi))
        dec_pos = [0]
        dec_on = [False]

        def pump(n):
            if not dec_on[0]:
                return
            for _ in range(n):
                if dec_pos[0] >= len(dec_units):
                    return
                kind, g, c, x2 = dec_units[dec_pos[0]]
                dec_pos[0] += 1
                if kind == "d":
                    emit_unit_down(g, c, x2)
                else:
                    emit_unit_gu(kind, g, c, x2)

        x_sl = scr[:, 0:2048]
        sq_sl = scr[:, 2048:4096]
        xnbf = wbig[:, 0:2048]
        slot_free = tok_scales_dve
        xnbf_pe_free = 0
        dec_on[0] = True
        FE("vector", "sAs")
        W("scalar", "sDV", tok_scales_dve)
        FE("scalar", "sAs")
        tr_evac_tok = {0: 0, 1: 0}
        tpsum = [pB[0][:, 0:64].bitcast(BF16), pB[1][:, 0:64].bitcast(BF16)]

        for kind, tt in [("m", t) for t in range(8)] + [("a", t) for t in range(8)]:
            src = x_in if kind == "m" else xa_in
            W("sync", "sDV", slot_free)
            t_ld = D("sync", "sLx", x_sl, src[tt * 128:(tt + 1) * 128, :])
            FE("vector", "sLx")
            FE("scalar", "sLx")
            E("vector", lambda e: e.tensor_reduce(lst[:, 0:1], x_sl, AX.X, ALU.add))
            t_s0 = T("vector", "sDV")
            E("scalar", lambda e: e.activation(sq_sl, x_sl, AF.Square))
            t_sq = T("scalar", "sAC")
            W("vector", "sAC", t_sq)
            E("vector", lambda e: e.tensor_reduce(lst[:, 1:2], sq_sl, AX.X, ALU.add))
            W("vector", "sDV", t_s0)
            E("vector", lambda e: e.tensor_scalar(lst[:, 2:3], lst[:, 0:1],
                                                  1.0 / H, None, ALU.mult))
            t_ms = T("vector", "sDV")
            W("scalar", "sDV", t_ms)
            E("scalar", lambda e: e.activation(lst[:, 3:4], lst[:, 1:2],
                                               AF.Identity, scale=1.0 / H))
            E("scalar", lambda e: e.activation(lst[:, 4:5], lst[:, 2:3],
                                               AF.Square))
            t_m2 = T("scalar", "sAC")
            W("vector", "sAC", t_m2)
            E("vector", lambda e: e.tensor_tensor(lst[:, 5:6], lst[:, 3:4],
                                                  lst[:, 4:5], ALU.subtract))
            t_var = T("vector", "sDV")
            W("scalar", "sDV", t_var)
            E("scalar", lambda e: e.activation(lst[:, 6:7], lst[:, 5:6],
                                               AF.Sqrt, bias=cons[:, 3:4]))
            t_std = T("scalar", "sAC")
            W("vector", "sAC", t_std)
            E("vector", lambda e: e.reciprocal(lst[:, 7:8], lst[:, 6:7]))
            t_rc = T("vector", "sDV")
            W("vector", "sDV", t_rc)
            W("vector", "sPE", xnbf_pe_free)
            E("vector", lambda e: e.tensor_scalar(xnbf, x_sl, lst[:, 2:3],
                                                  lst[:, 7:8], ALU.subtract,
                                                  ALU.mult))
            t_xn = T("vector", "sDV")
            slot_free = t_xn
            if kind == "m" and tt == 0 and "xnbf0" in dbg:
                dbg_x0 = nc.dram_tensor("dbg_x0", [128, 2048], F32,
                                        kind="ExternalOutput").ap()
                dbg_xb0 = nc.dram_tensor("dbg_xnbf0", [128, 2048], BF16,
                                         kind="ExternalOutput").ap()
                dbg_lst0 = nc.dram_tensor("dbg_lst0", [128, 16], F32,
                                          kind="ExternalOutput").ap()
                W("sync", "sDV", t_xn)
                tt_x0 = D("sync", "sLo", dbg_x0, x_sl)
                tt_xb = D("sync", "sLo", dbg_xb0, xnbf)
                tt_ls = D("sync", "sLo", dbg_lst0, lst[:, 0:16])
                FE("vector", "sLo")
                FE("tensor", "sLo")
            dstT = xnT if kind == "m" else xanT
            W("tensor", "sDV", t_xn)
            W("tensor", "sDV", tok_ident)
            for kc in range(16):
                pp = kc % 2
                W("tensor", "sDV", tr_evac_tok[pp])
                E("tensor", lambda e, kc=kc, pp=pp: e.transpose(
                    tpsum[pp], xnbf[:, kc * 128:(kc + 1) * 128], ident[:, :]))
                t_tr = T("tensor", "sPE")
                W("vector", "sPE", t_tr)
                E("vector", lambda e, dstT=dstT, kc=kc, tt=tt, pp=pp:
                  e.tensor_copy(
                      dstT[:, kc * TC + tt * 128: kc * TC + (tt + 1) * 128],
                      tpsum[pp]))
                tr_evac_tok[pp] = T("vector", "sDV")
            xnbf_pe_free = P.ctr["sPE"].v
            pump(2)
        xnT_ready = max(tr_evac_tok[0], tr_evac_tok[1])

        if "xnT_e" in dbg:
            dbg_xe = nc.dram_tensor("dbg_xnT_e", [128, 16 * TC], BF16,
                                    kind="ExternalOutput").ap()
            W("sync", "sDV", xnT_ready)
            D("sync", "sLo", dbg_xe, xnT[:, :])
        FE("gpsimd", "sAs")
        dec_on[0] = True

        pB_free = [xnT_ready, xnT_ready]
        bias_tok = [0]
        FE("vector", "sAs")

        def emit_fm(w_ap, bias_slot, dst_d):
            wslot_free = [P.ctr["sPE"].v, P.ctr["sPE"].v]
            for ocg in range(8):
                ws = ocg % 2
                W("gpsimd", "sPE", wslot_free[ws])
                t_w = D("gpsimd", "sGw%d" % ws, gu_sb[ws][:, 0:16 * 256].rearrange(
                    "p (c n) -> p c n", n=256),
                    w_ap.rearrange("(c p) n -> p c n", p=128)
                    [:, :, ocg * 256:(ocg + 1) * 256])
                pump(1)
                for oc2 in range(2):
                    oc = 2 * ocg + oc2
                    FE("tensor", "sGw%d" % ws)
                    FE("tensor", "sGb")
                    W("tensor", "sDV", pB_free[oc2])
                    for tb in range(2):
                        for kc in range(16):
                            E("tensor", lambda e, oc2=oc2, tb=tb, kc=kc, ws=ws:
                              e.matmul(pB[oc2][:, tb * 512:(tb + 1) * 512],
                                       gu_sb[ws][:, kc * 256 + oc2 * 128:
                                                 kc * 256 + oc2 * 128 + 128],
                                       xnT[:, kc * TC + tb * 512:
                                           kc * TC + (tb + 1) * 512],
                                       start=(kc == 0), stop=False))
                        E("tensor", lambda e, oc2=oc2, tb=tb, oc=oc, bs=bias_slot:
                          e.matmul(pB[oc2][:, tb * 512:(tb + 1) * 512],
                                   brow[0:1, oc * 128: oc * 128 + 128],
                                   onesb[0:1, 0:512], start=False, stop=True))
                    t_pe = T("tensor", "sPE")
                    W("vector", "sPE", t_pe)
                    FE("vector", "sAq%d" % oc2)
                    E("vector", lambda e, oc2=oc2: e.tensor_copy(
                        qe[:, oc2 * 1024:(oc2 + 1) * 1024], pB[oc2][:, :]))
                    t_ev = T("vector", "sDV")
                    pB_free[oc2] = t_ev
                    W("scalar", "sDV", t_ev)
                    D("scalar", "sAq%d" % oc2,
                      dst_d[:, oc * 1024:(oc + 1) * 1024],
                      qe[:, oc2 * 1024:(oc2 + 1) * 1024])
                    wslot_free[ws] = t_pe

        bias_tok[0] = load_bias("q")
        emit_fm(wq_in, 0, qT_d)
        bias_tok[0] = load_bias("k", wait_pe=P.ctr["sPE"].v)
        emit_fm(wk_in, 0, kT_d)

        tm_wfree = [P.ctr["sPE"].v] * 16

        def emit_tm(w_ap, bias_main, bias_alien, main_dst, alien_dst):
            for dq in range(4):
                w_toks = []
                for kc in range(16):
                    W("gpsimd", "sPE", tm_wfree[kc])
                    w_toks.append(D("gpsimd", "sGw0",
                                    wbig[:, kc * 512:(kc + 1) * 512],
                                    w_ap.rearrange("(c p) n -> p c n", p=128)
                                    [:, kc, dq * 512:(dq + 1) * 512]))
                    if kc % 4 == 3:
                        pump(1)
                for srcT, bname, dst_d in ((xnT, bias_main, main_dst),
                                           (xanT, bias_alien, alien_dst)):
                    bias_tok[0] = load_bias(bname, wait_pe=P.ctr["sPE"].v)
                    for tsub in range(8):
                        o2 = tsub % 2
                        reg = pB[o2][:, 0:512]
                        W("tensor", "sDV", pB_free[o2])
                        FE("tensor", "sGb")
                        FE("tensor", "sGw0")
                        for kc in range(16):
                            E("tensor", lambda e, reg=reg, kc=kc, tsub=tsub,
                              srcT=srcT:
                              e.matmul(reg,
                                       srcT[:, kc * TC + tsub * 128:
                                            kc * TC + tsub * 128 + 128],
                                       wbig[:, kc * 512:(kc + 1) * 512],
                                       start=(kc == 0), stop=False))
                        E("tensor", lambda e, reg=reg, dq=dq:
                          e.matmul(reg, onesb[0:1, 0:128],
                                   brow[0:1, dq * 512:(dq + 1) * 512],
                                   start=False, stop=True))
                        t_pe = T("tensor", "sPE")
                        W("vector", "sPE", t_pe)
                        FE("vector", "sAq%d" % o2)
                        E("vector", lambda e, o2=o2, reg=reg: e.tensor_copy(
                            qe[:, o2 * 1024: o2 * 1024 + 512], reg))
                        t_ev = T("vector", "sDV")
                        pB_free[o2] = t_ev
                        W("scalar", "sDV", t_ev)
                        D("scalar", "sAq%d" % o2,
                          dst_d[:, tsub * 2048 + dq * 512:
                                tsub * 2048 + (dq + 1) * 512],
                          qe[:, o2 * 1024: o2 * 1024 + 512])
                    pump(2)
                for kc in range(16):
                    tm_wfree[kc] = P.ctr["sPE"].v

        emit_tm(wk_in, "k", "ak", kt_d, akt_d)
        emit_tm(wv_in, "v", "av", v_d, av_d)

        qT_v = qT_d.rearrange("p (c t) -> p c t", t=TC)
        kT_v = kT_d.rearrange("p (c t) -> p c t", t=TC)

        pA_free = [pB_free[0]] * 4
        upd_free[0] = pB_free[0]
        scan_pe_free = P.ctr["sPE"].v
        scan_dve_free = P.ctr["sDV"].v
        last_upd = P.ctr["sDV"].v

        for phase, src_kt, src_v in (("a", akt_d, av_d), ("m", kt_d, v_d)):
            for c in range(NCH):
                FE("sync", "sAq0")
                FE("sync", "sAq1")
                W("sync", "sPE", scan_pe_free)
                W("sync", "sDV", last_upd)
                t_kt = D("sync", "sLx", wbig[:, 0:4096],
                         src_kt[:, (2 * c) * 2048:(2 * c + 2) * 2048])
                t_v = D("sync", "sLx", wbig[:, 4096:8192],
                        src_v[:, (2 * c) * 2048:(2 * c + 2) * 2048])
                t_q = t_k2 = None
                if phase == "m":
                    t_q = D("sync", "sLx", gu_sb[0][:, 0:4096].rearrange(
                        "p (k t) -> p k t", t=256),
                        qT_v[:, :, c * 256:(c + 1) * 256])
                    t_k2 = D("sync", "sLx", gu_sb[0][:, 4096:8192].rearrange(
                        "p (k t) -> p k t", t=256),
                        kT_v[:, :, c * 256:(c + 1) * 256])
                pump(1)
                for h in range(HEADS):
                    hp = h % 2
                    cdh = float(CD[h])
                    if phase == "m":
                        W("vector", "sDV", last_upd)
                        E("vector", lambda e, h=h: e.tensor_copy(
                            Sbf[:, :], Ssb[:, h * 512:(h + 1) * 512]))
                        t_sbf = T("vector", "sDV")
                        W("scalar", "sDV", mask_free[0])
                        for jc in range(2):
                            E("scalar", lambda e, h=h, jc=jc: e.activation(
                                dtt[:, jc * 256:(jc + 1) * 256],
                                dmsk[:, jc * 256:(jc + 1) * 256], AF.Exp,
                                scale=float(LNG[h])))
                        t_dt = T("scalar", "sAC")
                        FE("tensor", "sLx")
                        W("tensor", "sDV", pA_free[hp])
                        for jc in range(2):
                            for dc in range(2):
                                E("tensor", lambda e, h=h, jc=jc, dc=dc, hp=hp:
                                  e.matmul(pA[hp][:, jc * 256:(jc + 1) * 256],
                                           gu_sb[0][:, 4096 + (2 * h + dc) * 256 +
                                                 jc * 128:
                                                 4096 + (2 * h + dc) * 256 +
                                                 jc * 128 + 128],
                                           gu_sb[0][:, (2 * h + dc) * 256:
                                                 (2 * h + dc) * 256 + 256],
                                           start=(dc == 0), stop=(dc == 1)))
                        t_at = T("tensor", "sPE")
                        W("vector", "sPE", t_at)
                        W("vector", "sPE", attn_free[0])
                        W("vector", "sAC", t_dt)
                        for jc in range(2):
                            E("vector", lambda e, jc=jc, hp=hp: e.tensor_tensor(
                                attnM[:, jc * 256:(jc + 1) * 256],
                                pA[hp][:, jc * 256:(jc + 1) * 256],
                                dtt[:, jc * 256:(jc + 1) * 256], ALU.mult))
                            E("vector", lambda e, h=h, jc=jc: e.tensor_tensor(
                                qs_sb[:, jc * 256:(jc + 1) * 256],
                                gu_sb[0][:, (2 * h + jc) * 256:
                                      (2 * h + jc) * 256 + 256],
                                csb[:, h * 256:(h + 1) * 256], ALU.mult))
                        t_mq = T("vector", "sDV")
                        pA_free[hp] = t_mq
                        mask_free[0] = t_mq
                        W("tensor", "sDV", t_mq)
                        W("tensor", "sDV", t_sbf)
                        W("tensor", "sDV", pA_free[2 + hp])
                        for ih in range(2):
                            for jc in range(2):
                                E("tensor", lambda e, ih=ih, jc=jc, h=h, hp=hp:
                                  e.matmul(pA[2 + hp][:, ih * 256:(ih + 1) * 256],
                                           attnM[:, jc * 256 + ih * 128:
                                                 jc * 256 + ih * 128 + 128],
                                           wbig[:, 4096 + jc * 2048 + h * 256:
                                                4096 + jc * 2048 + h * 256 + 256],
                                           start=(jc == 0), stop=False))
                            for dc in range(2):
                                E("tensor", lambda e, ih=ih, dc=dc, hp=hp:
                                  e.matmul(pA[2 + hp][:, ih * 256:(ih + 1) * 256],
                                           qs_sb[:, dc * 256 + ih * 128:
                                                 dc * 256 + ih * 128 + 128],
                                           Sbf[:, dc * 256:(dc + 1) * 256],
                                           start=False, stop=(dc == 1)))
                        t_out = T("tensor", "sPE")
                        attn_free[0] = t_out
                        W("vector", "sPE", t_out)
                        FE("vector", "sAs")
                        for ih in range(2):
                            E("vector", lambda e, ih=ih, h=h, hp=hp:
                              e.tensor_copy(
                                  scr[:, ih * 2048 + h * 256:
                                      ih * 2048 + h * 256 + 256],
                                  pA[2 + hp][:, ih * 256:(ih + 1) * 256]))
                        t_sev = T("vector", "sDV")
                        pA_free[2 + hp] = t_sev
                    FE("vector", "sLx")
                    W("vector", "sPE", ks_free[0])
                    for tsl in range(2):
                        E("vector", lambda e: e.nop())
                        E("vector", lambda e, tsl=tsl, h=h: e.tensor_scalar(
                            ksb[:, tsl * 256:(tsl + 1) * 256],
                            wbig[:, tsl * 2048 + h * 256:
                                 tsl * 2048 + h * 256 + 256],
                            kvc[:, (tsl * 8 + h):(tsl * 8 + h) + 1], None,
                            ALU.mult))
                    t_ks = T("vector", "sDV")
                    W("tensor", "sDV", t_ks)
                    W("tensor", "sDV", upd_free[0])
                    FE("tensor", "sLx")
                    for dh in range(2):
                        for tsl in range(2):
                            E("tensor", lambda e, dh=dh, tsl=tsl, h=h:
                              e.matmul(p_upd[:, dh * 256:(dh + 1) * 256],
                                       ksb[:, tsl * 256 + dh * 128:
                                           tsl * 256 + dh * 128 + 128],
                                       wbig[:, 4096 + tsl * 2048 + h * 256:
                                            4096 + tsl * 2048 + h * 256 + 256],
                                       start=(tsl == 0), stop=(tsl == 1)))
                    t_upd = T("tensor", "sPE")
                    ks_free[0] = t_upd
                    W("vector", "sPE", t_upd)
                    for dh in range(2):
                        E("vector", lambda e: e.nop())
                        E("vector", lambda e, dh=dh, h=h, cdh=cdh:
                          e.scalar_tensor_tensor(
                              Ssb[:, h * 512 + dh * 256:
                                  h * 512 + dh * 256 + 256],
                              Ssb[:, h * 512 + dh * 256:
                                  h * 512 + dh * 256 + 256],
                              cdh, p_upd[:, dh * 256:(dh + 1) * 256],
                              ALU.mult, ALU.add))
                    last_upd = T("vector", "sDV")
                    upd_free[0] = last_upd
                if phase == "m":
                    W("scalar", "sDV", last_upd)
                    D("scalar", "sAs",
                      s_d[:, (2 * c) * 2048:(2 * c + 2) * 2048],
                      scr[:, 0:4096])
                scan_pe_free = P.ctr["sPE"].v
                pump(2)

        Gg_v = Gg.rearrange("(c p) n -> p c n", p=128)
        Gu_v = Gu.rearrange("(c p) n -> p c n", p=128)
        chain_tok = {}
        gu_ld_free = [P.ctr["sPE"].v, P.ctr["sPE"].v]
        pGU_free = [P.ctr["sDV"].v] * 4
        for i in range(4):
            fls_dve[i] = P.ctr["sDV"].v
        gelu_free[0] = gelu_free[1] = P.ctr["sDV"].v
        wd_free = P.ctr["sPE"].v
        gu_tok = (0, 0)

        for g in range(8):
            assert len(stage_tok[("g", g)]) == 16, (g, len(stage_tok[("g", g)]))
            assert len(stage_tok[("u", g)]) == 16
            for i in range(16):
                pump(2)
                sl = (i // 2) % 2
                if i % 2 == 0:
                    FE("sync", "sAs")
                    W("sync", "sPE", gu_ld_free[sl])
                    ncp0 = (16 * g + i) * 128
                    t_gg = D("sync", "sLg%d" % sl, gu_sb[sl][:, 0:16 * 256].rearrange(
                        "p (c n) -> p c n", n=256), Gg_v[:, :, ncp0:ncp0 + 256])
                    t_gu = D("sync", "sLg%d" % sl,
                             gu_sb[sl][:, 16 * 256:32 * 256].rearrange(
                                 "p (c n) -> p c n", n=256),
                             Gu_v[:, :, ncp0:ncp0 + 256])
                    gu_tok = (t_gg, t_gu)
                io = i % 2
                for th in range(2):
                    FE("tensor", "sLg%d" % sl)
                    W("tensor", "sDV", pGU_free[th])
                    W("tensor", "sDV", pGU_free[2 + th])
                    for kc in range(16):
                        E("tensor", lambda e, sl=sl, kc=kc, io=io, th=th:
                          e.matmul(pA[th][:, 0:512],
                                   gu_sb[sl][:, kc * 256 + io * 128:
                                             kc * 256 + io * 128 + 128],
                                   xnT[:, kc * TC + th * 512:
                                       kc * TC + (th + 1) * 512],
                                   start=(kc == 0), stop=(kc == 15)))
                    t_g = T("tensor", "sPE")
                    for kc in range(16):
                        E("tensor", lambda e, sl=sl, kc=kc, io=io, th=th:
                          e.matmul(pA[2 + th][:, 0:512],
                                   gu_sb[sl][:, 16 * 256 + kc * 256 + io * 128:
                                             16 * 256 + kc * 256 + io * 128 + 128],
                                   xnT[:, kc * TC + th * 512:
                                       kc * TC + (th + 1) * 512],
                                   start=(kc == 0), stop=(kc == 15)))
                    t_u = T("tensor", "sPE")
                    W("scalar", "sPE", t_g)
                    W("scalar", "sDV", gelu_free[th])
                    E("scalar", lambda e, th=th: e.activation(
                        scr2[:, 1024 + th * 512: 1024 + (th + 1) * 512],
                        pA[th][:, 0:512], AF.Gelu_apprx_tanh))
                    t_gelu = T("scalar", "sAC")
                    W("vector", "sAC", t_gelu)
                    W("vector", "sPE", t_u)
                    E("vector", lambda e, i=i, th=th: e.tensor_tensor(
                        itr[:, i * 1024 + th * 512: i * 1024 + (th + 1) * 512],
                        scr2[:, 1024 + th * 512: 1024 + (th + 1) * 512],
                        pA[2 + th][:, 0:512], ALU.mult))
                    t_ch = T("vector", "sDV")
                    chain_tok[(i, th)] = t_ch
                    gelu_free[th] = t_ch
                    pGU_free[th] = t_ch
                    pGU_free[2 + th] = t_ch
                gu_ld_free[sl] = P.ctr["sPE"].v
            assert len(stage_tok[("d", g)]) == 16
            for q in range(4):
                pump(2)
                W("sync", "sPE", wd_free)
                FE("sync", "sAs")
                for i in range(16):
                    r0 = (16 * g + i) * 128
                    D("sync", "sLw", wbig[:, i * 512:(i + 1) * 512],
                      Gd[r0:r0 + 128, q * 512:(q + 1) * 512])
                for ts_ in range(8):
                    o4 = ts_ % 4
                    reg = pB[o4 // 2][:, (o4 % 2) * 512:(o4 % 2) * 512 + 512]
                    W("tensor", "sDV", fls_dve[o4])
                    if ts_ == 0:
                        FE("tensor", "sLw")
                    for i in range(16):
                        if q == 0 and ts_ == 0:
                            W("tensor", "sDV", chain_tok[(i, 0)])
                            W("tensor", "sDV", chain_tok[(i, 1)])
                        E("tensor", lambda e, reg=reg, i=i, ts_=ts_:
                          e.matmul(reg,
                                   itr[:, i * 1024 + ts_ * 128:
                                       i * 1024 + ts_ * 128 + 128],
                                   wbig[:, i * 512:(i + 1) * 512],
                                   start=(i == 0), stop=(i == 15)))
                    t_dn = T("tensor", "sPE")
                    W("vector", "sPE", t_dn)
                    o2 = ts_ % 2
                    FE("vector", "sGa%d" % o2)
                    E("vector", lambda e, o2=o2, reg=reg: e.tensor_copy(
                        fls[:, o2 * 512:(o2 + 1) * 512], reg))
                    t_fl = T("vector", "sDV")
                    fls_dve[o4] = t_fl
                    W("gpsimd", "sDV", t_fl)
                    kw = {} if g == 0 else {"accum_op": ALU.add}
                    D("gpsimd", "sGa%d" % o2,
                      acc_d[:, ts_ * 2048 + q * 512:
                            ts_ * 2048 + q * 512 + 512],
                      fls[:, o2 * 512:(o2 + 1) * 512], **kw)
                wd_free = P.ctr["sPE"].v

        W("sync", "sDV", P.ctr["sDV"].v)
        FE("sync", "sGa0")
        FE("sync", "sGa1")
        FE("sync", "sAs")
        cmb_free = 0
        last_store = 0
        for ts_ in range(8):
            W("sync", "sDV", cmb_free)
            FE("sync", "sLo")
            t_x = D("sync", "sLx", scr[:, 0:2048],
                    x_in[ts_ * 128:(ts_ + 1) * 128, :])
            t_s = D("sync", "sLx", scr[:, 2048:4096],
                    s_d[:, ts_ * 2048:(ts_ + 1) * 2048])
            atile = scr2[:, 0:2048]
            t_a = D("sync", "sLx", atile, acc_d[:, ts_ * 2048:(ts_ + 1) * 2048])
            FE("scalar", "sLx")
            W("scalar", "sDV", cmb_free)
            E("scalar", lambda e, a=atile: e.activation(a, a, AF.Silu))
            t_sl = T("scalar", "sAC")
            W("vector", "sAC", t_sl)
            FE("vector", "sLx")
            E("vector", lambda e, a=atile: e.tensor_tensor(
                scr[:, 2048:4096], scr[:, 2048:4096], a, ALU.mult))
            t_sg = T("vector", "sDV")
            W("vector", "sDV", t_sg)
            E("vector", lambda e: e.tensor_tensor(
                scr[:, 0:2048], scr[:, 0:2048], scr[:, 2048:4096], ALU.add))
            t_cb = T("vector", "sDV")
            cmb_free = t_cb
            W("sync", "sDV", t_cb)
            last_store = D("sync", "sLo", out_t[ts_ * 128:(ts_ + 1) * 128, :],
                           scr[:, 0:2048])

        dbg_reg = {"xnT_t": None, "qT_d": qT_d, "kT_d": kT_d, "kt_d": kt_d,
                   "v_d": v_d, "akt_d": akt_d, "av_d": av_d, "s_d": s_d,
                   "acc_d": acc_d, "Gg": Gg, "Gu": Gu, "Gd": Gd,
                   "gsh_d": gsh_d, "dsh_d": dsh_d}
        if dbg:
            FE("sync", "sAs")
            FE("sync", "sGa0")
            FE("sync", "sGa1")
            W("sync", "sDV", P.ctr["sDV"].v)
            for name in dbg:
                if name in ("xnT_e", "xnbf0"):
                    continue
                if name == "xnT_t":
                    o = nc.dram_tensor("dbg_xnT", [128, 16 * TC], BF16,
                                       kind="ExternalOutput").ap()
                    last_store = D("sync", "sLo", o, xnT[:, :])
                    continue
                src = dbg_reg[name]
                o = nc.dram_tensor("dbg_" + name, list(src.shape),
                                   src.dtype, kind="ExternalOutput").ap()
                last_store = D("sync", "sLo", o, src)
            FE("sync", "sLo")

        FE("sync", "sLo")
        FE("gpsimd", "sLo")

        assert dec_pos[0] == len(dec_units), (dec_pos[0], len(dec_units))

        @block.tensor
        def _(e):
            for f in P.ops["tensor"]:
                f(e)

        @block.vector
        def _(e):
            for f in P.ops["vector"]:
                f(e)

        @block.scalar
        def _(e):
            for f in P.ops["scalar"]:
                f(e)

        @block.gpsimd
        def _(e):
            for f in P.ops["gpsimd"]:
                f(e)

        @block.sync
        def _(e):
            for f in P.ops["sync"]:
                f(e)

    return nc


def _host_prepack(w_int32):
    b = np.ascontiguousarray(w_int32).view(np.uint8)
    b = b.reshape(w_int32.shape[0], w_int32.shape[1], 4)[:, :, 0]
    bT = np.ascontiguousarray(b.T)
    n = bT.shape[1]
    il = np.stack([np.arange(n // 2), np.arange(n // 2) + n // 2], 1).reshape(-1)
    bTi = np.ascontiguousarray(bT[:, il])
    return bTi.view(np.int16)


_CACHE = {}


def kernel(L0_attn_x, L0_attn_Wq, L0_attn_bq, L0_attn_Wk, L0_attn_bk,
           L0_attn_Wv, L0_attn_bv, L0_ffn_gate_W_packed, L0_ffn_gate_scales,
           L0_ffn_up_W_packed, L0_ffn_up_scales, L0_ffn_down_W_packed,
           L0_ffn_down_scales):
    x = np.ascontiguousarray(np.asarray(L0_attn_x, dtype=np.float32)).reshape(8192, H)
    wq = np.ascontiguousarray(np.asarray(L0_attn_Wq, dtype=np.float32))
    wk = np.ascontiguousarray(np.asarray(L0_attn_Wk, dtype=np.float32))
    wv = np.ascontiguousarray(np.asarray(L0_attn_Wv, dtype=np.float32))
    bq = np.ascontiguousarray(np.asarray(L0_attn_bq, dtype=np.float32)).reshape(1, H)
    bk = np.ascontiguousarray(np.asarray(L0_attn_bk, dtype=np.float32)).reshape(1, H)
    bv = np.ascontiguousarray(np.asarray(L0_attn_bv, dtype=np.float32)).reshape(1, H)
    gp = _host_prepack(np.ascontiguousarray(np.asarray(L0_ffn_gate_W_packed, dtype=np.int32)))
    up = _host_prepack(np.ascontiguousarray(np.asarray(L0_ffn_up_W_packed, dtype=np.int32)))
    dp = _host_prepack(np.ascontiguousarray(np.asarray(L0_ffn_down_W_packed, dtype=np.int32)))
    gs = np.ascontiguousarray(np.asarray(L0_ffn_gate_scales, dtype=np.float32).T)
    us = np.ascontiguousarray(np.asarray(L0_ffn_up_scales, dtype=np.float32).T)
    ds = np.ascontiguousarray(np.asarray(L0_ffn_down_scales, dtype=np.float32).T)

    z_x = np.zeros((TC, H), np.float32)
    z_b = np.zeros((1, H), np.float32)

    key = tuple(DBG)
    if key not in _CACHE:
        _CACHE[key] = build(dbg=key)
    nc = _CACHE[key]

    in_maps = []
    for r in range(NCORES):
        odd = r % 2 == 1
        in_maps.append({
            "x": x[r * TC:(r + 1) * TC],
            "xa": x[(r - 1) * TC: r * TC] if odd else z_x,
            "wq": wq, "wk": wk, "wv": wv,
            "bq": bq, "bk": bk, "bv": bv,
            "bak": bk if odd else z_b, "bav": bv if odd else z_b,
            "gatep": gp, "upp": up, "downp": dp,
            "gates": gs, "ups": us, "downs": ds,
        })

    res = run_bass_kernel_spmd(nc, in_maps, core_ids=list(range(NCORES)))
    _LAST.clear()
    _LAST.update({"results": res.results})
    out = np.concatenate([res.results[r]["out"] for r in range(NCORES)], axis=0)
    return out.reshape(4, 2048, H)


if __name__ == "__main__":
    nc = build()
    n = sum(1 for _ in nc.all_instructions())
    print("built ok; instructions:", n)


# version 08
# speedup vs baseline: 1.4699x; 1.4409x over previous
from contextlib import ExitStack

import numpy as np

import concourse.bass as bass
import concourse.mybir as mybir
from concourse.bass_utils import run_bass_kernel_spmd

F32 = mybir.dt.float32
BF16 = mybir.dt.bfloat16
I16 = mybir.dt.int16
AF = mybir.ActivationFunctionType
ALU = mybir.AluOpType
AX = mybir.AxisListType

NCORES = 8
H = 2048
TC = 1024
HEADS = 8
CHUNK = 256
NCH = 4
MLP = 16384
EPS = 1e-5

_BETAS = np.linspace(-5.0, -9.0, 8).astype(np.float32).astype(np.float64)
GAMMA = 1.0 - np.exp2(_BETAS)
LNG = np.log(GAMMA)
CD = GAMMA ** CHUNK


class Ctr:
    __slots__ = ("v",)

    def __init__(self):
        self.v = 0

    def inc(self, d):
        self.v += d
        return self.v


class Prog:
    ENGINES = ("tensor", "vector", "scalar", "gpsimd", "sync")

    def __init__(self):
        self.ops = {e: [] for e in self.ENGINES}
        self.sem = {}
        self.ctr = {}

    def add_sem(self, name, handle):
        self.sem[name] = handle
        self.ctr[name] = Ctr()

    def emit(self, eng, fn):
        self.ops[eng].append(fn)

    def wait(self, eng, sem_name, val):
        if val is None or val <= 0:
            return
        s = self.sem[sem_name]
        self.emit(eng, lambda e, s=s, v=val: e.wait_ge(s, v))
        self.emit(eng, lambda e: e.nop(nofuse=True))

    def fence(self, eng, sem_name):
        self.wait(eng, sem_name, self.ctr[sem_name].v)

    def dma(self, eng, sem_name, out, in_, **kw):
        s = self.sem[sem_name]
        tok = self.ctr[sem_name].inc(16)
        self.emit(eng, lambda e, s=s, o=out, i=in_, kw=kw:
                  e.dma_start(out=o, in_=i, **kw).then_inc(s, 16))
        return tok

    def tag(self, eng, sem_name):
        s = self.sem[sem_name]
        tok = self.ctr[sem_name].inc(1)
        prev = self.ops[eng].pop()
        self.emit(eng, lambda e, p=prev, s=s: p(e).then_inc(s, 1))
        return tok


DBG = []
_LAST = {}


def build(dbg=()):
    nc = bass.Bass("TRN2", target_bir_lowering=False)

    def din(name, shape, dt):
        return nc.dram_tensor(name, list(shape), dt, kind="ExternalInput").ap()

    x_in = din("x", (TC, H), F32)
    xa_in = din("xa", (TC, H), F32)
    wq_in = din("wq", (H, H), F32)
    wk_in = din("wk", (H, H), F32)
    wv_in = din("wv", (H, H), F32)
    b_ins = {"q": din("bq", (1, H), F32), "k": din("bk", (1, H), F32),
             "v": din("bv", (1, H), F32), "ak": din("bak", (1, H), F32),
             "av": din("bav", (1, H), F32)}
    gp_in = din("gatep", (1024, 8192), I16)
    up_in = din("upp", (1024, 8192), I16)
    dp_in = din("downp", (8192, 1024), I16)
    gs_in = din("gates", (64, MLP), F32)
    us_in = din("ups", (64, MLP), F32)
    ds_in = din("downs", (512, H), F32)

    out_t = nc.dram_tensor("out", [TC, H], F32, kind="ExternalOutput").ap()

    def dint(name, shape, dt):
        return nc.dram_tensor(name, list(shape), dt).ap()

    qT_d = dint("qT_d", (128, 16 * TC), BF16)
    kT_d = dint("kT_d", (128, 16 * TC), BF16)
    kt_d = dint("kt_d", (128, 8 * H), BF16)
    v_d = dint("v_d", (128, 8 * H), BF16)
    akt_d = dint("akt_d", (128, 8 * H), BF16)
    av_d = dint("av_d", (128, 8 * H), BF16)
    s_d = dint("s_d", (128, 8 * H), F32)
    acc_d = dint("acc_d", (128, 8 * H), F32)
    Gg = dint("Gg", (H, MLP), BF16)
    Gu = dint("Gu", (H, MLP), BF16)
    Gd = dint("Gd", (MLP, H), BF16)
    gsh_d = dint("gsh_d", (64, MLP), BF16)
    ush_d = dint("ush_d", (64, MLP), BF16)
    dsh_d = dint("dsh_d", (512, H), BF16)

    with ExitStack() as ctx:
        sbt = lambda n, s, d: ctx.enter_context(nc.sbuf_tensor(n, list(s), d))
        pst = lambda n, s, d: ctx.enter_context(nc.psum_tensor(n, list(s), d))

        xnT = sbt("xnT", (128, 16 * TC), BF16)
        wbig = sbt("wbig", (128, 8 * 1024), BF16)
        scr = sbt("scr", (128, 4096), F32)
        scr2 = sbt("scr2", (128, 2048), F32)
        pk_sb = sbt("pk_sb", (128, 2 * 1024), I16)
        dreg = [sbt(f"dreg{i}", (128, 1024), I16) for i in range(8)]
        slab = sbt("slab_sb", (128, 4 * 1024), BF16)
        srep = sbt("srep_sb", (128, 2 * 1024), BF16)
        qe = sbt("qe", (128, 2 * 1024), BF16)
        Ssb = sbt("Ssb", (128, 16 * 256), F32)
        Sbf = sbt("Sbf", (128, 2 * 256), BF16)
        ksb = sbt("ksb", (128, 2 * 256), BF16)
        attnM = sbt("attnM", (128, 2 * 256), BF16)
        qs_sb = sbt("qs_sb", (128, 2 * 256), BF16)
        gu_sb = [sbt(f"gu{i}", (128, 16 * 512), BF16) for i in range(2)]
        itr = sbt("itr", (128, 16 * 1024), BF16)
        xanT = itr
        cons = sbt("cons", (128, 8), F32)
        csb = sbt("csb", (128, 8 * 256), BF16)
        kvc = sbt("kvc", (128, 16), F32)
        dmsk = sbt("dmsk", (128, 2 * 256), F32)
        ident = sbt("ident", (128, 128), BF16)
        onesb = sbt("onesb", (128, 512), BF16)
        brow = sbt("brow", (128, H), BF16)
        iop = sbt("iop", (128, 2), F32)
        lst = sbt("lst", (128, 16), F32)
        fls = sbt("fls", (128, 2 * 512), F32)

        dtt = scr2[:, 0:512]
        ioc = scr2[:, 512:768]

        pA = [pst(f"pA{i}", (128, 512), F32) for i in range(4)]
        pB = [pst(f"pB{i}", (128, 1024), F32) for i in range(2)]
        p_upd = pB[0][:, 0:512]

        P = Prog()
        for s in ("sLx", "sLp0", "sLp1", "sGs0", "sGs1", "sLg0", "sLg1",
                  "sLw", "sGw0", "sGw1", "sGb", "sAq0", "sAq1", "sAs",
                  "sGa0", "sGa1", "sLo",
                  "sPE", "sDV", "sAC", "sGP"):
            P.add_sem(s, ctx.enter_context(nc.semaphore(s)))

        block = ctx.enter_context(nc.Block())
        E, W, D, T, FE = P.emit, P.wait, P.dma, P.tag, P.fence

        mask_free = [0]
        s_spill_tok = [0]
        attn_free = [0]
        ks_free = [0]
        upd_free = [0]
        gelu_free = [0, 0]
        fls_dve = [0, 0, 0, 0]
        fls_gd = [0, 0]

        for col, val in {0: -4.0, 1: -16.0, 2: 1.0, 3: EPS}.items():
            E("gpsimd", lambda e, c=col, v=val: e.memset(cons[:, c:c + 1], v))
        E("gpsimd", lambda e: e.memset(onesb[0:1, :], 1.0))
        E("gpsimd", lambda e: e.memset(Ssb[:, :], 0.0))
        E("gpsimd", lambda e: e.iota(ioc, pattern=[[1, 256]], base=0,
                                     channel_multiplier=0,
                                     allow_small_or_imprecise_dtypes=True))
        E("gpsimd", lambda e: e.iota(iop[:, 0:1], pattern=[[0, 1]], base=0,
                                     channel_multiplier=1,
                                     allow_small_or_imprecise_dtypes=True))
        for jc in range(2):
            E("gpsimd", lambda e, jc=jc: e.iota(
                dmsk[:, jc * 256:(jc + 1) * 256], pattern=[[1, 256]],
                base=-128 * jc, channel_multiplier=-1,
                allow_small_or_imprecise_dtypes=True))
            t_io = T("gpsimd", "sGP")
            W("gpsimd", "sGP", t_io)
            E("gpsimd", lambda e, jc=jc: e.affine_select(
                dmsk[:, jc * 256:(jc + 1) * 256], dmsk[:, jc * 256:(jc + 1) * 256],
                pattern=[[1, 256]], compare_op=ALU.is_ge, fill=1.0e9,
                base=-128 * jc, channel_multiplier=-1))
        E("gpsimd", lambda e: e.iota(scr2[:, 768:896], pattern=[[1, 128]], base=0,
                                     channel_multiplier=-1,
                                     allow_small_or_imprecise_dtypes=True))
        tok_gp0 = T("gpsimd", "sGP")

        W("vector", "sGP", tok_gp0)
        E("vector", lambda e: e.tensor_scalar(scr2[:, 896:1024], scr2[:, 768:896],
                                              0.0, None, ALU.is_equal))
        t_ieq = T("vector", "sDV")
        W("vector", "sDV", t_ieq)
        E("vector", lambda e: e.tensor_copy(ident[:, :], scr2[:, 896:1024]))
        tok_ident = T("vector", "sDV")

        W("scalar", "sGP", tok_gp0)
        for h in range(HEADS):
            E("scalar", lambda e, h=h: e.activation(
                csb[:, h * 256:(h + 1) * 256], ioc, AF.Exp, scale=float(LNG[h])))
        for tsl in range(2):
            for h in range(HEADS):
                col = tsl * 8 + h
                E("scalar", lambda e, h=h, col=col: e.activation(
                    kvc[:, col:col + 1], iop[:, 0:1], AF.Exp, scale=-float(LNG[h])))
        tok_exp = T("scalar", "sAC")
        W("vector", "sAC", tok_exp)
        for h in range(HEADS):
            E("vector", lambda e, h=h: e.tensor_scalar(
                csb[:, h * 256:(h + 1) * 256], csb[:, h * 256:(h + 1) * 256],
                float(GAMMA[h]), None, ALU.mult))
            E("vector", lambda e: e.nop())
        for tsl in range(2):
            for h in range(HEADS):
                col = tsl * 8 + h
                mv = float(GAMMA[h] ** (255 - 128 * tsl))
                E("vector", lambda e: e.nop())
                E("vector", lambda e, col=col, mv=mv: e.tensor_scalar(
                    kvc[:, col:col + 1], kvc[:, col:col + 1], mv, None, ALU.mult))
        T("vector", "sDV")

        last_sc_st = 0
        sc_chunks = []
        for src, dst, rows, cols in ((gs_in, gsh_d, 64, MLP),
                                     (us_in, ush_d, 64, MLP),
                                     (ds_in, dsh_d, 512, H)):
            for r0 in range(0, rows, 64):
                rr = min(64, rows - r0)
                for c0 in range(0, cols, 4096):
                    cc = min(4096, cols - c0)
                    sc_chunks.append((src, dst, r0, rr, c0, cc))
        sc_st_tok = [0, 0]
        sc_ld_tok = [0, 0]
        for ci, (src, dst, r0, rr, c0, cc) in enumerate(sc_chunks):
            pp = ci % 2
            rb = pp * 64
            W("sync", "sDV", sc_st_tok[pp])
            t_ld = D("sync", "sLx", scr[rb:rb + rr, 0:cc],
                     src[r0:r0 + rr, c0:c0 + cc])
            W("vector", "sLx", t_ld)
            W("vector", "sAs", sc_ld_tok[pp])
            obuf = (qe[rb:rb + rr, 0:cc] if cc <= 2048 else
                    wbig[0:rr, pp * 4096: pp * 4096 + cc])
            E("vector", lambda e, rr=rr, cc=cc, rb=rb, obuf=obuf: e.tensor_scalar(
                obuf, scr[rb:rb + rr, 0:cc], 0.5, None, ALU.mult))
            t_h = T("vector", "sDV")
            sc_st_tok[pp] = t_h
            E("vector", lambda e: e.nop())
            W("scalar", "sDV", t_h)
            last_sc_st = D("scalar", "sAs", dst[r0:r0 + rr, c0:c0 + cc], obuf)
            sc_ld_tok[pp] = last_sc_st
        tok_scales_st = last_sc_st
        tok_scales_dve = P.ctr["sDV"].v

        def load_bias(name, wait_pe=None):
            if wait_pe:
                W("gpsimd", "sPE", wait_pe)
            return D("gpsimd", "sGb", brow[0:1, 0:H], b_ins[name])

        gsh_aps = {"g": gsh_d, "u": ush_d}
        pk_aps = {"g": gp_in, "u": up_in}
        G_aps = {"g": Gg, "u": Gu}
        stage_tok = {(m, g): [] for m in ("g", "u", "d") for g in range(8)}
        ds_pk_free = [0, 0]
        ds_srep_free = [0, 0]
        slab_st_tok = [0, 0]
        ds_slot = [0]
        FD = 1024

        def emit_chain_pair(planes, pk_ap):
            for j, (sh, _, _) in enumerate(planes):
                r = dreg[4 * j]
                if sh:
                    E("vector", lambda e, r=r, sh=sh: e.tensor_scalar(
                        r[:, :], pk_ap, sh, 15,
                        ALU.logical_shift_right, ALU.bitwise_and))
                else:
                    E("vector", lambda e, r=r: e.tensor_scalar(
                        r[:, :], pk_ap, 15, None, ALU.bitwise_and))
            for j in range(len(planes)):
                c0, m = dreg[4 * j], dreg[4 * j + 1]
                E("vector", lambda e, c0=c0, m=m: e.tensor_scalar(
                    m[:, :], c0[:, :], 7, None, ALU.bitwise_and))
            tok_m = T("vector", "sDV")
            W("scalar", "sDV", tok_m)
            for j in range(len(planes)):
                m, a, b = dreg[4 * j + 1], dreg[4 * j + 2], dreg[4 * j + 3]
                E("scalar", lambda e, m=m, a=a: e.activation(
                    a[:, :], m[:, :], AF.Identity, scale=2.0, bias=cons[:, 0:1]))
                E("scalar", lambda e, m=m, b=b: e.activation(
                    b[:, :], m[:, :], AF.Identity, scale=4.0, bias=cons[:, 1:2]))
            tok_ab = T("scalar", "sAC")
            W("vector", "sAC", tok_ab)
            for j in range(len(planes)):
                a, b = dreg[4 * j + 2], dreg[4 * j + 3]
                E("vector", lambda e, a=a, b=b: e.tensor_tensor(
                    a[:, :], a[:, :], b[:, :], ALU.max))
            for j in range(len(planes)):
                m, a = dreg[4 * j + 1], dreg[4 * j + 2]
                E("vector", lambda e, m=m, a=a: e.tensor_tensor(
                    a[:, :], m[:, :], a[:, :], ALU.max))
            for j in range(len(planes)):
                c0, b = dreg[4 * j], dreg[4 * j + 3]
                E("vector", lambda e, c0=c0, b=b: e.tensor_scalar(
                    b[:, :], c0[:, :], 3, 1,
                    ALU.logical_shift_right, ALU.bitwise_and))
            for j in range(len(planes)):
                b = dreg[4 * j + 3]
                E("vector", lambda e, b=b: e.tensor_scalar(
                    b[:, :], b[:, :], -2, 1, ALU.mult, ALU.add))
            for j in range(len(planes)):
                a, b = dreg[4 * j + 2], dreg[4 * j + 3]
                E("vector", lambda e, a=a, b=b: e.tensor_tensor(
                    a[:, :], a[:, :], b[:, :], ALU.mult))
            for j, (_, slab_sl, srep_sl) in enumerate(planes):
                a = dreg[4 * j + 2]
                E("vector", lambda e, a=a, o=slab_sl, sr=srep_sl:
                  e.tensor_tensor(o, a[:, :], sr, ALU.mult))
            return T("vector", "sDV")

        def emit_unit_gu(mat, g, c, ch):
            b = g // 4
            frq = 2048 * (g % 4) + 1024 * ch
            n0 = 2048 * g + 1024 * ch
            slot = ds_slot[0]
            ds_slot[0] ^= 1
            W("sync", "sDV", ds_pk_free[slot])
            pk_sl = pk_sb[:, slot * 1024: slot * 1024 + 1024]
            t_pk = D("sync", "sLp%d" % slot, pk_sl,
                     pk_aps[mat][c * 128:(c + 1) * 128, frq:frq + 1024])
            if ds_srep_free[slot] == 0:
                W("gpsimd", "sAs", tok_scales_st)
            W("gpsimd", "sDV", ds_srep_free[slot])
            sr_sl = srep[:, slot * 1024: slot * 1024 + 1024]
            t_sr = D("gpsimd", "sGs%d" % slot, sr_sl,
                     gsh_aps[mat][8 * c:8 * c + 8, n0:n0 + 1024]
                     .unsqueeze(1).broadcast_to([8, 16, 1024]))
            FE("vector", "sLp%d" % slot)
            FE("vector", "sGs%d" % slot)
            W("vector", "sAs", slab_st_tok[slot])
            sb0 = slab[:, slot * 2048: slot * 2048 + 1024]
            sb1 = slab[:, slot * 2048 + 1024: slot * 2048 + 2048]
            planes = [(8 * b, sb0, sr_sl), (8 * b + 4, sb1, sr_sl)]
            tok_w = emit_chain_pair(planes, pk_sl)
            ds_pk_free[slot] = tok_w
            ds_srep_free[slot] = tok_w
            W("scalar", "sDV", tok_w)
            Gm = G_aps[mat]
            D("scalar", "sAs", Gm[256 * c: 256 * c + 256: 2, n0:n0 + 1024], sb0)
            t_st = D("scalar", "sAs",
                     Gm[256 * c + 1: 256 * c + 256: 2, n0:n0 + 1024], sb1)
            slab_st_tok[slot] = t_st
            stage_tok[(mat, g)].append(t_st)

        def emit_unit_down(g, cr, pairi):
            cp = 8 * g + cr
            slot = ds_slot[0]
            ds_slot[0] ^= 1
            W("sync", "sDV", ds_pk_free[slot])
            pk_sl = pk_sb[:, slot * 1024: slot * 1024 + 1024]
            t_pk = D("sync", "sLp%d" % slot, pk_sl,
                     dp_in[cp * 128:(cp + 1) * 128, 0:1024])
            if ds_srep_free[slot] == 0:
                W("gpsimd", "sAs", tok_scales_st)
            W("gpsimd", "sDV", ds_srep_free[slot])
            sr_sl = srep[:, slot * 1024: slot * 1024 + 1024]
            t_sr = D("gpsimd", "sGs%d" % slot, sr_sl,
                     dsh_d[8 * cp:8 * cp + 8, pairi * 1024:(pairi + 1) * 1024]
                     .unsqueeze(1).broadcast_to([8, 16, 1024]))
            FE("vector", "sLp%d" % slot)
            FE("vector", "sGs%d" % slot)
            W("vector", "sAs", slab_st_tok[slot])
            sb0 = slab[:, slot * 2048: slot * 2048 + 1024]
            sb1 = slab[:, slot * 2048 + 1024: slot * 2048 + 2048]
            planes = [(8 * pairi, sb0, sr_sl), (8 * pairi + 4, sb1, sr_sl)]
            tok_w = emit_chain_pair(planes, pk_sl)
            ds_pk_free[slot] = tok_w
            ds_srep_free[slot] = tok_w
            W("scalar", "sDV", tok_w)
            t_last = 0
            for nib, sl in ((0, sb0), (1, sb1)):
                t_last = D("scalar", "sAs",
                           Gd[256 * cp + nib: 256 * cp + 256: 2,
                              pairi * 1024: (pairi + 1) * 1024], sl)
            slab_st_tok[slot] = t_last
            stage_tok[("d", g)].append(t_last)

        dec_units = []
        for g in range(8):
            for c in range(8):
                for ch in range(2):
                    dec_units.append(("g", g, c, ch))
                    dec_units.append(("u", g, c, ch))
            for cr in range(8):
                for pairi in range(2):
                    dec_units.append(("d", g, cr, pairi))
        dec_pos = [0]
        dec_on = [False]

        def pump(n):
            if not dec_on[0]:
                return
            for _ in range(n):
                if dec_pos[0] >= len(dec_units):
                    return
                kind, g, c, x2 = dec_units[dec_pos[0]]
                dec_pos[0] += 1
                if kind == "d":
                    emit_unit_down(g, c, x2)
                else:
                    emit_unit_gu(kind, g, c, x2)

        x_sl = scr[:, 0:2048]
        sq_sl = scr[:, 2048:4096]
        xnbf = wbig[:, 0:2048]
        slot_free = tok_scales_dve
        xnbf_pe_free = 0
        dec_on[0] = True
        FE("vector", "sAs")
        W("scalar", "sDV", tok_scales_dve)
        FE("scalar", "sAs")
        tr_evac_tok = {0: 0, 1: 0}
        tpsum = [pB[0][:, 0:64].bitcast(BF16), pB[1][:, 0:64].bitcast(BF16)]

        for kind, tt in [("m", t) for t in range(8)] + [("a", t) for t in range(8)]:
            src = x_in if kind == "m" else xa_in
            W("sync", "sDV", slot_free)
            t_ld = D("sync", "sLx", x_sl, src[tt * 128:(tt + 1) * 128, :])
            FE("vector", "sLx")
            FE("scalar", "sLx")
            E("vector", lambda e: e.tensor_reduce(lst[:, 0:1], x_sl, AX.X, ALU.add))
            t_s0 = T("vector", "sDV")
            E("scalar", lambda e: e.activation(sq_sl, x_sl, AF.Square))
            t_sq = T("scalar", "sAC")
            W("vector", "sAC", t_sq)
            E("vector", lambda e: e.tensor_reduce(lst[:, 1:2], sq_sl, AX.X, ALU.add))
            W("vector", "sDV", t_s0)
            E("vector", lambda e: e.tensor_scalar(lst[:, 2:3], lst[:, 0:1],
                                                  1.0 / H, None, ALU.mult))
            t_ms = T("vector", "sDV")
            W("scalar", "sDV", t_ms)
            E("scalar", lambda e: e.activation(lst[:, 3:4], lst[:, 1:2],
                                               AF.Identity, scale=1.0 / H))
            E("scalar", lambda e: e.activation(lst[:, 4:5], lst[:, 2:3],
                                               AF.Square))
            t_m2 = T("scalar", "sAC")
            W("vector", "sAC", t_m2)
            E("vector", lambda e: e.tensor_tensor(lst[:, 5:6], lst[:, 3:4],
                                                  lst[:, 4:5], ALU.subtract))
            t_var = T("vector", "sDV")
            W("scalar", "sDV", t_var)
            E("scalar", lambda e: e.activation(lst[:, 6:7], lst[:, 5:6],
                                               AF.Sqrt, bias=cons[:, 3:4]))
            t_std = T("scalar", "sAC")
            W("vector", "sAC", t_std)
            E("vector", lambda e: e.reciprocal(lst[:, 7:8], lst[:, 6:7]))
            t_rc = T("vector", "sDV")
            W("vector", "sDV", t_rc)
            W("vector", "sPE", xnbf_pe_free)
            E("vector", lambda e: e.tensor_scalar(xnbf, x_sl, lst[:, 2:3],
                                                  lst[:, 7:8], ALU.subtract,
                                                  ALU.mult))
            t_xn = T("vector", "sDV")
            slot_free = t_xn
            if kind == "m" and tt == 0 and "xnbf0" in dbg:
                dbg_x0 = nc.dram_tensor("dbg_x0", [128, 2048], F32,
                                        kind="ExternalOutput").ap()
                dbg_xb0 = nc.dram_tensor("dbg_xnbf0", [128, 2048], BF16,
                                         kind="ExternalOutput").ap()
                dbg_lst0 = nc.dram_tensor("dbg_lst0", [128, 16], F32,
                                          kind="ExternalOutput").ap()
                W("sync", "sDV", t_xn)
                tt_x0 = D("sync", "sLo", dbg_x0, x_sl)
                tt_xb = D("sync", "sLo", dbg_xb0, xnbf)
                tt_ls = D("sync", "sLo", dbg_lst0, lst[:, 0:16])
                FE("vector", "sLo")
                FE("tensor", "sLo")
            dstT = xnT if kind == "m" else xanT
            W("tensor", "sDV", t_xn)
            W("tensor", "sDV", tok_ident)
            for kc in range(16):
                pp = kc % 2
                W("tensor", "sDV", tr_evac_tok[pp])
                E("tensor", lambda e, kc=kc, pp=pp: e.transpose(
                    tpsum[pp], xnbf[:, kc * 128:(kc + 1) * 128], ident[:, :]))
                t_tr = T("tensor", "sPE")
                W("vector", "sPE", t_tr)
                E("vector", lambda e, dstT=dstT, kc=kc, tt=tt, pp=pp:
                  e.tensor_copy(
                      dstT[:, kc * TC + tt * 128: kc * TC + (tt + 1) * 128],
                      tpsum[pp]))
                tr_evac_tok[pp] = T("vector", "sDV")
            xnbf_pe_free = P.ctr["sPE"].v
            pump(2)
        xnT_ready = max(tr_evac_tok[0], tr_evac_tok[1])

        if "xnT_e" in dbg:
            dbg_xe = nc.dram_tensor("dbg_xnT_e", [128, 16 * TC], BF16,
                                    kind="ExternalOutput").ap()
            W("sync", "sDV", xnT_ready)
            D("sync", "sLo", dbg_xe, xnT[:, :])
        FE("gpsimd", "sAs")
        dec_on[0] = True

        pB_free = [xnT_ready, xnT_ready]
        bias_tok = [0]
        FE("vector", "sAs")

        def emit_fm(w_ap, bias_slot, dst_d):
            wslot_free = [P.ctr["sPE"].v, P.ctr["sPE"].v]
            for ocg in range(8):
                ws = ocg % 2
                W("gpsimd", "sPE", wslot_free[ws])
                t_w = D("gpsimd", "sGw%d" % ws, gu_sb[ws][:, 0:16 * 256].rearrange(
                    "p (c n) -> p c n", n=256),
                    w_ap.rearrange("(c p) n -> p c n", p=128)
                    [:, :, ocg * 256:(ocg + 1) * 256])
                pump(1)
                for oc2 in range(2):
                    oc = 2 * ocg + oc2
                    FE("tensor", "sGw%d" % ws)
                    FE("tensor", "sGb")
                    W("tensor", "sDV", pB_free[oc2])
                    for tb in range(2):
                        for kc in range(16):
                            E("tensor", lambda e, oc2=oc2, tb=tb, kc=kc, ws=ws:
                              e.matmul(pB[oc2][:, tb * 512:(tb + 1) * 512],
                                       gu_sb[ws][:, kc * 256 + oc2 * 128:
                                                 kc * 256 + oc2 * 128 + 128],
                                       xnT[:, kc * TC + tb * 512:
                                           kc * TC + (tb + 1) * 512],
                                       start=(kc == 0), stop=False))
                        E("tensor", lambda e, oc2=oc2, tb=tb, oc=oc, bs=bias_slot:
                          e.matmul(pB[oc2][:, tb * 512:(tb + 1) * 512],
                                   brow[0:1, oc * 128: oc * 128 + 128],
                                   onesb[0:1, 0:512], start=False, stop=True))
                    t_pe = T("tensor", "sPE")
                    W("vector", "sPE", t_pe)
                    FE("vector", "sAq%d" % oc2)
                    E("vector", lambda e, oc2=oc2: e.tensor_copy(
                        qe[:, oc2 * 1024:(oc2 + 1) * 1024], pB[oc2][:, :]))
                    t_ev = T("vector", "sDV")
                    pB_free[oc2] = t_ev
                    W("scalar", "sDV", t_ev)
                    D("scalar", "sAq%d" % oc2,
                      dst_d[:, oc * 1024:(oc + 1) * 1024],
                      qe[:, oc2 * 1024:(oc2 + 1) * 1024])
                    wslot_free[ws] = t_pe

        bias_tok[0] = load_bias("q")
        emit_fm(wq_in, 0, qT_d)
        bias_tok[0] = load_bias("k", wait_pe=P.ctr["sPE"].v)
        emit_fm(wk_in, 0, kT_d)

        tm_wfree = [P.ctr["sPE"].v] * 16

        def emit_tm(w_ap, bias_main, bias_alien, main_dst, alien_dst):
            for dq in range(4):
                w_toks = []
                for kc in range(16):
                    W("gpsimd", "sPE", tm_wfree[kc])
                    w_toks.append(D("gpsimd", "sGw0",
                                    wbig[:, kc * 512:(kc + 1) * 512],
                                    w_ap.rearrange("(c p) n -> p c n", p=128)
                                    [:, kc, dq * 512:(dq + 1) * 512]))
                    if kc % 4 == 3:
                        pump(1)
                for srcT, bname, dst_d in ((xnT, bias_main, main_dst),
                                           (xanT, bias_alien, alien_dst)):
                    bias_tok[0] = load_bias(bname, wait_pe=P.ctr["sPE"].v)
                    for tsub in range(8):
                        o2 = tsub % 2
                        reg = pB[o2][:, 0:512]
                        W("tensor", "sDV", pB_free[o2])
                        FE("tensor", "sGb")
                        FE("tensor", "sGw0")
                        for kc in range(16):
                            E("tensor", lambda e, reg=reg, kc=kc, tsub=tsub,
                              srcT=srcT:
                              e.matmul(reg,
                                       srcT[:, kc * TC + tsub * 128:
                                            kc * TC + tsub * 128 + 128],
                                       wbig[:, kc * 512:(kc + 1) * 512],
                                       start=(kc == 0), stop=False))
                        E("tensor", lambda e, reg=reg, dq=dq:
                          e.matmul(reg, onesb[0:1, 0:128],
                                   brow[0:1, dq * 512:(dq + 1) * 512],
                                   start=False, stop=True))
                        t_pe = T("tensor", "sPE")
                        W("vector", "sPE", t_pe)
                        FE("vector", "sAq%d" % o2)
                        E("vector", lambda e, o2=o2, reg=reg: e.tensor_copy(
                            qe[:, o2 * 1024: o2 * 1024 + 512], reg))
                        t_ev = T("vector", "sDV")
                        pB_free[o2] = t_ev
                        W("scalar", "sDV", t_ev)
                        D("scalar", "sAq%d" % o2,
                          dst_d[:, tsub * 2048 + dq * 512:
                                tsub * 2048 + (dq + 1) * 512],
                          qe[:, o2 * 1024: o2 * 1024 + 512])
                    pump(2)
                for kc in range(16):
                    tm_wfree[kc] = P.ctr["sPE"].v

        emit_tm(wk_in, "k", "ak", kt_d, akt_d)
        emit_tm(wv_in, "v", "av", v_d, av_d)

        qT_v = qT_d.rearrange("p (c t) -> p c t", t=TC)
        kT_v = kT_d.rearrange("p (c t) -> p c t", t=TC)

        pA_free = [pB_free[0]] * 4
        upd_free[0] = pB_free[0]
        scan_pe_free = P.ctr["sPE"].v
        scan_dve_free = P.ctr["sDV"].v
        last_upd = P.ctr["sDV"].v

        for phase, src_kt, src_v in (("a", akt_d, av_d), ("m", kt_d, v_d)):
            for c in range(NCH):
                FE("sync", "sAq0")
                FE("sync", "sAq1")
                W("sync", "sPE", scan_pe_free)
                W("sync", "sDV", last_upd)
                t_kt = D("sync", "sLx", wbig[:, 0:4096],
                         src_kt[:, (2 * c) * 2048:(2 * c + 2) * 2048])
                t_v = D("sync", "sLx", wbig[:, 4096:8192],
                        src_v[:, (2 * c) * 2048:(2 * c + 2) * 2048])
                t_q = t_k2 = None
                if phase == "m":
                    t_q = D("sync", "sLx", gu_sb[0][:, 0:4096].rearrange(
                        "p (k t) -> p k t", t=256),
                        qT_v[:, :, c * 256:(c + 1) * 256])
                    t_k2 = D("sync", "sLx", gu_sb[0][:, 4096:8192].rearrange(
                        "p (k t) -> p k t", t=256),
                        kT_v[:, :, c * 256:(c + 1) * 256])
                pump(1)
                for h in range(HEADS):
                    hp = h % 2
                    cdh = float(CD[h])
                    if phase == "m":
                        W("vector", "sDV", last_upd)
                        E("vector", lambda e, h=h: e.tensor_copy(
                            Sbf[:, :], Ssb[:, h * 512:(h + 1) * 512]))
                        t_sbf = T("vector", "sDV")
                        W("scalar", "sDV", mask_free[0])
                        for jc in range(2):
                            E("scalar", lambda e, h=h, jc=jc: e.activation(
                                dtt[:, jc * 256:(jc + 1) * 256],
                                dmsk[:, jc * 256:(jc + 1) * 256], AF.Exp,
                                scale=float(LNG[h])))
                        t_dt = T("scalar", "sAC")
                        FE("tensor", "sLx")
                        W("tensor", "sDV", pA_free[hp])
                        for jc in range(2):
                            for dc in range(2):
                                E("tensor", lambda e, h=h, jc=jc, dc=dc, hp=hp:
                                  e.matmul(pA[hp][:, jc * 256:(jc + 1) * 256],
                                           gu_sb[0][:, 4096 + (2 * h + dc) * 256 +
                                                 jc * 128:
                                                 4096 + (2 * h + dc) * 256 +
                                                 jc * 128 + 128],
                                           gu_sb[0][:, (2 * h + dc) * 256:
                                                 (2 * h + dc) * 256 + 256],
                                           start=(dc == 0), stop=(dc == 1)))
                        t_at = T("tensor", "sPE")
                        W("vector", "sPE", t_at)
                        W("vector", "sPE", attn_free[0])
                        W("vector", "sAC", t_dt)
                        for jc in range(2):
                            E("vector", lambda e, jc=jc, hp=hp: e.tensor_tensor(
                                attnM[:, jc * 256:(jc + 1) * 256],
                                pA[hp][:, jc * 256:(jc + 1) * 256],
                                dtt[:, jc * 256:(jc + 1) * 256], ALU.mult))
                            E("vector", lambda e, h=h, jc=jc: e.tensor_tensor(
                                qs_sb[:, jc * 256:(jc + 1) * 256],
                                gu_sb[0][:, (2 * h + jc) * 256:
                                      (2 * h + jc) * 256 + 256],
                                csb[:, h * 256:(h + 1) * 256], ALU.mult))
                        t_mq = T("vector", "sDV")
                        pA_free[hp] = t_mq
                        mask_free[0] = t_mq
                        W("tensor", "sDV", t_mq)
                        W("tensor", "sDV", t_sbf)
                        W("tensor", "sDV", pA_free[2 + hp])
                        for ih in range(2):
                            for jc in range(2):
                                E("tensor", lambda e, ih=ih, jc=jc, h=h, hp=hp:
                                  e.matmul(pA[2 + hp][:, ih * 256:(ih + 1) * 256],
                                           attnM[:, jc * 256 + ih * 128:
                                                 jc * 256 + ih * 128 + 128],
                                           wbig[:, 4096 + jc * 2048 + h * 256:
                                                4096 + jc * 2048 + h * 256 + 256],
                                           start=(jc == 0), stop=False))
                            for dc in range(2):
                                E("tensor", lambda e, ih=ih, dc=dc, hp=hp:
                                  e.matmul(pA[2 + hp][:, ih * 256:(ih + 1) * 256],
                                           qs_sb[:, dc * 256 + ih * 128:
                                                 dc * 256 + ih * 128 + 128],
                                           Sbf[:, dc * 256:(dc + 1) * 256],
                                           start=False, stop=(dc == 1)))
                        t_out = T("tensor", "sPE")
                        attn_free[0] = t_out
                        W("vector", "sPE", t_out)
                        W("vector", "sAs", s_spill_tok[0])
                        for ih in range(2):
                            E("vector", lambda e, ih=ih, h=h, hp=hp:
                              e.tensor_copy(
                                  scr[:, ih * 2048 + h * 256:
                                      ih * 2048 + h * 256 + 256],
                                  pA[2 + hp][:, ih * 256:(ih + 1) * 256]))
                        t_sev = T("vector", "sDV")
                        pA_free[2 + hp] = t_sev
                    FE("vector", "sLx")
                    W("vector", "sPE", ks_free[0])
                    for tsl in range(2):
                        E("vector", lambda e: e.nop())
                        E("vector", lambda e, tsl=tsl, h=h: e.tensor_scalar(
                            ksb[:, tsl * 256:(tsl + 1) * 256],
                            wbig[:, tsl * 2048 + h * 256:
                                 tsl * 2048 + h * 256 + 256],
                            kvc[:, (tsl * 8 + h):(tsl * 8 + h) + 1], None,
                            ALU.mult))
                    t_ks = T("vector", "sDV")
                    W("tensor", "sDV", t_ks)
                    W("tensor", "sDV", upd_free[0])
                    FE("tensor", "sLx")
                    for dh in range(2):
                        for tsl in range(2):
                            E("tensor", lambda e, dh=dh, tsl=tsl, h=h:
                              e.matmul(p_upd[:, dh * 256:(dh + 1) * 256],
                                       ksb[:, tsl * 256 + dh * 128:
                                           tsl * 256 + dh * 128 + 128],
                                       wbig[:, 4096 + tsl * 2048 + h * 256:
                                            4096 + tsl * 2048 + h * 256 + 256],
                                       start=(tsl == 0), stop=(tsl == 1)))
                    t_upd = T("tensor", "sPE")
                    ks_free[0] = t_upd
                    W("vector", "sPE", t_upd)
                    for dh in range(2):
                        E("vector", lambda e: e.nop())
                        E("vector", lambda e, dh=dh, h=h, cdh=cdh:
                          e.scalar_tensor_tensor(
                              Ssb[:, h * 512 + dh * 256:
                                  h * 512 + dh * 256 + 256],
                              Ssb[:, h * 512 + dh * 256:
                                  h * 512 + dh * 256 + 256],
                              cdh, p_upd[:, dh * 256:(dh + 1) * 256],
                              ALU.mult, ALU.add))
                    last_upd = T("vector", "sDV")
                    upd_free[0] = last_upd
                if phase == "m":
                    W("scalar", "sDV", last_upd)
                    s_spill_tok[0] = D("scalar", "sAs",
                                       s_d[:, (2 * c) * 2048:(2 * c + 2) * 2048],
                                       scr[:, 0:4096])
                scan_pe_free = P.ctr["sPE"].v
                pump(2)

        Gg_v = Gg.rearrange("(c p) n -> p c n", p=128)
        Gu_v = Gu.rearrange("(c p) n -> p c n", p=128)
        chain_tok = {}
        gu_ld_free = [P.ctr["sPE"].v, P.ctr["sPE"].v]
        pGU_free = [P.ctr["sDV"].v] * 4
        for i in range(4):
            fls_dve[i] = P.ctr["sDV"].v
        gelu_free[0] = gelu_free[1] = P.ctr["sDV"].v
        wd_free = P.ctr["sPE"].v
        gu_tok = (0, 0)

        for g in range(8):
            assert len(stage_tok[("g", g)]) == 16, (g, len(stage_tok[("g", g)]))
            assert len(stage_tok[("u", g)]) == 16
            tok_gu_staged = max(stage_tok[("g", g)][-1], stage_tok[("u", g)][-1])
            for i in range(16):
                pump(2)
                sl = (i // 2) % 2
                if i % 2 == 0:
                    W("sync", "sAs", tok_gu_staged)
                    W("sync", "sPE", gu_ld_free[sl])
                    ncp0 = (16 * g + i) * 128
                    t_gg = D("sync", "sLg%d" % sl, gu_sb[sl][:, 0:16 * 256].rearrange(
                        "p (c n) -> p c n", n=256), Gg_v[:, :, ncp0:ncp0 + 256])
                    t_gu = D("sync", "sLg%d" % sl,
                             gu_sb[sl][:, 16 * 256:32 * 256].rearrange(
                                 "p (c n) -> p c n", n=256),
                             Gu_v[:, :, ncp0:ncp0 + 256])
                    gu_tok = (t_gg, t_gu)
                io = i % 2
                for th in range(2):
                    FE("tensor", "sLg%d" % sl)
                    W("tensor", "sDV", pGU_free[th])
                    W("tensor", "sDV", pGU_free[2 + th])
                    for kc in range(16):
                        E("tensor", lambda e, sl=sl, kc=kc, io=io, th=th:
                          e.matmul(pA[th][:, 0:512],
                                   gu_sb[sl][:, kc * 256 + io * 128:
                                             kc * 256 + io * 128 + 128],
                                   xnT[:, kc * TC + th * 512:
                                       kc * TC + (th + 1) * 512],
                                   start=(kc == 0), stop=(kc == 15)))
                    t_g = T("tensor", "sPE")
                    for kc in range(16):
                        E("tensor", lambda e, sl=sl, kc=kc, io=io, th=th:
                          e.matmul(pA[2 + th][:, 0:512],
                                   gu_sb[sl][:, 16 * 256 + kc * 256 + io * 128:
                                             16 * 256 + kc * 256 + io * 128 + 128],
                                   xnT[:, kc * TC + th * 512:
                                       kc * TC + (th + 1) * 512],
                                   start=(kc == 0), stop=(kc == 15)))
                    t_u = T("tensor", "sPE")
                    W("scalar", "sPE", t_g)
                    W("scalar", "sDV", gelu_free[th])
                    E("scalar", lambda e, th=th: e.activation(
                        scr2[:, 1024 + th * 512: 1024 + (th + 1) * 512],
                        pA[th][:, 0:512], AF.Gelu_apprx_tanh))
                    t_gelu = T("scalar", "sAC")
                    W("vector", "sAC", t_gelu)
                    W("vector", "sPE", t_u)
                    E("vector", lambda e, i=i, th=th: e.tensor_tensor(
                        itr[:, i * 1024 + th * 512: i * 1024 + (th + 1) * 512],
                        scr2[:, 1024 + th * 512: 1024 + (th + 1) * 512],
                        pA[2 + th][:, 0:512], ALU.mult))
                    t_ch = T("vector", "sDV")
                    chain_tok[(i, th)] = t_ch
                    gelu_free[th] = t_ch
                    pGU_free[th] = t_ch
                    pGU_free[2 + th] = t_ch
                gu_ld_free[sl] = P.ctr["sPE"].v
            assert len(stage_tok[("d", g)]) == 16
            tok_d_staged = stage_tok[("d", g)][-1]
            for q in range(4):
                pump(2)
                W("sync", "sPE", wd_free)
                W("sync", "sAs", tok_d_staged)
                for i in range(16):
                    r0 = (16 * g + i) * 128
                    D("sync", "sLw", wbig[:, i * 512:(i + 1) * 512],
                      Gd[r0:r0 + 128, q * 512:(q + 1) * 512])
                for ts_ in range(8):
                    o4 = ts_ % 4
                    reg = pB[o4 // 2][:, (o4 % 2) * 512:(o4 % 2) * 512 + 512]
                    W("tensor", "sDV", fls_dve[o4])
                    if ts_ == 0:
                        FE("tensor", "sLw")
                    for i in range(16):
                        if q == 0 and ts_ == 0:
                            W("tensor", "sDV", chain_tok[(i, 0)])
                            W("tensor", "sDV", chain_tok[(i, 1)])
                        E("tensor", lambda e, reg=reg, i=i, ts_=ts_:
                          e.matmul(reg,
                                   itr[:, i * 1024 + ts_ * 128:
                                       i * 1024 + ts_ * 128 + 128],
                                   wbig[:, i * 512:(i + 1) * 512],
                                   start=(i == 0), stop=(i == 15)))
                    t_dn = T("tensor", "sPE")
                    W("vector", "sPE", t_dn)
                    o2 = ts_ % 2
                    FE("vector", "sGa%d" % o2)
                    E("vector", lambda e, o2=o2, reg=reg: e.tensor_copy(
                        fls[:, o2 * 512:(o2 + 1) * 512], reg))
                    t_fl = T("vector", "sDV")
                    fls_dve[o4] = t_fl
                    W("gpsimd", "sDV", t_fl)
                    kw = {} if g == 0 else {"accum_op": ALU.add}
                    D("gpsimd", "sGa%d" % o2,
                      acc_d[:, ts_ * 2048 + q * 512:
                            ts_ * 2048 + q * 512 + 512],
                      fls[:, o2 * 512:(o2 + 1) * 512], **kw)
                wd_free = P.ctr["sPE"].v

        W("sync", "sDV", P.ctr["sDV"].v)
        FE("sync", "sGa0")
        FE("sync", "sGa1")
        FE("sync", "sAs")
        cmb_free = 0
        last_store = 0
        for ts_ in range(8):
            W("sync", "sDV", cmb_free)
            FE("sync", "sLo")
            t_x = D("sync", "sLx", scr[:, 0:2048],
                    x_in[ts_ * 128:(ts_ + 1) * 128, :])
            t_s = D("sync", "sLx", scr[:, 2048:4096],
                    s_d[:, ts_ * 2048:(ts_ + 1) * 2048])
            atile = scr2[:, 0:2048]
            t_a = D("sync", "sLx", atile, acc_d[:, ts_ * 2048:(ts_ + 1) * 2048])
            FE("scalar", "sLx")
            W("scalar", "sDV", cmb_free)
            E("scalar", lambda e, a=atile: e.activation(a, a, AF.Silu))
            t_sl = T("scalar", "sAC")
            W("vector", "sAC", t_sl)
            FE("vector", "sLx")
            E("vector", lambda e, a=atile: e.tensor_tensor(
                scr[:, 2048:4096], scr[:, 2048:4096], a, ALU.mult))
            t_sg = T("vector", "sDV")
            W("vector", "sDV", t_sg)
            E("vector", lambda e: e.tensor_tensor(
                scr[:, 0:2048], scr[:, 0:2048], scr[:, 2048:4096], ALU.add))
            t_cb = T("vector", "sDV")
            cmb_free = t_cb
            W("sync", "sDV", t_cb)
            last_store = D("sync", "sLo", out_t[ts_ * 128:(ts_ + 1) * 128, :],
                           scr[:, 0:2048])

        dbg_reg = {"xnT_t": None, "qT_d": qT_d, "kT_d": kT_d, "kt_d": kt_d,
                   "v_d": v_d, "akt_d": akt_d, "av_d": av_d, "s_d": s_d,
                   "acc_d": acc_d, "Gg": Gg, "Gu": Gu, "Gd": Gd,
                   "gsh_d": gsh_d, "dsh_d": dsh_d}
        if dbg:
            FE("sync", "sAs")
            FE("sync", "sGa0")
            FE("sync", "sGa1")
            W("sync", "sDV", P.ctr["sDV"].v)
            for name in dbg:
                if name in ("xnT_e", "xnbf0"):
                    continue
                if name == "xnT_t":
                    o = nc.dram_tensor("dbg_xnT", [128, 16 * TC], BF16,
                                       kind="ExternalOutput").ap()
                    last_store = D("sync", "sLo", o, xnT[:, :])
                    continue
                src = dbg_reg[name]
                o = nc.dram_tensor("dbg_" + name, list(src.shape),
                                   src.dtype, kind="ExternalOutput").ap()
                last_store = D("sync", "sLo", o, src)
            FE("sync", "sLo")

        FE("sync", "sLo")
        FE("gpsimd", "sLo")

        assert dec_pos[0] == len(dec_units), (dec_pos[0], len(dec_units))

        @block.tensor
        def _(e):
            for f in P.ops["tensor"]:
                f(e)

        @block.vector
        def _(e):
            for f in P.ops["vector"]:
                f(e)

        @block.scalar
        def _(e):
            for f in P.ops["scalar"]:
                f(e)

        @block.gpsimd
        def _(e):
            for f in P.ops["gpsimd"]:
                f(e)

        @block.sync
        def _(e):
            for f in P.ops["sync"]:
                f(e)

    return nc


def _host_prepack(w_int32):
    b = np.ascontiguousarray(w_int32).view(np.uint8)
    b = b.reshape(w_int32.shape[0], w_int32.shape[1], 4)[:, :, 0]
    bT = np.ascontiguousarray(b.T)
    n = bT.shape[1]
    il = np.stack([np.arange(n // 2), np.arange(n // 2) + n // 2], 1).reshape(-1)
    bTi = np.ascontiguousarray(bT[:, il])
    return bTi.view(np.int16)


_CACHE = {}


def kernel(L0_attn_x, L0_attn_Wq, L0_attn_bq, L0_attn_Wk, L0_attn_bk,
           L0_attn_Wv, L0_attn_bv, L0_ffn_gate_W_packed, L0_ffn_gate_scales,
           L0_ffn_up_W_packed, L0_ffn_up_scales, L0_ffn_down_W_packed,
           L0_ffn_down_scales):
    x = np.ascontiguousarray(np.asarray(L0_attn_x, dtype=np.float32)).reshape(8192, H)
    wq = np.ascontiguousarray(np.asarray(L0_attn_Wq, dtype=np.float32))
    wk = np.ascontiguousarray(np.asarray(L0_attn_Wk, dtype=np.float32))
    wv = np.ascontiguousarray(np.asarray(L0_attn_Wv, dtype=np.float32))
    bq = np.ascontiguousarray(np.asarray(L0_attn_bq, dtype=np.float32)).reshape(1, H)
    bk = np.ascontiguousarray(np.asarray(L0_attn_bk, dtype=np.float32)).reshape(1, H)
    bv = np.ascontiguousarray(np.asarray(L0_attn_bv, dtype=np.float32)).reshape(1, H)
    gp = _host_prepack(np.ascontiguousarray(np.asarray(L0_ffn_gate_W_packed, dtype=np.int32)))
    up = _host_prepack(np.ascontiguousarray(np.asarray(L0_ffn_up_W_packed, dtype=np.int32)))
    dp = _host_prepack(np.ascontiguousarray(np.asarray(L0_ffn_down_W_packed, dtype=np.int32)))
    gs = np.ascontiguousarray(np.asarray(L0_ffn_gate_scales, dtype=np.float32).T)
    us = np.ascontiguousarray(np.asarray(L0_ffn_up_scales, dtype=np.float32).T)
    ds = np.ascontiguousarray(np.asarray(L0_ffn_down_scales, dtype=np.float32).T)

    z_x = np.zeros((TC, H), np.float32)
    z_b = np.zeros((1, H), np.float32)

    key = tuple(DBG)
    if key not in _CACHE:
        _CACHE[key] = build(dbg=key)
    nc = _CACHE[key]

    in_maps = []
    for r in range(NCORES):
        odd = r % 2 == 1
        in_maps.append({
            "x": x[r * TC:(r + 1) * TC],
            "xa": x[(r - 1) * TC: r * TC] if odd else z_x,
            "wq": wq, "wk": wk, "wv": wv,
            "bq": bq, "bk": bk, "bv": bv,
            "bak": bk if odd else z_b, "bav": bv if odd else z_b,
            "gatep": gp, "upp": up, "downp": dp,
            "gates": gs, "ups": us, "downs": ds,
        })

    res = run_bass_kernel_spmd(nc, in_maps, core_ids=list(range(NCORES)))
    _LAST.clear()
    _LAST.update({"results": res.results})
    out = np.concatenate([res.results[r]["out"] for r in range(NCORES)], axis=0)
    return out.reshape(4, 2048, H)


if __name__ == "__main__":
    nc = build()
    n = sum(1 for _ in nc.all_instructions())
    print("built ok; instructions:", n)


# version 09
# speedup vs baseline: 1.4731x; 1.0022x over previous
from contextlib import ExitStack

import numpy as np

import concourse.bass as bass
import concourse.mybir as mybir
from concourse.bass_utils import run_bass_kernel_spmd

F32 = mybir.dt.float32
BF16 = mybir.dt.bfloat16
I16 = mybir.dt.int16
AF = mybir.ActivationFunctionType
ALU = mybir.AluOpType
AX = mybir.AxisListType

NCORES = 8
H = 2048
TC = 1024
HEADS = 8
CHUNK = 256
NCH = 4
MLP = 16384
EPS = 1e-5

_BETAS = np.linspace(-5.0, -9.0, 8).astype(np.float32).astype(np.float64)
GAMMA = 1.0 - np.exp2(_BETAS)
LNG = np.log(GAMMA)
CD = GAMMA ** CHUNK


class Ctr:
    __slots__ = ("v",)

    def __init__(self):
        self.v = 0

    def inc(self, d):
        self.v += d
        return self.v


class Prog:
    ENGINES = ("tensor", "vector", "scalar", "gpsimd", "sync")

    def __init__(self):
        self.ops = {e: [] for e in self.ENGINES}
        self.sem = {}
        self.ctr = {}

    def add_sem(self, name, handle):
        self.sem[name] = handle
        self.ctr[name] = Ctr()

    def emit(self, eng, fn):
        self.ops[eng].append(fn)

    def wait(self, eng, sem_name, val):
        if val is None or val <= 0:
            return
        s = self.sem[sem_name]
        self.emit(eng, lambda e, s=s, v=val: e.wait_ge(s, v))
        self.emit(eng, lambda e: e.nop(nofuse=True))

    def fence(self, eng, sem_name):
        self.wait(eng, sem_name, self.ctr[sem_name].v)

    def dma(self, eng, sem_name, out, in_, **kw):
        s = self.sem[sem_name]
        tok = self.ctr[sem_name].inc(16)
        self.emit(eng, lambda e, s=s, o=out, i=in_, kw=kw:
                  e.dma_start(out=o, in_=i, **kw).then_inc(s, 16))
        return tok

    def tag(self, eng, sem_name):
        s = self.sem[sem_name]
        tok = self.ctr[sem_name].inc(1)
        prev = self.ops[eng].pop()
        self.emit(eng, lambda e, p=prev, s=s: p(e).then_inc(s, 1))
        return tok


DBG = []
_LAST = {}


def build(dbg=()):
    nc = bass.Bass("TRN2", target_bir_lowering=False)

    def din(name, shape, dt):
        return nc.dram_tensor(name, list(shape), dt, kind="ExternalInput").ap()

    x_in = din("x", (TC, H), F32)
    xa_in = din("xa", (TC, H), F32)
    wq_in = din("wq", (H, H), F32)
    wk_in = din("wk", (H, H), F32)
    wv_in = din("wv", (H, H), F32)
    b_ins = {"q": din("bq", (1, H), F32), "k": din("bk", (1, H), F32),
             "v": din("bv", (1, H), F32), "ak": din("bak", (1, H), F32),
             "av": din("bav", (1, H), F32)}
    gp_in = din("gatep", (1024, 8192), I16)
    up_in = din("upp", (1024, 8192), I16)
    dp_in = din("downp", (8192, 1024), I16)
    gs_in = din("gates", (64, MLP), F32)
    us_in = din("ups", (64, MLP), F32)
    ds_in = din("downs", (512, H), F32)

    out_t = nc.dram_tensor("out", [TC, H], F32, kind="ExternalOutput").ap()

    def dint(name, shape, dt):
        return nc.dram_tensor(name, list(shape), dt).ap()

    qT_d = dint("qT_d", (128, 16 * TC), BF16)
    kT_d = dint("kT_d", (128, 16 * TC), BF16)
    kt_d = dint("kt_d", (128, 8 * H), BF16)
    v_d = dint("v_d", (128, 8 * H), BF16)
    akt_d = dint("akt_d", (128, 8 * H), BF16)
    av_d = dint("av_d", (128, 8 * H), BF16)
    s_d = dint("s_d", (128, 8 * H), F32)
    acc_d = dint("acc_d", (128, 8 * H), F32)
    Gg = dint("Gg", (H, MLP), BF16)
    Gu = dint("Gu", (H, MLP), BF16)
    Gd = dint("Gd", (MLP, H), BF16)
    gsh_d = dint("gsh_d", (64, MLP), BF16)
    ush_d = dint("ush_d", (64, MLP), BF16)
    dsh_d = dint("dsh_d", (512, H), BF16)

    with ExitStack() as ctx:
        sbt = lambda n, s, d: ctx.enter_context(nc.sbuf_tensor(n, list(s), d))
        pst = lambda n, s, d: ctx.enter_context(nc.psum_tensor(n, list(s), d))

        xnT = sbt("xnT", (128, 16 * TC), BF16)
        wbig = sbt("wbig", (128, 8 * 1024), BF16)
        scr = sbt("scr", (128, 4096), F32)
        scr2 = sbt("scr2", (128, 2048), F32)
        pk_sb = sbt("pk_sb", (128, 2 * 1024), I16)
        dreg = [sbt(f"dreg{i}", (128, 1024), I16) for i in range(8)]
        slab = sbt("slab_sb", (128, 4 * 1024), BF16)
        srep = sbt("srep_sb", (128, 2 * 1024), BF16)
        qe = sbt("qe", (128, 2 * 1024), BF16)
        Ssb = sbt("Ssb", (128, 16 * 256), F32)
        Sbf = sbt("Sbf", (128, 2 * 256), BF16)
        ksb = sbt("ksb", (128, 2 * 256), BF16)
        attnM = sbt("attnM", (128, 2 * 256), BF16)
        qs_sb = sbt("qs_sb", (128, 2 * 256), BF16)
        gu_sb = [sbt(f"gu{i}", (128, 16 * 512), BF16) for i in range(2)]
        itr = sbt("itr", (128, 16 * 1024), BF16)
        xanT = itr
        cons = sbt("cons", (128, 8), F32)
        csb = sbt("csb", (128, 8 * 256), BF16)
        kvc = sbt("kvc", (128, 16), F32)
        dmsk = sbt("dmsk", (128, 2 * 256), F32)
        ident = sbt("ident", (128, 128), BF16)
        onesb = sbt("onesb", (128, 512), BF16)
        brow = sbt("brow", (128, H), BF16)
        iop = sbt("iop", (128, 2), F32)
        lst = sbt("lst", (128, 16), F32)
        fls = sbt("fls", (128, 2 * 512), F32)

        dtt = scr2[:, 0:512]
        ioc = scr2[:, 512:768]

        pA = [pst(f"pA{i}", (128, 512), F32) for i in range(4)]
        pB = [pst(f"pB{i}", (128, 1024), F32) for i in range(2)]
        p_upd = pB[0][:, 0:512]

        P = Prog()
        for s in ("sLx", "sLp0", "sLp1", "sGs0", "sGs1", "sLg0", "sLg1",
                  "sLw", "sGw0", "sGw1", "sGb", "sAq0", "sAq1", "sAs",
                  "sGa0", "sGa1", "sLo",
                  "sPE", "sDV", "sAC", "sGP"):
            P.add_sem(s, ctx.enter_context(nc.semaphore(s)))

        block = ctx.enter_context(nc.Block())
        E, W, D, T, FE = P.emit, P.wait, P.dma, P.tag, P.fence

        mask_free = [0]
        s_spill_tok = [0]
        attn_free = [0]
        ks_free = [0]
        upd_free = [0]
        gelu_free = [0, 0]
        fls_dve = [0, 0, 0, 0]
        fls_gd = [0, 0]

        for col, val in {0: -4.0, 1: -16.0, 2: 1.0, 3: EPS}.items():
            E("gpsimd", lambda e, c=col, v=val: e.memset(cons[:, c:c + 1], v))
        E("gpsimd", lambda e: e.memset(onesb[0:1, :], 1.0))
        E("gpsimd", lambda e: e.memset(Ssb[:, :], 0.0))
        E("gpsimd", lambda e: e.iota(ioc, pattern=[[1, 256]], base=0,
                                     channel_multiplier=0,
                                     allow_small_or_imprecise_dtypes=True))
        E("gpsimd", lambda e: e.iota(iop[:, 0:1], pattern=[[0, 1]], base=0,
                                     channel_multiplier=1,
                                     allow_small_or_imprecise_dtypes=True))
        for jc in range(2):
            E("gpsimd", lambda e, jc=jc: e.iota(
                dmsk[:, jc * 256:(jc + 1) * 256], pattern=[[1, 256]],
                base=-128 * jc, channel_multiplier=-1,
                allow_small_or_imprecise_dtypes=True))
            t_io = T("gpsimd", "sGP")
            W("gpsimd", "sGP", t_io)
            E("gpsimd", lambda e, jc=jc: e.affine_select(
                dmsk[:, jc * 256:(jc + 1) * 256], dmsk[:, jc * 256:(jc + 1) * 256],
                pattern=[[1, 256]], compare_op=ALU.is_ge, fill=1.0e9,
                base=-128 * jc, channel_multiplier=-1))
        E("gpsimd", lambda e: e.iota(scr2[:, 768:896], pattern=[[1, 128]], base=0,
                                     channel_multiplier=-1,
                                     allow_small_or_imprecise_dtypes=True))
        tok_gp0 = T("gpsimd", "sGP")

        W("vector", "sGP", tok_gp0)
        E("vector", lambda e: e.tensor_scalar(scr2[:, 896:1024], scr2[:, 768:896],
                                              0.0, None, ALU.is_equal))
        t_ieq = T("vector", "sDV")
        W("vector", "sDV", t_ieq)
        E("vector", lambda e: e.tensor_copy(ident[:, :], scr2[:, 896:1024]))
        tok_ident = T("vector", "sDV")

        W("scalar", "sGP", tok_gp0)
        for h in range(HEADS):
            E("scalar", lambda e, h=h: e.activation(
                csb[:, h * 256:(h + 1) * 256], ioc, AF.Exp, scale=float(LNG[h])))
        for tsl in range(2):
            for h in range(HEADS):
                col = tsl * 8 + h
                E("scalar", lambda e, h=h, col=col: e.activation(
                    kvc[:, col:col + 1], iop[:, 0:1], AF.Exp, scale=-float(LNG[h])))
        tok_exp = T("scalar", "sAC")
        W("vector", "sAC", tok_exp)
        for h in range(HEADS):
            E("vector", lambda e, h=h: e.tensor_scalar(
                csb[:, h * 256:(h + 1) * 256], csb[:, h * 256:(h + 1) * 256],
                float(GAMMA[h]), None, ALU.mult))
            E("vector", lambda e: e.nop())
        for tsl in range(2):
            for h in range(HEADS):
                col = tsl * 8 + h
                mv = float(GAMMA[h] ** (255 - 128 * tsl))
                E("vector", lambda e: e.nop())
                E("vector", lambda e, col=col, mv=mv: e.tensor_scalar(
                    kvc[:, col:col + 1], kvc[:, col:col + 1], mv, None, ALU.mult))
        T("vector", "sDV")

        last_sc_st = 0
        sc_chunks = []
        for src, dst, rows, cols in ((gs_in, gsh_d, 64, MLP),
                                     (us_in, ush_d, 64, MLP),
                                     (ds_in, dsh_d, 512, H)):
            for r0 in range(0, rows, 64):
                rr = min(64, rows - r0)
                for c0 in range(0, cols, 4096):
                    cc = min(4096, cols - c0)
                    sc_chunks.append((src, dst, r0, rr, c0, cc))
        sc_st_tok = [0, 0]
        sc_ld_tok = [0, 0]
        for ci, (src, dst, r0, rr, c0, cc) in enumerate(sc_chunks):
            pp = ci % 2
            rb = pp * 64
            W("sync", "sDV", sc_st_tok[pp])
            t_ld = D("sync", "sLx", scr[rb:rb + rr, 0:cc],
                     src[r0:r0 + rr, c0:c0 + cc])
            W("vector", "sLx", t_ld)
            W("vector", "sAs", sc_ld_tok[pp])
            obuf = (qe[rb:rb + rr, 0:cc] if cc <= 2048 else
                    wbig[0:rr, pp * 4096: pp * 4096 + cc])
            E("vector", lambda e, rr=rr, cc=cc, rb=rb, obuf=obuf: e.tensor_scalar(
                obuf, scr[rb:rb + rr, 0:cc], 0.5, None, ALU.mult))
            t_h = T("vector", "sDV")
            sc_st_tok[pp] = t_h
            E("vector", lambda e: e.nop())
            W("scalar", "sDV", t_h)
            last_sc_st = D("scalar", "sAs", dst[r0:r0 + rr, c0:c0 + cc], obuf)
            sc_ld_tok[pp] = last_sc_st
        tok_scales_st = last_sc_st
        tok_scales_dve = P.ctr["sDV"].v

        def load_bias(name, wait_pe=None):
            if wait_pe:
                W("gpsimd", "sPE", wait_pe)
            return D("gpsimd", "sGb", brow[0:1, 0:H], b_ins[name])

        gsh_aps = {"g": gsh_d, "u": ush_d}
        pk_aps = {"g": gp_in, "u": up_in}
        G_aps = {"g": Gg, "u": Gu}
        stage_tok = {(m, g): [] for m in ("g", "u", "d") for g in range(8)}
        ds_pk_free = [0, 0]
        ds_srep_free = [0, 0]
        slab_st_tok = [0, 0]
        ds_slot = [0]
        FD = 1024

        def emit_chain_pair(planes, pk_ap):
            for j, (sh, _, _) in enumerate(planes):
                r = dreg[4 * j]
                if sh:
                    E("vector", lambda e, r=r, sh=sh: e.tensor_scalar(
                        r[:, :], pk_ap, sh, 15,
                        ALU.logical_shift_right, ALU.bitwise_and))
                else:
                    E("vector", lambda e, r=r: e.tensor_scalar(
                        r[:, :], pk_ap, 15, None, ALU.bitwise_and))
            for j in range(len(planes)):
                c0, m = dreg[4 * j], dreg[4 * j + 1]
                E("vector", lambda e, c0=c0, m=m: e.tensor_scalar(
                    m[:, :], c0[:, :], 7, None, ALU.bitwise_and))
            tok_m = T("vector", "sDV")
            W("scalar", "sDV", tok_m)
            for j in range(len(planes)):
                m, a, b = dreg[4 * j + 1], dreg[4 * j + 2], dreg[4 * j + 3]
                E("scalar", lambda e, m=m, a=a: e.activation(
                    a[:, :], m[:, :], AF.Identity, scale=2.0, bias=cons[:, 0:1]))
                E("scalar", lambda e, m=m, b=b: e.activation(
                    b[:, :], m[:, :], AF.Identity, scale=4.0, bias=cons[:, 1:2]))
            tok_ab = T("scalar", "sAC")
            W("vector", "sAC", tok_ab)
            for j in range(len(planes)):
                a, b = dreg[4 * j + 2], dreg[4 * j + 3]
                E("vector", lambda e, a=a, b=b: e.tensor_tensor(
                    a[:, :], a[:, :], b[:, :], ALU.max))
            for j in range(len(planes)):
                m, a = dreg[4 * j + 1], dreg[4 * j + 2]
                E("vector", lambda e, m=m, a=a: e.tensor_tensor(
                    a[:, :], m[:, :], a[:, :], ALU.max))
            for j in range(len(planes)):
                c0, b = dreg[4 * j], dreg[4 * j + 3]
                E("vector", lambda e, c0=c0, b=b: e.tensor_scalar(
                    b[:, :], c0[:, :], 3, 1,
                    ALU.logical_shift_right, ALU.bitwise_and))
            for j in range(len(planes)):
                b = dreg[4 * j + 3]
                E("vector", lambda e, b=b: e.tensor_scalar(
                    b[:, :], b[:, :], -2, 1, ALU.mult, ALU.add))
            for j in range(len(planes)):
                a, b = dreg[4 * j + 2], dreg[4 * j + 3]
                E("vector", lambda e, a=a, b=b: e.tensor_tensor(
                    a[:, :], a[:, :], b[:, :], ALU.mult))
            for j, (_, slab_sl, srep_sl) in enumerate(planes):
                a = dreg[4 * j + 2]
                E("vector", lambda e, a=a, o=slab_sl, sr=srep_sl:
                  e.tensor_tensor(o, a[:, :], sr, ALU.mult))
            return T("vector", "sDV")

        def emit_unit_gu(mat, g, c, ch):
            b = g // 4
            frq = 2048 * (g % 4) + 1024 * ch
            n0 = 2048 * g + 1024 * ch
            slot = ds_slot[0]
            ds_slot[0] ^= 1
            W("sync", "sDV", ds_pk_free[slot])
            pk_sl = pk_sb[:, slot * 1024: slot * 1024 + 1024]
            t_pk = D("sync", "sLp%d" % slot, pk_sl,
                     pk_aps[mat][c * 128:(c + 1) * 128, frq:frq + 1024])
            if ds_srep_free[slot] == 0:
                W("gpsimd", "sAs", tok_scales_st)
            W("gpsimd", "sDV", ds_srep_free[slot])
            sr_sl = srep[:, slot * 1024: slot * 1024 + 1024]
            t_sr = D("gpsimd", "sGs%d" % slot, sr_sl,
                     gsh_aps[mat][8 * c:8 * c + 8, n0:n0 + 1024]
                     .unsqueeze(1).broadcast_to([8, 16, 1024]))
            FE("vector", "sLp%d" % slot)
            FE("vector", "sGs%d" % slot)
            W("vector", "sAs", slab_st_tok[slot])
            sb0 = slab[:, slot * 2048: slot * 2048 + 1024]
            sb1 = slab[:, slot * 2048 + 1024: slot * 2048 + 2048]
            planes = [(8 * b, sb0, sr_sl), (8 * b + 4, sb1, sr_sl)]
            tok_w = emit_chain_pair(planes, pk_sl)
            ds_pk_free[slot] = tok_w
            ds_srep_free[slot] = tok_w
            W("scalar", "sDV", tok_w)
            Gm = G_aps[mat]
            D("scalar", "sAs", Gm[256 * c: 256 * c + 256: 2, n0:n0 + 1024], sb0)
            t_st = D("scalar", "sAs",
                     Gm[256 * c + 1: 256 * c + 256: 2, n0:n0 + 1024], sb1)
            slab_st_tok[slot] = t_st
            stage_tok[(mat, g)].append(t_st)

        def emit_unit_down(g, cr, pairi):
            cp = 8 * g + cr
            slot = ds_slot[0]
            ds_slot[0] ^= 1
            W("sync", "sDV", ds_pk_free[slot])
            pk_sl = pk_sb[:, slot * 1024: slot * 1024 + 1024]
            t_pk = D("sync", "sLp%d" % slot, pk_sl,
                     dp_in[cp * 128:(cp + 1) * 128, 0:1024])
            if ds_srep_free[slot] == 0:
                W("gpsimd", "sAs", tok_scales_st)
            W("gpsimd", "sDV", ds_srep_free[slot])
            sr_sl = srep[:, slot * 1024: slot * 1024 + 1024]
            t_sr = D("gpsimd", "sGs%d" % slot, sr_sl,
                     dsh_d[8 * cp:8 * cp + 8, pairi * 1024:(pairi + 1) * 1024]
                     .unsqueeze(1).broadcast_to([8, 16, 1024]))
            FE("vector", "sLp%d" % slot)
            FE("vector", "sGs%d" % slot)
            W("vector", "sAs", slab_st_tok[slot])
            sb0 = slab[:, slot * 2048: slot * 2048 + 1024]
            sb1 = slab[:, slot * 2048 + 1024: slot * 2048 + 2048]
            planes = [(8 * pairi, sb0, sr_sl), (8 * pairi + 4, sb1, sr_sl)]
            tok_w = emit_chain_pair(planes, pk_sl)
            ds_pk_free[slot] = tok_w
            ds_srep_free[slot] = tok_w
            W("scalar", "sDV", tok_w)
            t_last = 0
            for nib, sl in ((0, sb0), (1, sb1)):
                t_last = D("scalar", "sAs",
                           Gd[256 * cp + nib: 256 * cp + 256: 2,
                              pairi * 1024: (pairi + 1) * 1024], sl)
            slab_st_tok[slot] = t_last
            stage_tok[("d", g)].append(t_last)

        dec_units = []
        for g in range(8):
            for c in range(8):
                for ch in range(2):
                    dec_units.append(("g", g, c, ch))
                    dec_units.append(("u", g, c, ch))
            for cr in range(8):
                for pairi in range(2):
                    dec_units.append(("d", g, cr, pairi))
        dec_pos = [0]
        dec_on = [False]

        def pump(n):
            if not dec_on[0]:
                return
            for _ in range(n):
                if dec_pos[0] >= len(dec_units):
                    return
                kind, g, c, x2 = dec_units[dec_pos[0]]
                dec_pos[0] += 1
                if kind == "d":
                    emit_unit_down(g, c, x2)
                else:
                    emit_unit_gu(kind, g, c, x2)

        x_sl = scr[:, 0:2048]
        sq_sl = scr[:, 2048:4096]
        xnbf = wbig[:, 0:2048]
        slot_free = tok_scales_dve
        xnbf_pe_free = 0
        dec_on[0] = True
        FE("vector", "sAs")
        W("scalar", "sDV", tok_scales_dve)
        FE("scalar", "sAs")
        tr_evac_tok = {0: 0, 1: 0}
        tpsum = [pB[0][:, 0:64].bitcast(BF16), pB[1][:, 0:64].bitcast(BF16)]

        for kind, tt in [("m", t) for t in range(8)] + [("a", t) for t in range(8)]:
            src = x_in if kind == "m" else xa_in
            W("sync", "sDV", slot_free)
            t_ld = D("sync", "sLx", x_sl, src[tt * 128:(tt + 1) * 128, :])
            FE("vector", "sLx")
            FE("scalar", "sLx")
            E("vector", lambda e: e.tensor_reduce(lst[:, 0:1], x_sl, AX.X, ALU.add))
            t_s0 = T("vector", "sDV")
            E("scalar", lambda e: e.activation(sq_sl, x_sl, AF.Square))
            t_sq = T("scalar", "sAC")
            W("vector", "sAC", t_sq)
            E("vector", lambda e: e.tensor_reduce(lst[:, 1:2], sq_sl, AX.X, ALU.add))
            W("vector", "sDV", t_s0)
            E("vector", lambda e: e.tensor_scalar(lst[:, 2:3], lst[:, 0:1],
                                                  1.0 / H, None, ALU.mult))
            t_ms = T("vector", "sDV")
            W("scalar", "sDV", t_ms)
            E("scalar", lambda e: e.activation(lst[:, 3:4], lst[:, 1:2],
                                               AF.Identity, scale=1.0 / H))
            E("scalar", lambda e: e.activation(lst[:, 4:5], lst[:, 2:3],
                                               AF.Square))
            t_m2 = T("scalar", "sAC")
            W("vector", "sAC", t_m2)
            E("vector", lambda e: e.tensor_tensor(lst[:, 5:6], lst[:, 3:4],
                                                  lst[:, 4:5], ALU.subtract))
            t_var = T("vector", "sDV")
            W("scalar", "sDV", t_var)
            E("scalar", lambda e: e.activation(lst[:, 6:7], lst[:, 5:6],
                                               AF.Sqrt, bias=cons[:, 3:4]))
            t_std = T("scalar", "sAC")
            W("vector", "sAC", t_std)
            E("vector", lambda e: e.reciprocal(lst[:, 7:8], lst[:, 6:7]))
            t_rc = T("vector", "sDV")
            W("vector", "sDV", t_rc)
            W("vector", "sPE", xnbf_pe_free)
            E("vector", lambda e: e.tensor_scalar(xnbf, x_sl, lst[:, 2:3],
                                                  lst[:, 7:8], ALU.subtract,
                                                  ALU.mult))
            t_xn = T("vector", "sDV")
            slot_free = t_xn
            if kind == "m" and tt == 0 and "xnbf0" in dbg:
                dbg_x0 = nc.dram_tensor("dbg_x0", [128, 2048], F32,
                                        kind="ExternalOutput").ap()
                dbg_xb0 = nc.dram_tensor("dbg_xnbf0", [128, 2048], BF16,
                                         kind="ExternalOutput").ap()
                dbg_lst0 = nc.dram_tensor("dbg_lst0", [128, 16], F32,
                                          kind="ExternalOutput").ap()
                W("sync", "sDV", t_xn)
                tt_x0 = D("sync", "sLo", dbg_x0, x_sl)
                tt_xb = D("sync", "sLo", dbg_xb0, xnbf)
                tt_ls = D("sync", "sLo", dbg_lst0, lst[:, 0:16])
                FE("vector", "sLo")
                FE("tensor", "sLo")
            dstT = xnT if kind == "m" else xanT
            W("tensor", "sDV", t_xn)
            W("tensor", "sDV", tok_ident)
            for kc in range(16):
                pp = kc % 2
                W("tensor", "sDV", tr_evac_tok[pp])
                E("tensor", lambda e, kc=kc, pp=pp: e.transpose(
                    tpsum[pp], xnbf[:, kc * 128:(kc + 1) * 128], ident[:, :]))
                t_tr = T("tensor", "sPE")
                W("vector", "sPE", t_tr)
                E("vector", lambda e, dstT=dstT, kc=kc, tt=tt, pp=pp:
                  e.tensor_copy(
                      dstT[:, kc * TC + tt * 128: kc * TC + (tt + 1) * 128],
                      tpsum[pp]))
                tr_evac_tok[pp] = T("vector", "sDV")
            xnbf_pe_free = P.ctr["sPE"].v
            pump(2)
        xnT_ready = max(tr_evac_tok[0], tr_evac_tok[1])

        if "xnT_e" in dbg:
            dbg_xe = nc.dram_tensor("dbg_xnT_e", [128, 16 * TC], BF16,
                                    kind="ExternalOutput").ap()
            W("sync", "sDV", xnT_ready)
            D("sync", "sLo", dbg_xe, xnT[:, :])
        FE("gpsimd", "sAs")
        dec_on[0] = True

        pB_free = [xnT_ready, xnT_ready]
        bias_tok = [0]
        FE("vector", "sAs")

        def emit_fm(w_ap, bias_slot, dst_d):
            wslot_free = [P.ctr["sPE"].v, P.ctr["sPE"].v]
            for ocg in range(8):
                ws = ocg % 2
                W("gpsimd", "sPE", wslot_free[ws])
                t_w = D("gpsimd", "sGw%d" % ws, gu_sb[ws][:, 0:16 * 256].rearrange(
                    "p (c n) -> p c n", n=256),
                    w_ap.rearrange("(c p) n -> p c n", p=128)
                    [:, :, ocg * 256:(ocg + 1) * 256])
                pump(1)
                for oc2 in range(2):
                    oc = 2 * ocg + oc2
                    FE("tensor", "sGw%d" % ws)
                    FE("tensor", "sGb")
                    W("tensor", "sDV", pB_free[oc2])
                    for tb in range(2):
                        for kc in range(16):
                            E("tensor", lambda e, oc2=oc2, tb=tb, kc=kc, ws=ws:
                              e.matmul(pB[oc2][:, tb * 512:(tb + 1) * 512],
                                       gu_sb[ws][:, kc * 256 + oc2 * 128:
                                                 kc * 256 + oc2 * 128 + 128],
                                       xnT[:, kc * TC + tb * 512:
                                           kc * TC + (tb + 1) * 512],
                                       start=(kc == 0), stop=False))
                        E("tensor", lambda e, oc2=oc2, tb=tb, oc=oc, bs=bias_slot:
                          e.matmul(pB[oc2][:, tb * 512:(tb + 1) * 512],
                                   brow[0:1, oc * 128: oc * 128 + 128],
                                   onesb[0:1, 0:512], start=False, stop=True))
                    t_pe = T("tensor", "sPE")
                    W("vector", "sPE", t_pe)
                    FE("vector", "sAq%d" % oc2)
                    E("vector", lambda e, oc2=oc2: e.tensor_copy(
                        qe[:, oc2 * 1024:(oc2 + 1) * 1024], pB[oc2][:, :]))
                    t_ev = T("vector", "sDV")
                    pB_free[oc2] = t_ev
                    W("scalar", "sDV", t_ev)
                    D("scalar", "sAq%d" % oc2,
                      dst_d[:, oc * 1024:(oc + 1) * 1024],
                      qe[:, oc2 * 1024:(oc2 + 1) * 1024])
                    wslot_free[ws] = t_pe

        bias_tok[0] = load_bias("q")
        emit_fm(wq_in, 0, qT_d)
        bias_tok[0] = load_bias("k", wait_pe=P.ctr["sPE"].v)
        emit_fm(wk_in, 0, kT_d)

        tm_wfree = [[P.ctr["sPE"].v] * 16, [P.ctr["sPE"].v] * 16]
        tm_par = [0]

        def emit_tm(w_ap, bias_main, bias_alien, main_dst, alien_dst):
            for dq in range(4):
                wp = tm_par[0]
                tm_par[0] ^= 1
                wbuf = wbig if wp == 0 else gu_sb[1][:, 0:8192]
                for kc in range(16):
                    W("gpsimd", "sPE", tm_wfree[wp][kc])
                    D("gpsimd", "sGw%d" % wp,
                      wbuf[:, kc * 512:(kc + 1) * 512],
                      w_ap.rearrange("(c p) n -> p c n", p=128)
                      [:, kc, dq * 512:(dq + 1) * 512])
                    if kc % 4 == 3:
                        pump(1)
                for srcT, bname, dst_d in ((xnT, bias_main, main_dst),
                                           (xanT, bias_alien, alien_dst)):
                    bias_tok[0] = load_bias(bname, wait_pe=P.ctr["sPE"].v)
                    for tsub in range(8):
                        o2 = tsub % 2
                        reg = pB[o2][:, 0:512]
                        W("tensor", "sDV", pB_free[o2])
                        FE("tensor", "sGb")
                        FE("tensor", "sGw%d" % wp)
                        for kc in range(16):
                            E("tensor", lambda e, reg=reg, kc=kc, tsub=tsub,
                              srcT=srcT, wbuf=wbuf:
                              e.matmul(reg,
                                       srcT[:, kc * TC + tsub * 128:
                                            kc * TC + tsub * 128 + 128],
                                       wbuf[:, kc * 512:(kc + 1) * 512],
                                       start=(kc == 0), stop=False))
                        E("tensor", lambda e, reg=reg, dq=dq:
                          e.matmul(reg, onesb[0:1, 0:128],
                                   brow[0:1, dq * 512:(dq + 1) * 512],
                                   start=False, stop=True))
                        t_pe = T("tensor", "sPE")
                        W("vector", "sPE", t_pe)
                        FE("vector", "sAq%d" % o2)
                        E("vector", lambda e, o2=o2, reg=reg: e.tensor_copy(
                            qe[:, o2 * 1024: o2 * 1024 + 512], reg))
                        t_ev = T("vector", "sDV")
                        pB_free[o2] = t_ev
                        W("scalar", "sDV", t_ev)
                        D("scalar", "sAq%d" % o2,
                          dst_d[:, tsub * 2048 + dq * 512:
                                tsub * 2048 + (dq + 1) * 512],
                          qe[:, o2 * 1024: o2 * 1024 + 512])
                    pump(2)
                for kc in range(16):
                    tm_wfree[wp][kc] = P.ctr["sPE"].v

        emit_tm(wk_in, "k", "ak", kt_d, akt_d)
        emit_tm(wv_in, "v", "av", v_d, av_d)

        qT_v = qT_d.rearrange("p (c t) -> p c t", t=TC)
        kT_v = kT_d.rearrange("p (c t) -> p c t", t=TC)

        pA_free = [pB_free[0]] * 4
        upd_free[0] = pB_free[0]
        scan_pe_free = P.ctr["sPE"].v
        scan_dve_free = P.ctr["sDV"].v
        last_upd = P.ctr["sDV"].v

        for phase, src_kt, src_v in (("a", akt_d, av_d), ("m", kt_d, v_d)):
            for c in range(NCH):
                FE("sync", "sAq0")
                FE("sync", "sAq1")
                W("sync", "sPE", scan_pe_free)
                W("sync", "sDV", last_upd)
                t_kt = D("sync", "sLx", wbig[:, 0:4096],
                         src_kt[:, (2 * c) * 2048:(2 * c + 2) * 2048])
                t_v = D("sync", "sLx", wbig[:, 4096:8192],
                        src_v[:, (2 * c) * 2048:(2 * c + 2) * 2048])
                t_q = t_k2 = None
                if phase == "m":
                    t_q = D("sync", "sLx", gu_sb[0][:, 0:4096].rearrange(
                        "p (k t) -> p k t", t=256),
                        qT_v[:, :, c * 256:(c + 1) * 256])
                    t_k2 = D("sync", "sLx", gu_sb[0][:, 4096:8192].rearrange(
                        "p (k t) -> p k t", t=256),
                        kT_v[:, :, c * 256:(c + 1) * 256])
                pump(1)
                for h in range(HEADS):
                    hp = h % 2
                    cdh = float(CD[h])
                    if phase == "m":
                        W("vector", "sDV", last_upd)
                        E("vector", lambda e, h=h: e.tensor_copy(
                            Sbf[:, :], Ssb[:, h * 512:(h + 1) * 512]))
                        t_sbf = T("vector", "sDV")
                        W("scalar", "sDV", mask_free[0])
                        for jc in range(2):
                            E("scalar", lambda e, h=h, jc=jc: e.activation(
                                dtt[:, jc * 256:(jc + 1) * 256],
                                dmsk[:, jc * 256:(jc + 1) * 256], AF.Exp,
                                scale=float(LNG[h])))
                        t_dt = T("scalar", "sAC")
                        FE("tensor", "sLx")
                        W("tensor", "sDV", pA_free[hp])
                        for jc in range(2):
                            for dc in range(2):
                                E("tensor", lambda e, h=h, jc=jc, dc=dc, hp=hp:
                                  e.matmul(pA[hp][:, jc * 256:(jc + 1) * 256],
                                           gu_sb[0][:, 4096 + (2 * h + dc) * 256 +
                                                 jc * 128:
                                                 4096 + (2 * h + dc) * 256 +
                                                 jc * 128 + 128],
                                           gu_sb[0][:, (2 * h + dc) * 256:
                                                 (2 * h + dc) * 256 + 256],
                                           start=(dc == 0), stop=(dc == 1)))
                        t_at = T("tensor", "sPE")
                        W("vector", "sPE", t_at)
                        W("vector", "sPE", attn_free[0])
                        W("vector", "sAC", t_dt)
                        for jc in range(2):
                            E("vector", lambda e, jc=jc, hp=hp: e.tensor_tensor(
                                attnM[:, jc * 256:(jc + 1) * 256],
                                pA[hp][:, jc * 256:(jc + 1) * 256],
                                dtt[:, jc * 256:(jc + 1) * 256], ALU.mult))
                            E("vector", lambda e, h=h, jc=jc: e.tensor_tensor(
                                qs_sb[:, jc * 256:(jc + 1) * 256],
                                gu_sb[0][:, (2 * h + jc) * 256:
                                      (2 * h + jc) * 256 + 256],
                                csb[:, h * 256:(h + 1) * 256], ALU.mult))
                        t_mq = T("vector", "sDV")
                        pA_free[hp] = t_mq
                        mask_free[0] = t_mq
                        W("tensor", "sDV", t_mq)
                        W("tensor", "sDV", t_sbf)
                        W("tensor", "sDV", pA_free[2 + hp])
                        for ih in range(2):
                            for jc in range(2):
                                E("tensor", lambda e, ih=ih, jc=jc, h=h, hp=hp:
                                  e.matmul(pA[2 + hp][:, ih * 256:(ih + 1) * 256],
                                           attnM[:, jc * 256 + ih * 128:
                                                 jc * 256 + ih * 128 + 128],
                                           wbig[:, 4096 + jc * 2048 + h * 256:
                                                4096 + jc * 2048 + h * 256 + 256],
                                           start=(jc == 0), stop=False))
                            for dc in range(2):
                                E("tensor", lambda e, ih=ih, dc=dc, hp=hp:
                                  e.matmul(pA[2 + hp][:, ih * 256:(ih + 1) * 256],
                                           qs_sb[:, dc * 256 + ih * 128:
                                                 dc * 256 + ih * 128 + 128],
                                           Sbf[:, dc * 256:(dc + 1) * 256],
                                           start=False, stop=(dc == 1)))
                        t_out = T("tensor", "sPE")
                        attn_free[0] = t_out
                        W("vector", "sPE", t_out)
                        W("vector", "sAs", s_spill_tok[0])
                        for ih in range(2):
                            E("vector", lambda e, ih=ih, h=h, hp=hp:
                              e.tensor_copy(
                                  scr[:, ih * 2048 + h * 256:
                                      ih * 2048 + h * 256 + 256],
                                  pA[2 + hp][:, ih * 256:(ih + 1) * 256]))
                        t_sev = T("vector", "sDV")
                        pA_free[2 + hp] = t_sev
                    FE("vector", "sLx")
                    W("vector", "sPE", ks_free[0])
                    for tsl in range(2):
                        E("vector", lambda e: e.nop())
                        E("vector", lambda e, tsl=tsl, h=h: e.tensor_scalar(
                            ksb[:, tsl * 256:(tsl + 1) * 256],
                            wbig[:, tsl * 2048 + h * 256:
                                 tsl * 2048 + h * 256 + 256],
                            kvc[:, (tsl * 8 + h):(tsl * 8 + h) + 1], None,
                            ALU.mult))
                    t_ks = T("vector", "sDV")
                    W("tensor", "sDV", t_ks)
                    W("tensor", "sDV", upd_free[0])
                    FE("tensor", "sLx")
                    for dh in range(2):
                        for tsl in range(2):
                            E("tensor", lambda e, dh=dh, tsl=tsl, h=h:
                              e.matmul(p_upd[:, dh * 256:(dh + 1) * 256],
                                       ksb[:, tsl * 256 + dh * 128:
                                           tsl * 256 + dh * 128 + 128],
                                       wbig[:, 4096 + tsl * 2048 + h * 256:
                                            4096 + tsl * 2048 + h * 256 + 256],
                                       start=(tsl == 0), stop=(tsl == 1)))
                    t_upd = T("tensor", "sPE")
                    ks_free[0] = t_upd
                    W("vector", "sPE", t_upd)
                    for dh in range(2):
                        E("vector", lambda e: e.nop())
                        E("vector", lambda e, dh=dh, h=h, cdh=cdh:
                          e.scalar_tensor_tensor(
                              Ssb[:, h * 512 + dh * 256:
                                  h * 512 + dh * 256 + 256],
                              Ssb[:, h * 512 + dh * 256:
                                  h * 512 + dh * 256 + 256],
                              cdh, p_upd[:, dh * 256:(dh + 1) * 256],
                              ALU.mult, ALU.add))
                    last_upd = T("vector", "sDV")
                    upd_free[0] = last_upd
                if phase == "m":
                    W("scalar", "sDV", last_upd)
                    s_spill_tok[0] = D("scalar", "sAs",
                                       s_d[:, (2 * c) * 2048:(2 * c + 2) * 2048],
                                       scr[:, 0:4096])
                scan_pe_free = P.ctr["sPE"].v
                pump(2)

        Gg_v = Gg.rearrange("(c p) n -> p c n", p=128)
        Gu_v = Gu.rearrange("(c p) n -> p c n", p=128)
        chain_tok = {}
        gu_ld_free = [P.ctr["sPE"].v, P.ctr["sPE"].v]
        pGU_free = [P.ctr["sDV"].v] * 4
        for i in range(4):
            fls_dve[i] = P.ctr["sDV"].v
        gelu_free[0] = gelu_free[1] = P.ctr["sDV"].v
        wd_free = P.ctr["sPE"].v
        gu_tok = (0, 0)

        for g in range(8):
            assert len(stage_tok[("g", g)]) == 16, (g, len(stage_tok[("g", g)]))
            assert len(stage_tok[("u", g)]) == 16
            tok_gu_staged = max(stage_tok[("g", g)][-1], stage_tok[("u", g)][-1])
            for i in range(16):
                pump(2)
                sl = (i // 2) % 2
                if i % 2 == 0:
                    W("sync", "sAs", tok_gu_staged)
                    W("sync", "sPE", gu_ld_free[sl])
                    ncp0 = (16 * g + i) * 128
                    t_gg = D("sync", "sLg%d" % sl, gu_sb[sl][:, 0:16 * 256].rearrange(
                        "p (c n) -> p c n", n=256), Gg_v[:, :, ncp0:ncp0 + 256])
                    t_gu = D("sync", "sLg%d" % sl,
                             gu_sb[sl][:, 16 * 256:32 * 256].rearrange(
                                 "p (c n) -> p c n", n=256),
                             Gu_v[:, :, ncp0:ncp0 + 256])
                    gu_tok = (t_gg, t_gu)
                io = i % 2
                for th in range(2):
                    FE("tensor", "sLg%d" % sl)
                    W("tensor", "sDV", pGU_free[th])
                    W("tensor", "sDV", pGU_free[2 + th])
                    for kc in range(16):
                        E("tensor", lambda e, sl=sl, kc=kc, io=io, th=th:
                          e.matmul(pA[th][:, 0:512],
                                   gu_sb[sl][:, kc * 256 + io * 128:
                                             kc * 256 + io * 128 + 128],
                                   xnT[:, kc * TC + th * 512:
                                       kc * TC + (th + 1) * 512],
                                   start=(kc == 0), stop=(kc == 15)))
                    t_g = T("tensor", "sPE")
                    for kc in range(16):
                        E("tensor", lambda e, sl=sl, kc=kc, io=io, th=th:
                          e.matmul(pA[2 + th][:, 0:512],
                                   gu_sb[sl][:, 16 * 256 + kc * 256 + io * 128:
                                             16 * 256 + kc * 256 + io * 128 + 128],
                                   xnT[:, kc * TC + th * 512:
                                       kc * TC + (th + 1) * 512],
                                   start=(kc == 0), stop=(kc == 15)))
                    t_u = T("tensor", "sPE")
                    W("scalar", "sPE", t_g)
                    W("scalar", "sDV", gelu_free[th])
                    E("scalar", lambda e, th=th: e.activation(
                        scr2[:, 1024 + th * 512: 1024 + (th + 1) * 512],
                        pA[th][:, 0:512], AF.Gelu_apprx_tanh))
                    t_gelu = T("scalar", "sAC")
                    W("vector", "sAC", t_gelu)
                    W("vector", "sPE", t_u)
                    E("vector", lambda e, i=i, th=th: e.tensor_tensor(
                        itr[:, i * 1024 + th * 512: i * 1024 + (th + 1) * 512],
                        scr2[:, 1024 + th * 512: 1024 + (th + 1) * 512],
                        pA[2 + th][:, 0:512], ALU.mult))
                    t_ch = T("vector", "sDV")
                    chain_tok[(i, th)] = t_ch
                    gelu_free[th] = t_ch
                    pGU_free[th] = t_ch
                    pGU_free[2 + th] = t_ch
                gu_ld_free[sl] = P.ctr["sPE"].v
            assert len(stage_tok[("d", g)]) == 16
            tok_d_staged = stage_tok[("d", g)][-1]
            for q in range(4):
                pump(2)
                W("sync", "sPE", wd_free)
                W("sync", "sAs", tok_d_staged)
                for i in range(16):
                    r0 = (16 * g + i) * 128
                    D("sync", "sLw", wbig[:, i * 512:(i + 1) * 512],
                      Gd[r0:r0 + 128, q * 512:(q + 1) * 512])
                for ts_ in range(8):
                    o4 = ts_ % 4
                    reg = pB[o4 // 2][:, (o4 % 2) * 512:(o4 % 2) * 512 + 512]
                    W("tensor", "sDV", fls_dve[o4])
                    if ts_ == 0:
                        FE("tensor", "sLw")
                    for i in range(16):
                        if q == 0 and ts_ == 0:
                            W("tensor", "sDV", chain_tok[(i, 0)])
                            W("tensor", "sDV", chain_tok[(i, 1)])
                        E("tensor", lambda e, reg=reg, i=i, ts_=ts_:
                          e.matmul(reg,
                                   itr[:, i * 1024 + ts_ * 128:
                                       i * 1024 + ts_ * 128 + 128],
                                   wbig[:, i * 512:(i + 1) * 512],
                                   start=(i == 0), stop=(i == 15)))
                    t_dn = T("tensor", "sPE")
                    W("vector", "sPE", t_dn)
                    o2 = ts_ % 2
                    FE("vector", "sGa%d" % o2)
                    E("vector", lambda e, o2=o2, reg=reg: e.tensor_copy(
                        fls[:, o2 * 512:(o2 + 1) * 512], reg))
                    t_fl = T("vector", "sDV")
                    fls_dve[o4] = t_fl
                    W("gpsimd", "sDV", t_fl)
                    kw = {} if g == 0 else {"accum_op": ALU.add}
                    D("gpsimd", "sGa%d" % o2,
                      acc_d[:, ts_ * 2048 + q * 512:
                            ts_ * 2048 + q * 512 + 512],
                      fls[:, o2 * 512:(o2 + 1) * 512], **kw)
                wd_free = P.ctr["sPE"].v

        W("sync", "sDV", P.ctr["sDV"].v)
        FE("sync", "sGa0")
        FE("sync", "sGa1")
        FE("sync", "sAs")
        cmb_free = 0
        last_store = 0
        for ts_ in range(8):
            W("sync", "sDV", cmb_free)
            FE("sync", "sLo")
            t_x = D("sync", "sLx", scr[:, 0:2048],
                    x_in[ts_ * 128:(ts_ + 1) * 128, :])
            t_s = D("sync", "sLx", scr[:, 2048:4096],
                    s_d[:, ts_ * 2048:(ts_ + 1) * 2048])
            atile = scr2[:, 0:2048]
            t_a = D("sync", "sLx", atile, acc_d[:, ts_ * 2048:(ts_ + 1) * 2048])
            FE("scalar", "sLx")
            W("scalar", "sDV", cmb_free)
            E("scalar", lambda e, a=atile: e.activation(a, a, AF.Silu))
            t_sl = T("scalar", "sAC")
            W("vector", "sAC", t_sl)
            FE("vector", "sLx")
            E("vector", lambda e, a=atile: e.tensor_tensor(
                scr[:, 2048:4096], scr[:, 2048:4096], a, ALU.mult))
            t_sg = T("vector", "sDV")
            W("vector", "sDV", t_sg)
            E("vector", lambda e: e.tensor_tensor(
                scr[:, 0:2048], scr[:, 0:2048], scr[:, 2048:4096], ALU.add))
            t_cb = T("vector", "sDV")
            cmb_free = t_cb
            W("sync", "sDV", t_cb)
            last_store = D("sync", "sLo", out_t[ts_ * 128:(ts_ + 1) * 128, :],
                           scr[:, 0:2048])

        dbg_reg = {"xnT_t": None, "qT_d": qT_d, "kT_d": kT_d, "kt_d": kt_d,
                   "v_d": v_d, "akt_d": akt_d, "av_d": av_d, "s_d": s_d,
                   "acc_d": acc_d, "Gg": Gg, "Gu": Gu, "Gd": Gd,
                   "gsh_d": gsh_d, "dsh_d": dsh_d}
        if dbg:
            FE("sync", "sAs")
            FE("sync", "sGa0")
            FE("sync", "sGa1")
            W("sync", "sDV", P.ctr["sDV"].v)
            for name in dbg:
                if name in ("xnT_e", "xnbf0"):
                    continue
                if name == "xnT_t":
                    o = nc.dram_tensor("dbg_xnT", [128, 16 * TC], BF16,
                                       kind="ExternalOutput").ap()
                    last_store = D("sync", "sLo", o, xnT[:, :])
                    continue
                src = dbg_reg[name]
                o = nc.dram_tensor("dbg_" + name, list(src.shape),
                                   src.dtype, kind="ExternalOutput").ap()
                last_store = D("sync", "sLo", o, src)
            FE("sync", "sLo")

        FE("sync", "sLo")
        FE("gpsimd", "sLo")

        assert dec_pos[0] == len(dec_units), (dec_pos[0], len(dec_units))

        @block.tensor
        def _(e):
            for f in P.ops["tensor"]:
                f(e)

        @block.vector
        def _(e):
            for f in P.ops["vector"]:
                f(e)

        @block.scalar
        def _(e):
            for f in P.ops["scalar"]:
                f(e)

        @block.gpsimd
        def _(e):
            for f in P.ops["gpsimd"]:
                f(e)

        @block.sync
        def _(e):
            for f in P.ops["sync"]:
                f(e)

    return nc


def _host_prepack(w_int32):
    b = np.ascontiguousarray(w_int32).view(np.uint8)
    b = b.reshape(w_int32.shape[0], w_int32.shape[1], 4)[:, :, 0]
    bT = np.ascontiguousarray(b.T)
    n = bT.shape[1]
    il = np.stack([np.arange(n // 2), np.arange(n // 2) + n // 2], 1).reshape(-1)
    bTi = np.ascontiguousarray(bT[:, il])
    return bTi.view(np.int16)


_CACHE = {}


def kernel(L0_attn_x, L0_attn_Wq, L0_attn_bq, L0_attn_Wk, L0_attn_bk,
           L0_attn_Wv, L0_attn_bv, L0_ffn_gate_W_packed, L0_ffn_gate_scales,
           L0_ffn_up_W_packed, L0_ffn_up_scales, L0_ffn_down_W_packed,
           L0_ffn_down_scales):
    x = np.ascontiguousarray(np.asarray(L0_attn_x, dtype=np.float32)).reshape(8192, H)
    wq = np.ascontiguousarray(np.asarray(L0_attn_Wq, dtype=np.float32))
    wk = np.ascontiguousarray(np.asarray(L0_attn_Wk, dtype=np.float32))
    wv = np.ascontiguousarray(np.asarray(L0_attn_Wv, dtype=np.float32))
    bq = np.ascontiguousarray(np.asarray(L0_attn_bq, dtype=np.float32)).reshape(1, H)
    bk = np.ascontiguousarray(np.asarray(L0_attn_bk, dtype=np.float32)).reshape(1, H)
    bv = np.ascontiguousarray(np.asarray(L0_attn_bv, dtype=np.float32)).reshape(1, H)
    gp = _host_prepack(np.ascontiguousarray(np.asarray(L0_ffn_gate_W_packed, dtype=np.int32)))
    up = _host_prepack(np.ascontiguousarray(np.asarray(L0_ffn_up_W_packed, dtype=np.int32)))
    dp = _host_prepack(np.ascontiguousarray(np.asarray(L0_ffn_down_W_packed, dtype=np.int32)))
    gs = np.ascontiguousarray(np.asarray(L0_ffn_gate_scales, dtype=np.float32).T)
    us = np.ascontiguousarray(np.asarray(L0_ffn_up_scales, dtype=np.float32).T)
    ds = np.ascontiguousarray(np.asarray(L0_ffn_down_scales, dtype=np.float32).T)

    z_x = np.zeros((TC, H), np.float32)
    z_b = np.zeros((1, H), np.float32)

    key = tuple(DBG)
    if key not in _CACHE:
        _CACHE[key] = build(dbg=key)
    nc = _CACHE[key]

    in_maps = []
    for r in range(NCORES):
        odd = r % 2 == 1
        in_maps.append({
            "x": x[r * TC:(r + 1) * TC],
            "xa": x[(r - 1) * TC: r * TC] if odd else z_x,
            "wq": wq, "wk": wk, "wv": wv,
            "bq": bq, "bk": bk, "bv": bv,
            "bak": bk if odd else z_b, "bav": bv if odd else z_b,
            "gatep": gp, "upp": up, "downp": dp,
            "gates": gs, "ups": us, "downs": ds,
        })

    res = run_bass_kernel_spmd(nc, in_maps, core_ids=list(range(NCORES)))
    _LAST.clear()
    _LAST.update({"results": res.results})
    out = np.concatenate([res.results[r]["out"] for r in range(NCORES)], axis=0)
    return out.reshape(4, 2048, H)


if __name__ == "__main__":
    nc = build()
    n = sum(1 for _ in nc.all_instructions())
    print("built ok; instructions:", n)


# version 11
# speedup vs baseline: 1.4878x; 1.0099x over previous
from contextlib import ExitStack

import numpy as np

import concourse.bass as bass
import concourse.mybir as mybir
from concourse.bass_utils import run_bass_kernel_spmd

F32 = mybir.dt.float32
BF16 = mybir.dt.bfloat16
I16 = mybir.dt.int16
AF = mybir.ActivationFunctionType
ALU = mybir.AluOpType
AX = mybir.AxisListType

NCORES = 8
H = 2048
TC = 1024
HEADS = 8
CHUNK = 256
NCH = 4
MLP = 16384
EPS = 1e-5

_BETAS = np.linspace(-5.0, -9.0, 8).astype(np.float32).astype(np.float64)
GAMMA = 1.0 - np.exp2(_BETAS)
LNG = np.log(GAMMA)
CD = GAMMA ** CHUNK


class Ctr:
    __slots__ = ("v",)

    def __init__(self):
        self.v = 0

    def inc(self, d):
        self.v += d
        return self.v


class Prog:
    ENGINES = ("tensor", "vector", "scalar", "gpsimd", "sync")

    def __init__(self):
        self.ops = {e: [] for e in self.ENGINES}
        self.sem = {}
        self.ctr = {}

    def add_sem(self, name, handle):
        self.sem[name] = handle
        self.ctr[name] = Ctr()

    def emit(self, eng, fn):
        self.ops[eng].append(fn)

    def wait(self, eng, sem_name, val):
        if val is None or val <= 0:
            return
        s = self.sem[sem_name]
        self.emit(eng, lambda e, s=s, v=val: e.wait_ge(s, v))
        self.emit(eng, lambda e: e.nop(nofuse=True))

    def fence(self, eng, sem_name):
        self.wait(eng, sem_name, self.ctr[sem_name].v)

    def dma(self, eng, sem_name, out, in_, **kw):
        s = self.sem[sem_name]
        tok = self.ctr[sem_name].inc(16)
        self.emit(eng, lambda e, s=s, o=out, i=in_, kw=kw:
                  e.dma_start(out=o, in_=i, **kw).then_inc(s, 16))
        return tok

    def tag(self, eng, sem_name):
        s = self.sem[sem_name]
        tok = self.ctr[sem_name].inc(1)
        prev = self.ops[eng].pop()
        self.emit(eng, lambda e, p=prev, s=s: p(e).then_inc(s, 1))
        return tok


DBG = []
_LAST = {}


def build(dbg=()):
    nc = bass.Bass("TRN2", target_bir_lowering=False)

    def din(name, shape, dt):
        return nc.dram_tensor(name, list(shape), dt, kind="ExternalInput").ap()

    x_in = din("x", (TC, H), F32)
    xa_in = din("xa", (TC, H), F32)
    wq_in = din("wq", (H, H), F32)
    wk_in = din("wk", (H, H), F32)
    wv_in = din("wv", (H, H), F32)
    b_ins = {"q": din("bq", (1, H), F32), "k": din("bk", (1, H), F32),
             "v": din("bv", (1, H), F32), "ak": din("bak", (1, H), F32),
             "av": din("bav", (1, H), F32)}
    gp_in = din("gatep", (1024, 8192), I16)
    up_in = din("upp", (1024, 8192), I16)
    dp_in = din("downp", (8192, 1024), I16)
    gs_in = din("gates", (64, MLP), F32)
    us_in = din("ups", (64, MLP), F32)
    ds_in = din("downs", (512, H), F32)

    out_t = nc.dram_tensor("out", [TC, H], F32, kind="ExternalOutput").ap()

    def dint(name, shape, dt):
        return nc.dram_tensor(name, list(shape), dt).ap()

    qT_d = dint("qT_d", (128, 16 * TC), BF16)
    kT_d = dint("kT_d", (128, 16 * TC), BF16)
    kt_d = dint("kt_d", (128, 8 * H), BF16)
    v_d = dint("v_d", (128, 8 * H), BF16)
    akt_d = dint("akt_d", (128, 8 * H), BF16)
    av_d = dint("av_d", (128, 8 * H), BF16)
    s_d = dint("s_d", (128, 8 * H), F32)
    acc_d = dint("acc_d", (128, 8 * H), F32)
    Gg = dint("Gg", (H, MLP), BF16)
    Gu = dint("Gu", (H, MLP), BF16)
    Gd = dint("Gd", (MLP, H), BF16)
    gsh_d = dint("gsh_d", (64, MLP), BF16)
    ush_d = dint("ush_d", (64, MLP), BF16)
    dsh_d = dint("dsh_d", (512, H), BF16)

    with ExitStack() as ctx:
        sbt = lambda n, s, d: ctx.enter_context(nc.sbuf_tensor(n, list(s), d))
        pst = lambda n, s, d: ctx.enter_context(nc.psum_tensor(n, list(s), d))

        xnT = sbt("xnT", (128, 16 * TC), BF16)
        wbig = sbt("wbig", (128, 8 * 1024), BF16)
        scr = sbt("scr", (128, 4096), F32)
        scr2 = sbt("scr2", (128, 2048), F32)
        pk_sb = sbt("pk_sb", (128, 2 * 1024), I16)
        dreg = [sbt(f"dreg{i}", (128, 1024), I16) for i in range(8)]
        slab = sbt("slab_sb", (128, 4 * 1024), BF16)
        srep = sbt("srep_sb", (128, 2 * 1024), BF16)
        qe = sbt("qe", (128, 2 * 1024), BF16)
        Ssb = sbt("Ssb", (128, 16 * 256), F32)
        Sbf = sbt("Sbf", (128, 2 * 256), BF16)
        ksb = sbt("ksb", (128, 2 * 256), BF16)
        attnM = sbt("attnM", (128, 2 * 256), BF16)
        qs_sb = sbt("qs_sb", (128, 2 * 256), BF16)
        gu_sb = [sbt(f"gu{i}", (128, 16 * 512), BF16) for i in range(2)]
        itr = sbt("itr", (128, 16 * 1024), BF16)
        xanT = itr
        cons = sbt("cons", (128, 8), F32)
        csb = sbt("csb", (128, 8 * 256), BF16)
        kvc = sbt("kvc", (128, 16), F32)
        dmsk = sbt("dmsk", (128, 2 * 256), F32)
        ident = sbt("ident", (128, 128), BF16)
        onesb = sbt("onesb", (128, 512), BF16)
        brow = sbt("brow", (128, H), BF16)
        iop = sbt("iop", (128, 2), F32)
        lst = sbt("lst", (128, 16), F32)
        fls = sbt("fls", (128, 2 * 512), F32)

        dtt = scr2[:, 0:512]
        ioc = scr2[:, 512:768]

        pA = [pst(f"pA{i}", (128, 512), F32) for i in range(4)]
        pB = [pst(f"pB{i}", (128, 1024), F32) for i in range(2)]
        p_upd = pB[0][:, 0:512]

        P = Prog()
        for s in ("sLx", "sLp0", "sLp1", "sGs0", "sGs1", "sLg0", "sLg1",
                  "sLw", "sGw0", "sGw1", "sGb", "sAq0", "sAq1", "sAs",
                  "sGa0", "sGa1", "sLo",
                  "sPE", "sDV", "sAC", "sGP"):
            P.add_sem(s, ctx.enter_context(nc.semaphore(s)))

        block = ctx.enter_context(nc.Block())
        E, W, D, T, FE = P.emit, P.wait, P.dma, P.tag, P.fence

        mask_free = [0]
        s_spill_tok = [0]
        attn_free = [0]
        ks_free = [0]
        upd_free = [0]
        gelu_free = [0, 0]
        fls_dve = [0, 0, 0, 0]
        fls_gd = [0, 0]

        for col, val in {0: -4.0, 1: -16.0, 2: 1.0, 3: EPS}.items():
            E("gpsimd", lambda e, c=col, v=val: e.memset(cons[:, c:c + 1], v))
        E("gpsimd", lambda e: e.memset(onesb[0:1, :], 1.0))
        E("gpsimd", lambda e: e.memset(onesb[32:33, :], 1.0))
        E("gpsimd", lambda e: e.memset(onesb[64:65, :], 1.0))
        E("gpsimd", lambda e: e.memset(Ssb[:, :], 0.0))
        E("gpsimd", lambda e: e.iota(ioc, pattern=[[1, 256]], base=0,
                                     channel_multiplier=0,
                                     allow_small_or_imprecise_dtypes=True))
        E("gpsimd", lambda e: e.iota(iop[:, 0:1], pattern=[[0, 1]], base=0,
                                     channel_multiplier=1,
                                     allow_small_or_imprecise_dtypes=True))
        for jc in range(2):
            E("gpsimd", lambda e, jc=jc: e.iota(
                dmsk[:, jc * 256:(jc + 1) * 256], pattern=[[1, 256]],
                base=-128 * jc, channel_multiplier=-1,
                allow_small_or_imprecise_dtypes=True))
            t_io = T("gpsimd", "sGP")
            W("gpsimd", "sGP", t_io)
            E("gpsimd", lambda e, jc=jc: e.affine_select(
                dmsk[:, jc * 256:(jc + 1) * 256], dmsk[:, jc * 256:(jc + 1) * 256],
                pattern=[[1, 256]], compare_op=ALU.is_ge, fill=1.0e9,
                base=-128 * jc, channel_multiplier=-1))
        E("gpsimd", lambda e: e.iota(scr2[:, 768:896], pattern=[[1, 128]], base=0,
                                     channel_multiplier=-1,
                                     allow_small_or_imprecise_dtypes=True))
        tok_gp0 = T("gpsimd", "sGP")

        W("vector", "sGP", tok_gp0)
        E("vector", lambda e: e.tensor_scalar(scr2[:, 896:1024], scr2[:, 768:896],
                                              0.0, None, ALU.is_equal))
        t_ieq = T("vector", "sDV")
        W("vector", "sDV", t_ieq)
        E("vector", lambda e: e.tensor_copy(ident[:, :], scr2[:, 896:1024]))
        tok_ident = T("vector", "sDV")

        W("scalar", "sGP", tok_gp0)
        for h in range(HEADS):
            E("scalar", lambda e, h=h: e.activation(
                csb[:, h * 256:(h + 1) * 256], ioc, AF.Exp, scale=float(LNG[h])))
        for tsl in range(2):
            for h in range(HEADS):
                col = tsl * 8 + h
                E("scalar", lambda e, h=h, col=col: e.activation(
                    kvc[:, col:col + 1], iop[:, 0:1], AF.Exp, scale=-float(LNG[h])))
        tok_exp = T("scalar", "sAC")
        W("vector", "sAC", tok_exp)
        for h in range(HEADS):
            E("vector", lambda e, h=h: e.tensor_scalar(
                csb[:, h * 256:(h + 1) * 256], csb[:, h * 256:(h + 1) * 256],
                float(GAMMA[h]), None, ALU.mult))
            E("vector", lambda e: e.nop())
        for tsl in range(2):
            for h in range(HEADS):
                col = tsl * 8 + h
                mv = float(GAMMA[h] ** (255 - 128 * tsl))
                E("vector", lambda e: e.nop())
                E("vector", lambda e, col=col, mv=mv: e.tensor_scalar(
                    kvc[:, col:col + 1], kvc[:, col:col + 1], mv, None, ALU.mult))
        T("vector", "sDV")

        last_sc_st = 0
        sc_chunks = []
        for src, dst, rows, cols in ((gs_in, gsh_d, 64, MLP),
                                     (us_in, ush_d, 64, MLP),
                                     (ds_in, dsh_d, 512, H)):
            for r0 in range(0, rows, 64):
                rr = min(64, rows - r0)
                for c0 in range(0, cols, 4096):
                    cc = min(4096, cols - c0)
                    sc_chunks.append((src, dst, r0, rr, c0, cc))
        sc_st_tok = [0, 0]
        sc_ld_tok = [0, 0]
        for ci, (src, dst, r0, rr, c0, cc) in enumerate(sc_chunks):
            pp = ci % 2
            rb = pp * 64
            W("sync", "sDV", sc_st_tok[pp])
            t_ld = D("sync", "sLx", scr[rb:rb + rr, 0:cc],
                     src[r0:r0 + rr, c0:c0 + cc])
            W("vector", "sLx", t_ld)
            W("vector", "sAs", sc_ld_tok[pp])
            obuf = (qe[rb:rb + rr, 0:cc] if cc <= 2048 else
                    wbig[0:rr, pp * 4096: pp * 4096 + cc])
            E("vector", lambda e, rr=rr, cc=cc, rb=rb, obuf=obuf: e.tensor_scalar(
                obuf, scr[rb:rb + rr, 0:cc], 0.5, None, ALU.mult))
            t_h = T("vector", "sDV")
            sc_st_tok[pp] = t_h
            E("vector", lambda e: e.nop())
            W("scalar", "sDV", t_h)
            last_sc_st = D("scalar", "sAs", dst[r0:r0 + rr, c0:c0 + cc], obuf)
            sc_ld_tok[pp] = last_sc_st
        tok_scales_st = last_sc_st
        tok_scales_dve = P.ctr["sDV"].v

        def load_bias(name, wait_pe=None):
            if wait_pe:
                W("gpsimd", "sPE", wait_pe)
            return D("gpsimd", "sGb", brow[0:1, 0:H], b_ins[name])

        gsh_aps = {"g": gsh_d, "u": ush_d}
        pk_aps = {"g": gp_in, "u": up_in}
        G_aps = {"g": Gg, "u": Gu}
        stage_tok = {(m, g): [] for m in ("g", "u", "d") for g in range(8)}
        ds_pk_free = [0, 0]
        ds_srep_free = [0, 0]
        slab_st_tok = [0, 0]
        ds_slot = [0]
        FD = 1024

        def emit_chain_pair(planes, pk_ap):
            for j, (sh, _, _) in enumerate(planes):
                r = dreg[4 * j]
                if sh:
                    E("vector", lambda e, r=r, sh=sh: e.tensor_scalar(
                        r[:, :], pk_ap, sh, 15,
                        ALU.logical_shift_right, ALU.bitwise_and))
                else:
                    E("vector", lambda e, r=r: e.tensor_scalar(
                        r[:, :], pk_ap, 15, None, ALU.bitwise_and))
            for j in range(len(planes)):
                c0, m = dreg[4 * j], dreg[4 * j + 1]
                E("vector", lambda e, c0=c0, m=m: e.tensor_scalar(
                    m[:, :], c0[:, :], 7, None, ALU.bitwise_and))
            tok_m = T("vector", "sDV")
            W("scalar", "sDV", tok_m)
            for j in range(len(planes)):
                m, a, b = dreg[4 * j + 1], dreg[4 * j + 2], dreg[4 * j + 3]
                E("scalar", lambda e, m=m, a=a: e.activation(
                    a[:, :], m[:, :], AF.Identity, scale=2.0, bias=cons[:, 0:1]))
                E("scalar", lambda e, m=m, b=b: e.activation(
                    b[:, :], m[:, :], AF.Identity, scale=4.0, bias=cons[:, 1:2]))
            tok_ab = T("scalar", "sAC")
            W("vector", "sAC", tok_ab)
            for j in range(len(planes)):
                a, b = dreg[4 * j + 2], dreg[4 * j + 3]
                E("vector", lambda e, a=a, b=b: e.tensor_tensor(
                    a[:, :], a[:, :], b[:, :], ALU.max))
            for j in range(len(planes)):
                m, a = dreg[4 * j + 1], dreg[4 * j + 2]
                E("vector", lambda e, m=m, a=a: e.tensor_tensor(
                    a[:, :], m[:, :], a[:, :], ALU.max))
            for j in range(len(planes)):
                c0, b = dreg[4 * j], dreg[4 * j + 3]
                E("vector", lambda e, c0=c0, b=b: e.tensor_scalar(
                    b[:, :], c0[:, :], 3, 1,
                    ALU.logical_shift_right, ALU.bitwise_and))
            for j in range(len(planes)):
                b = dreg[4 * j + 3]
                E("vector", lambda e, b=b: e.tensor_scalar(
                    b[:, :], b[:, :], -2, 1, ALU.mult, ALU.add))
            for j in range(len(planes)):
                a, b = dreg[4 * j + 2], dreg[4 * j + 3]
                E("vector", lambda e, a=a, b=b: e.tensor_tensor(
                    a[:, :], a[:, :], b[:, :], ALU.mult))
            for j, (_, slab_sl, srep_sl) in enumerate(planes):
                a = dreg[4 * j + 2]
                E("vector", lambda e, a=a, o=slab_sl, sr=srep_sl:
                  e.tensor_tensor(o, a[:, :], sr, ALU.mult))
            return T("vector", "sDV")

        def emit_unit_gu(mat, g, c, ch):
            b = g // 4
            frq = 2048 * (g % 4) + 1024 * ch
            n0 = 2048 * g + 1024 * ch
            slot = ds_slot[0]
            ds_slot[0] ^= 1
            W("sync", "sDV", ds_pk_free[slot])
            pk_sl = pk_sb[:, slot * 1024: slot * 1024 + 1024]
            t_pk = D("sync", "sLp%d" % slot, pk_sl,
                     pk_aps[mat][c * 128:(c + 1) * 128, frq:frq + 1024])
            if ds_srep_free[slot] == 0:
                W("gpsimd", "sAs", tok_scales_st)
            W("gpsimd", "sDV", ds_srep_free[slot])
            sr_sl = srep[:, slot * 1024: slot * 1024 + 1024]
            t_sr = D("gpsimd", "sGs%d" % slot, sr_sl,
                     gsh_aps[mat][8 * c:8 * c + 8, n0:n0 + 1024]
                     .unsqueeze(1).broadcast_to([8, 16, 1024]))
            FE("vector", "sLp%d" % slot)
            FE("vector", "sGs%d" % slot)
            W("vector", "sAs", slab_st_tok[slot])
            sb0 = slab[:, slot * 2048: slot * 2048 + 1024]
            sb1 = slab[:, slot * 2048 + 1024: slot * 2048 + 2048]
            planes = [(8 * b, sb0, sr_sl), (8 * b + 4, sb1, sr_sl)]
            tok_w = emit_chain_pair(planes, pk_sl)
            ds_pk_free[slot] = tok_w
            ds_srep_free[slot] = tok_w
            W("scalar", "sDV", tok_w)
            Gm = G_aps[mat]
            D("scalar", "sAs", Gm[256 * c: 256 * c + 256: 2, n0:n0 + 1024], sb0)
            t_st = D("scalar", "sAs",
                     Gm[256 * c + 1: 256 * c + 256: 2, n0:n0 + 1024], sb1)
            slab_st_tok[slot] = t_st
            stage_tok[(mat, g)].append(t_st)

        def emit_unit_down(g, cr, pairi):
            cp = 8 * g + cr
            slot = ds_slot[0]
            ds_slot[0] ^= 1
            W("sync", "sDV", ds_pk_free[slot])
            pk_sl = pk_sb[:, slot * 1024: slot * 1024 + 1024]
            t_pk = D("sync", "sLp%d" % slot, pk_sl,
                     dp_in[cp * 128:(cp + 1) * 128, 0:1024])
            if ds_srep_free[slot] == 0:
                W("gpsimd", "sAs", tok_scales_st)
            W("gpsimd", "sDV", ds_srep_free[slot])
            sr_sl = srep[:, slot * 1024: slot * 1024 + 1024]
            t_sr = D("gpsimd", "sGs%d" % slot, sr_sl,
                     dsh_d[8 * cp:8 * cp + 8, pairi * 1024:(pairi + 1) * 1024]
                     .unsqueeze(1).broadcast_to([8, 16, 1024]))
            FE("vector", "sLp%d" % slot)
            FE("vector", "sGs%d" % slot)
            W("vector", "sAs", slab_st_tok[slot])
            sb0 = slab[:, slot * 2048: slot * 2048 + 1024]
            sb1 = slab[:, slot * 2048 + 1024: slot * 2048 + 2048]
            planes = [(8 * pairi, sb0, sr_sl), (8 * pairi + 4, sb1, sr_sl)]
            tok_w = emit_chain_pair(planes, pk_sl)
            ds_pk_free[slot] = tok_w
            ds_srep_free[slot] = tok_w
            W("scalar", "sDV", tok_w)
            t_last = 0
            for nib, sl in ((0, sb0), (1, sb1)):
                t_last = D("scalar", "sAs",
                           Gd[256 * cp + nib: 256 * cp + 256: 2,
                              pairi * 1024: (pairi + 1) * 1024], sl)
            slab_st_tok[slot] = t_last
            stage_tok[("d", g)].append(t_last)

        dec_units = []
        for g in range(8):
            for c in range(8):
                for ch in range(2):
                    dec_units.append(("g", g, c, ch))
                    dec_units.append(("u", g, c, ch))
            for cr in range(8):
                for pairi in range(2):
                    dec_units.append(("d", g, cr, pairi))
        dec_pos = [0]
        dec_on = [False]

        def pump(n):
            if not dec_on[0]:
                return
            for _ in range(n):
                if dec_pos[0] >= len(dec_units):
                    return
                kind, g, c, x2 = dec_units[dec_pos[0]]
                dec_pos[0] += 1
                if kind == "d":
                    emit_unit_down(g, c, x2)
                else:
                    emit_unit_gu(kind, g, c, x2)

        x_sl = scr[:, 0:2048]
        sq_sl = scr[:, 2048:4096]
        xnbf = wbig[:, 0:2048]
        slot_free = tok_scales_dve
        xnbf_pe_free = 0
        dec_on[0] = True
        FE("vector", "sAs")
        W("scalar", "sDV", tok_scales_dve)
        FE("scalar", "sAs")
        tr_evac_tok = {0: 0, 1: 0}
        tpsum = [pB[0][:, 0:64].bitcast(BF16), pB[1][:, 0:64].bitcast(BF16)]

        for kind, tt in [("m", t) for t in range(8)] + [("a", t) for t in range(8)]:
            src = x_in if kind == "m" else xa_in
            W("sync", "sDV", slot_free)
            t_ld = D("sync", "sLx", x_sl, src[tt * 128:(tt + 1) * 128, :])
            FE("vector", "sLx")
            FE("scalar", "sLx")
            E("vector", lambda e: e.tensor_reduce(lst[:, 0:1], x_sl, AX.X, ALU.add))
            t_s0 = T("vector", "sDV")
            E("scalar", lambda e: e.activation(sq_sl, x_sl, AF.Square))
            t_sq = T("scalar", "sAC")
            W("vector", "sAC", t_sq)
            E("vector", lambda e: e.tensor_reduce(lst[:, 1:2], sq_sl, AX.X, ALU.add))
            W("vector", "sDV", t_s0)
            E("vector", lambda e: e.tensor_scalar(lst[:, 2:3], lst[:, 0:1],
                                                  1.0 / H, None, ALU.mult))
            t_ms = T("vector", "sDV")
            W("scalar", "sDV", t_ms)
            E("scalar", lambda e: e.activation(lst[:, 3:4], lst[:, 1:2],
                                               AF.Identity, scale=1.0 / H))
            E("scalar", lambda e: e.activation(lst[:, 4:5], lst[:, 2:3],
                                               AF.Square))
            t_m2 = T("scalar", "sAC")
            W("vector", "sAC", t_m2)
            E("vector", lambda e: e.tensor_tensor(lst[:, 5:6], lst[:, 3:4],
                                                  lst[:, 4:5], ALU.subtract))
            t_var = T("vector", "sDV")
            W("scalar", "sDV", t_var)
            E("scalar", lambda e: e.activation(lst[:, 6:7], lst[:, 5:6],
                                               AF.Sqrt, bias=cons[:, 3:4]))
            t_std = T("scalar", "sAC")
            W("vector", "sAC", t_std)
            E("vector", lambda e: e.reciprocal(lst[:, 7:8], lst[:, 6:7]))
            t_rc = T("vector", "sDV")
            W("vector", "sDV", t_rc)
            W("vector", "sPE", xnbf_pe_free)
            E("vector", lambda e: e.tensor_scalar(xnbf, x_sl, lst[:, 2:3],
                                                  lst[:, 7:8], ALU.subtract,
                                                  ALU.mult))
            t_xn = T("vector", "sDV")
            slot_free = t_xn
            if kind == "m" and tt == 0 and "xnbf0" in dbg:
                dbg_x0 = nc.dram_tensor("dbg_x0", [128, 2048], F32,
                                        kind="ExternalOutput").ap()
                dbg_xb0 = nc.dram_tensor("dbg_xnbf0", [128, 2048], BF16,
                                         kind="ExternalOutput").ap()
                dbg_lst0 = nc.dram_tensor("dbg_lst0", [128, 16], F32,
                                          kind="ExternalOutput").ap()
                W("sync", "sDV", t_xn)
                tt_x0 = D("sync", "sLo", dbg_x0, x_sl)
                tt_xb = D("sync", "sLo", dbg_xb0, xnbf)
                tt_ls = D("sync", "sLo", dbg_lst0, lst[:, 0:16])
                FE("vector", "sLo")
                FE("tensor", "sLo")
            dstT = xnT if kind == "m" else xanT
            W("tensor", "sDV", t_xn)
            W("tensor", "sDV", tok_ident)
            for kc in range(16):
                pp = kc % 2
                W("tensor", "sDV", tr_evac_tok[pp])
                E("tensor", lambda e, kc=kc, pp=pp: e.transpose(
                    tpsum[pp], xnbf[:, kc * 128:(kc + 1) * 128], ident[:, :]))
                t_tr = T("tensor", "sPE")
                W("vector", "sPE", t_tr)
                E("vector", lambda e, dstT=dstT, kc=kc, tt=tt, pp=pp:
                  e.tensor_copy(
                      dstT[:, kc * TC + tt * 128: kc * TC + (tt + 1) * 128],
                      tpsum[pp]))
                tr_evac_tok[pp] = T("vector", "sDV")
            xnbf_pe_free = P.ctr["sPE"].v
            pump(2)
        xnT_ready = max(tr_evac_tok[0], tr_evac_tok[1])
        tok_p1_pe = P.ctr["sPE"].v

        if "xnT_e" in dbg:
            dbg_xe = nc.dram_tensor("dbg_xnT_e", [128, 16 * TC], BF16,
                                    kind="ExternalOutput").ap()
            W("sync", "sDV", xnT_ready)
            D("sync", "sLo", dbg_xe, xnT[:, :])
        FE("gpsimd", "sAs")
        dec_on[0] = True

        pB_free = [xnT_ready, xnT_ready]
        bias_tok = [0]
        FE("vector", "sAs")

        def emit_fm(w_ap, bias_slot, dst_d):
            wslot_free = [P.ctr["sPE"].v, P.ctr["sPE"].v]
            for ocg in range(8):
                ws = ocg % 2
                W("gpsimd", "sPE", wslot_free[ws])
                t_w = D("gpsimd", "sGw%d" % ws, gu_sb[ws][:, 0:16 * 256].rearrange(
                    "p (c n) -> p c n", n=256),
                    w_ap.rearrange("(c p) n -> p c n", p=128)
                    [:, :, ocg * 256:(ocg + 1) * 256])
                pump(1)
                for oc2 in range(2):
                    oc = 2 * ocg + oc2
                    FE("tensor", "sGw%d" % ws)
                    FE("tensor", "sGb")
                    W("tensor", "sDV", pB_free[oc2])
                    for tb in range(2):
                        for kc in range(16):
                            E("tensor", lambda e, oc2=oc2, tb=tb, kc=kc, ws=ws:
                              e.matmul(pB[oc2][:, tb * 512:(tb + 1) * 512],
                                       gu_sb[ws][:, kc * 256 + oc2 * 128:
                                                 kc * 256 + oc2 * 128 + 128],
                                       xnT[:, kc * TC + tb * 512:
                                           kc * TC + (tb + 1) * 512],
                                       start=(kc == 0), stop=False))
                        E("tensor", lambda e, oc2=oc2, tb=tb, oc=oc, bs=bias_slot:
                          e.matmul(pB[oc2][:, tb * 512:(tb + 1) * 512],
                                   brow[0:1, oc * 128: oc * 128 + 128],
                                   onesb[0:1, 0:512], start=False, stop=True))
                    t_pe = T("tensor", "sPE")
                    W("vector", "sPE", t_pe)
                    FE("vector", "sAq%d" % oc2)
                    E("vector", lambda e, oc2=oc2: e.tensor_copy(
                        qe[:, oc2 * 1024:(oc2 + 1) * 1024], pB[oc2][:, :]))
                    t_ev = T("vector", "sDV")
                    pB_free[oc2] = t_ev
                    W("scalar", "sDV", t_ev)
                    D("scalar", "sAq%d" % oc2,
                      dst_d[:, oc * 1024:(oc + 1) * 1024],
                      qe[:, oc2 * 1024:(oc2 + 1) * 1024])
                    wslot_free[ws] = t_pe

        bias_tok[0] = load_bias("q")
        emit_fm(wq_in, 0, qT_d)
        bias_tok[0] = load_bias("k", wait_pe=P.ctr["sPE"].v)
        emit_fm(wk_in, 0, kT_d)

        tm_wfree = [[tok_p1_pe] * 16, [P.ctr["sPE"].v] * 16]
        tm_par = [0]
        D("gpsimd", "sGb", brow[32:33, 0:H], b_ins["ak"])
        bias_tok[0] = D("gpsimd", "sGb", brow[64:65, 0:H], b_ins["v"])

        def emit_tm(w_ap, boff_main, boff_alien, main_dst, alien_dst,
                    av_reload=False):
            for dq in range(4):
                wp = tm_par[0]
                tm_par[0] ^= 1
                wbuf = wbig if wp == 0 else gu_sb[1][:, 0:8192]
                for kc in range(16):
                    W("gpsimd", "sPE", tm_wfree[wp][kc])
                    D("gpsimd", "sGw%d" % wp,
                      wbuf[:, kc * 512:(kc + 1) * 512],
                      w_ap.rearrange("(c p) n -> p c n", p=128)
                      [:, kc, dq * 512:(dq + 1) * 512])
                    if kc % 4 == 3:
                        pump(1)
                if av_reload and dq == 0:
                    W("gpsimd", "sPE", P.ctr["sPE"].v)
                    bias_tok[0] = D("gpsimd", "sGb", brow[32:33, 0:H],
                                    b_ins["av"])
                for srcT, boff, dst_d in ((xnT, boff_main, main_dst),
                                          (xanT, boff_alien, alien_dst)):
                    for tsub in range(8):
                        o2 = tsub % 2
                        reg = pB[o2][:, 0:512]
                        W("tensor", "sDV", pB_free[o2])
                        FE("tensor", "sGb")
                        FE("tensor", "sGw%d" % wp)
                        for kc in range(16):
                            E("tensor", lambda e, reg=reg, kc=kc, tsub=tsub,
                              srcT=srcT, wbuf=wbuf:
                              e.matmul(reg,
                                       srcT[:, kc * TC + tsub * 128:
                                            kc * TC + tsub * 128 + 128],
                                       wbuf[:, kc * 512:(kc + 1) * 512],
                                       start=(kc == 0), stop=False))
                        E("tensor", lambda e, reg=reg, dq=dq, boff=boff:
                          e.matmul(reg, onesb[boff:boff + 1, 0:128],
                                   brow[boff:boff + 1, dq * 512:(dq + 1) * 512],
                                   start=False, stop=True))
                        t_pe = T("tensor", "sPE")
                        W("vector", "sPE", t_pe)
                        FE("vector", "sAq%d" % o2)
                        E("vector", lambda e, o2=o2, reg=reg: e.tensor_copy(
                            qe[:, o2 * 1024: o2 * 1024 + 512], reg))
                        t_ev = T("vector", "sDV")
                        pB_free[o2] = t_ev
                        W("scalar", "sDV", t_ev)
                        D("scalar", "sAq%d" % o2,
                          dst_d[:, tsub * 2048 + dq * 512:
                                tsub * 2048 + (dq + 1) * 512],
                          qe[:, o2 * 1024: o2 * 1024 + 512])
                    pump(2)
                for kc in range(16):
                    tm_wfree[wp][kc] = P.ctr["sPE"].v

        emit_tm(wk_in, 0, 32, kt_d, akt_d)
        emit_tm(wv_in, 64, 32, v_d, av_d, av_reload=True)

        qT_v = qT_d.rearrange("p (c t) -> p c t", t=TC)
        kT_v = kT_d.rearrange("p (c t) -> p c t", t=TC)

        pA_free = [pB_free[0]] * 4
        upd_free[0] = pB_free[0]
        scan_pe_free = P.ctr["sPE"].v
        scan_dve_free = P.ctr["sDV"].v
        last_upd = P.ctr["sDV"].v

        for phase, src_kt, src_v in (("a", akt_d, av_d), ("m", kt_d, v_d)):
            for c in range(NCH):
                FE("sync", "sAq0")
                FE("sync", "sAq1")
                W("sync", "sPE", scan_pe_free)
                W("sync", "sDV", last_upd)
                t_kt = D("sync", "sLx", wbig[:, 0:4096],
                         src_kt[:, (2 * c) * 2048:(2 * c + 2) * 2048])
                t_v = D("sync", "sLx", wbig[:, 4096:8192],
                        src_v[:, (2 * c) * 2048:(2 * c + 2) * 2048])
                t_q = t_k2 = None
                if phase == "m":
                    t_q = D("sync", "sLx", gu_sb[0][:, 0:4096].rearrange(
                        "p (k t) -> p k t", t=256),
                        qT_v[:, :, c * 256:(c + 1) * 256])
                    t_k2 = D("sync", "sLx", gu_sb[0][:, 4096:8192].rearrange(
                        "p (k t) -> p k t", t=256),
                        kT_v[:, :, c * 256:(c + 1) * 256])
                pump(1)
                for h in range(HEADS):
                    hp = h % 2
                    cdh = float(CD[h])
                    if phase == "m":
                        W("vector", "sDV", last_upd)
                        E("vector", lambda e, h=h: e.tensor_copy(
                            Sbf[:, :], Ssb[:, h * 512:(h + 1) * 512]))
                        t_sbf = T("vector", "sDV")
                        W("scalar", "sDV", mask_free[0])
                        for jc in range(2):
                            E("scalar", lambda e, h=h, jc=jc: e.activation(
                                dtt[:, jc * 256:(jc + 1) * 256],
                                dmsk[:, jc * 256:(jc + 1) * 256], AF.Exp,
                                scale=float(LNG[h])))
                        t_dt = T("scalar", "sAC")
                        FE("tensor", "sLx")
                        W("tensor", "sDV", pA_free[hp])
                        for jc in range(2):
                            for dc in range(2):
                                E("tensor", lambda e, h=h, jc=jc, dc=dc, hp=hp:
                                  e.matmul(pA[hp][:, jc * 256:(jc + 1) * 256],
                                           gu_sb[0][:, 4096 + (2 * h + dc) * 256 +
                                                 jc * 128:
                                                 4096 + (2 * h + dc) * 256 +
                                                 jc * 128 + 128],
                                           gu_sb[0][:, (2 * h + dc) * 256:
                                                 (2 * h + dc) * 256 + 256],
                                           start=(dc == 0), stop=(dc == 1)))
                        t_at = T("tensor", "sPE")
                        W("vector", "sPE", t_at)
                        W("vector", "sPE", attn_free[0])
                        W("vector", "sAC", t_dt)
                        for jc in range(2):
                            E("vector", lambda e, jc=jc, hp=hp: e.tensor_tensor(
                                attnM[:, jc * 256:(jc + 1) * 256],
                                pA[hp][:, jc * 256:(jc + 1) * 256],
                                dtt[:, jc * 256:(jc + 1) * 256], ALU.mult))
                            E("vector", lambda e, h=h, jc=jc: e.tensor_tensor(
                                qs_sb[:, jc * 256:(jc + 1) * 256],
                                gu_sb[0][:, (2 * h + jc) * 256:
                                      (2 * h + jc) * 256 + 256],
                                csb[:, h * 256:(h + 1) * 256], ALU.mult))
                        t_mq = T("vector", "sDV")
                        pA_free[hp] = t_mq
                        mask_free[0] = t_mq
                        W("tensor", "sDV", t_mq)
                        W("tensor", "sDV", t_sbf)
                        W("tensor", "sDV", pA_free[2 + hp])
                        for ih in range(2):
                            for jc in range(2):
                                E("tensor", lambda e, ih=ih, jc=jc, h=h, hp=hp:
                                  e.matmul(pA[2 + hp][:, ih * 256:(ih + 1) * 256],
                                           attnM[:, jc * 256 + ih * 128:
                                                 jc * 256 + ih * 128 + 128],
                                           wbig[:, 4096 + jc * 2048 + h * 256:
                                                4096 + jc * 2048 + h * 256 + 256],
                                           start=(jc == 0), stop=False))
                            for dc in range(2):
                                E("tensor", lambda e, ih=ih, dc=dc, hp=hp:
                                  e.matmul(pA[2 + hp][:, ih * 256:(ih + 1) * 256],
                                           qs_sb[:, dc * 256 + ih * 128:
                                                 dc * 256 + ih * 128 + 128],
                                           Sbf[:, dc * 256:(dc + 1) * 256],
                                           start=False, stop=(dc == 1)))
                        t_out = T("tensor", "sPE")
                        attn_free[0] = t_out
                        W("vector", "sPE", t_out)
                        W("vector", "sAs", s_spill_tok[0])
                        for ih in range(2):
                            E("vector", lambda e, ih=ih, h=h, hp=hp:
                              e.tensor_copy(
                                  scr[:, ih * 2048 + h * 256:
                                      ih * 2048 + h * 256 + 256],
                                  pA[2 + hp][:, ih * 256:(ih + 1) * 256]))
                        t_sev = T("vector", "sDV")
                        pA_free[2 + hp] = t_sev
                    FE("vector", "sLx")
                    W("vector", "sPE", ks_free[0])
                    for tsl in range(2):
                        E("vector", lambda e: e.nop())
                        E("vector", lambda e, tsl=tsl, h=h: e.tensor_scalar(
                            ksb[:, tsl * 256:(tsl + 1) * 256],
                            wbig[:, tsl * 2048 + h * 256:
                                 tsl * 2048 + h * 256 + 256],
                            kvc[:, (tsl * 8 + h):(tsl * 8 + h) + 1], None,
                            ALU.mult))
                    t_ks = T("vector", "sDV")
                    W("tensor", "sDV", t_ks)
                    W("tensor", "sDV", upd_free[0])
                    FE("tensor", "sLx")
                    for dh in range(2):
                        for tsl in range(2):
                            E("tensor", lambda e, dh=dh, tsl=tsl, h=h:
                              e.matmul(p_upd[:, dh * 256:(dh + 1) * 256],
                                       ksb[:, tsl * 256 + dh * 128:
                                           tsl * 256 + dh * 128 + 128],
                                       wbig[:, 4096 + tsl * 2048 + h * 256:
                                            4096 + tsl * 2048 + h * 256 + 256],
                                       start=(tsl == 0), stop=(tsl == 1)))
                    t_upd = T("tensor", "sPE")
                    ks_free[0] = t_upd
                    W("vector", "sPE", t_upd)
                    for dh in range(2):
                        E("vector", lambda e: e.nop())
                        E("vector", lambda e, dh=dh, h=h, cdh=cdh:
                          e.scalar_tensor_tensor(
                              Ssb[:, h * 512 + dh * 256:
                                  h * 512 + dh * 256 + 256],
                              Ssb[:, h * 512 + dh * 256:
                                  h * 512 + dh * 256 + 256],
                              cdh, p_upd[:, dh * 256:(dh + 1) * 256],
                              ALU.mult, ALU.add))
                    last_upd = T("vector", "sDV")
                    upd_free[0] = last_upd
                if phase == "m":
                    W("scalar", "sDV", last_upd)
                    s_spill_tok[0] = D("scalar", "sAs",
                                       s_d[:, (2 * c) * 2048:(2 * c + 2) * 2048],
                                       scr[:, 0:4096])
                scan_pe_free = P.ctr["sPE"].v
                pump(2)

        Gg_v = Gg.rearrange("(c p) n -> p c n", p=128)
        Gu_v = Gu.rearrange("(c p) n -> p c n", p=128)
        chain_tok = {}
        gu_ld_free = [P.ctr["sPE"].v, P.ctr["sPE"].v]
        pGU_free = [P.ctr["sDV"].v] * 4
        for i in range(4):
            fls_dve[i] = P.ctr["sDV"].v
        gelu_free[0] = gelu_free[1] = P.ctr["sDV"].v
        wd_free = P.ctr["sPE"].v
        gu_tok = (0, 0)

        for g in range(8):
            assert len(stage_tok[("g", g)]) == 16, (g, len(stage_tok[("g", g)]))
            assert len(stage_tok[("u", g)]) == 16
            tok_gu_staged = max(stage_tok[("g", g)][-1], stage_tok[("u", g)][-1])
            for i in range(16):
                pump(2)
                sl = (i // 2) % 2
                if i % 2 == 0:
                    W("sync", "sAs", tok_gu_staged)
                    W("sync", "sPE", gu_ld_free[sl])
                    ncp0 = (16 * g + i) * 128
                    t_gg = D("sync", "sLg%d" % sl, gu_sb[sl][:, 0:16 * 256].rearrange(
                        "p (c n) -> p c n", n=256), Gg_v[:, :, ncp0:ncp0 + 256])
                    t_gu = D("sync", "sLg%d" % sl,
                             gu_sb[sl][:, 16 * 256:32 * 256].rearrange(
                                 "p (c n) -> p c n", n=256),
                             Gu_v[:, :, ncp0:ncp0 + 256])
                    gu_tok = (t_gg, t_gu)
                io = i % 2
                for th in range(2):
                    FE("tensor", "sLg%d" % sl)
                    W("tensor", "sDV", pGU_free[th])
                    W("tensor", "sDV", pGU_free[2 + th])
                    for kc in range(16):
                        E("tensor", lambda e, sl=sl, kc=kc, io=io, th=th:
                          e.matmul(pA[th][:, 0:512],
                                   gu_sb[sl][:, kc * 256 + io * 128:
                                             kc * 256 + io * 128 + 128],
                                   xnT[:, kc * TC + th * 512:
                                       kc * TC + (th + 1) * 512],
                                   start=(kc == 0), stop=(kc == 15)))
                    t_g = T("tensor", "sPE")
                    for kc in range(16):
                        E("tensor", lambda e, sl=sl, kc=kc, io=io, th=th:
                          e.matmul(pA[2 + th][:, 0:512],
                                   gu_sb[sl][:, 16 * 256 + kc * 256 + io * 128:
                                             16 * 256 + kc * 256 + io * 128 + 128],
                                   xnT[:, kc * TC + th * 512:
                                       kc * TC + (th + 1) * 512],
                                   start=(kc == 0), stop=(kc == 15)))
                    t_u = T("tensor", "sPE")
                    W("scalar", "sPE", t_g)
                    W("scalar", "sDV", gelu_free[th])
                    E("scalar", lambda e, th=th: e.activation(
                        scr2[:, 1024 + th * 512: 1024 + (th + 1) * 512],
                        pA[th][:, 0:512], AF.Gelu_apprx_tanh))
                    t_gelu = T("scalar", "sAC")
                    W("vector", "sAC", t_gelu)
                    W("vector", "sPE", t_u)
                    E("vector", lambda e, i=i, th=th: e.tensor_tensor(
                        itr[:, i * 1024 + th * 512: i * 1024 + (th + 1) * 512],
                        scr2[:, 1024 + th * 512: 1024 + (th + 1) * 512],
                        pA[2 + th][:, 0:512], ALU.mult))
                    t_ch = T("vector", "sDV")
                    chain_tok[(i, th)] = t_ch
                    gelu_free[th] = t_ch
                    pGU_free[th] = t_ch
                    pGU_free[2 + th] = t_ch
                gu_ld_free[sl] = P.ctr["sPE"].v
            assert len(stage_tok[("d", g)]) == 16
            tok_d_staged = stage_tok[("d", g)][-1]
            for q in range(4):
                pump(2)
                W("sync", "sPE", wd_free)
                W("sync", "sAs", tok_d_staged)
                for i in range(16):
                    r0 = (16 * g + i) * 128
                    D("sync", "sLw", wbig[:, i * 512:(i + 1) * 512],
                      Gd[r0:r0 + 128, q * 512:(q + 1) * 512])
                for ts_ in range(8):
                    o4 = ts_ % 4
                    reg = pB[o4 // 2][:, (o4 % 2) * 512:(o4 % 2) * 512 + 512]
                    W("tensor", "sDV", fls_dve[o4])
                    if ts_ == 0:
                        FE("tensor", "sLw")
                    for i in range(16):
                        if q == 0 and ts_ == 0:
                            W("tensor", "sDV", chain_tok[(i, 0)])
                            W("tensor", "sDV", chain_tok[(i, 1)])
                        E("tensor", lambda e, reg=reg, i=i, ts_=ts_:
                          e.matmul(reg,
                                   itr[:, i * 1024 + ts_ * 128:
                                       i * 1024 + ts_ * 128 + 128],
                                   wbig[:, i * 512:(i + 1) * 512],
                                   start=(i == 0), stop=(i == 15)))
                    t_dn = T("tensor", "sPE")
                    W("vector", "sPE", t_dn)
                    o2 = ts_ % 2
                    FE("vector", "sGa%d" % o2)
                    E("vector", lambda e, o2=o2, reg=reg: e.tensor_copy(
                        fls[:, o2 * 512:(o2 + 1) * 512], reg))
                    t_fl = T("vector", "sDV")
                    fls_dve[o4] = t_fl
                    W("gpsimd", "sDV", t_fl)
                    kw = {} if g == 0 else {"accum_op": ALU.add}
                    D("gpsimd", "sGa%d" % o2,
                      acc_d[:, ts_ * 2048 + q * 512:
                            ts_ * 2048 + q * 512 + 512],
                      fls[:, o2 * 512:(o2 + 1) * 512], **kw)
                wd_free = P.ctr["sPE"].v

        W("sync", "sDV", P.ctr["sDV"].v)
        FE("sync", "sGa0")
        FE("sync", "sGa1")
        FE("sync", "sAs")
        cmb_free = 0
        last_store = 0
        for ts_ in range(8):
            W("sync", "sDV", cmb_free)
            FE("sync", "sLo")
            t_x = D("sync", "sLx", scr[:, 0:2048],
                    x_in[ts_ * 128:(ts_ + 1) * 128, :])
            t_s = D("sync", "sLx", scr[:, 2048:4096],
                    s_d[:, ts_ * 2048:(ts_ + 1) * 2048])
            atile = scr2[:, 0:2048]
            t_a = D("sync", "sLx", atile, acc_d[:, ts_ * 2048:(ts_ + 1) * 2048])
            FE("scalar", "sLx")
            W("scalar", "sDV", cmb_free)
            E("scalar", lambda e, a=atile: e.activation(a, a, AF.Silu))
            t_sl = T("scalar", "sAC")
            W("vector", "sAC", t_sl)
            FE("vector", "sLx")
            E("vector", lambda e, a=atile: e.tensor_tensor(
                scr[:, 2048:4096], scr[:, 2048:4096], a, ALU.mult))
            t_sg = T("vector", "sDV")
            W("vector", "sDV", t_sg)
            E("vector", lambda e: e.tensor_tensor(
                scr[:, 0:2048], scr[:, 0:2048], scr[:, 2048:4096], ALU.add))
            t_cb = T("vector", "sDV")
            cmb_free = t_cb
            W("sync", "sDV", t_cb)
            last_store = D("sync", "sLo", out_t[ts_ * 128:(ts_ + 1) * 128, :],
                           scr[:, 0:2048])

        dbg_reg = {"xnT_t": None, "qT_d": qT_d, "kT_d": kT_d, "kt_d": kt_d,
                   "v_d": v_d, "akt_d": akt_d, "av_d": av_d, "s_d": s_d,
                   "acc_d": acc_d, "Gg": Gg, "Gu": Gu, "Gd": Gd,
                   "gsh_d": gsh_d, "dsh_d": dsh_d}
        if dbg:
            FE("sync", "sAs")
            FE("sync", "sGa0")
            FE("sync", "sGa1")
            W("sync", "sDV", P.ctr["sDV"].v)
            for name in dbg:
                if name in ("xnT_e", "xnbf0"):
                    continue
                if name == "xnT_t":
                    o = nc.dram_tensor("dbg_xnT", [128, 16 * TC], BF16,
                                       kind="ExternalOutput").ap()
                    last_store = D("sync", "sLo", o, xnT[:, :])
                    continue
                src = dbg_reg[name]
                o = nc.dram_tensor("dbg_" + name, list(src.shape),
                                   src.dtype, kind="ExternalOutput").ap()
                last_store = D("sync", "sLo", o, src)
            FE("sync", "sLo")

        FE("sync", "sLo")
        FE("gpsimd", "sLo")

        assert dec_pos[0] == len(dec_units), (dec_pos[0], len(dec_units))

        @block.tensor
        def _(e):
            for f in P.ops["tensor"]:
                f(e)

        @block.vector
        def _(e):
            for f in P.ops["vector"]:
                f(e)

        @block.scalar
        def _(e):
            for f in P.ops["scalar"]:
                f(e)

        @block.gpsimd
        def _(e):
            for f in P.ops["gpsimd"]:
                f(e)

        @block.sync
        def _(e):
            for f in P.ops["sync"]:
                f(e)

    return nc


def _host_prepack(w_int32):
    b = np.ascontiguousarray(w_int32).view(np.uint8)
    b = b.reshape(w_int32.shape[0], w_int32.shape[1], 4)[:, :, 0]
    bT = np.ascontiguousarray(b.T)
    n = bT.shape[1]
    il = np.stack([np.arange(n // 2), np.arange(n // 2) + n // 2], 1).reshape(-1)
    bTi = np.ascontiguousarray(bT[:, il])
    return bTi.view(np.int16)


_CACHE = {}


def kernel(L0_attn_x, L0_attn_Wq, L0_attn_bq, L0_attn_Wk, L0_attn_bk,
           L0_attn_Wv, L0_attn_bv, L0_ffn_gate_W_packed, L0_ffn_gate_scales,
           L0_ffn_up_W_packed, L0_ffn_up_scales, L0_ffn_down_W_packed,
           L0_ffn_down_scales):
    x = np.ascontiguousarray(np.asarray(L0_attn_x, dtype=np.float32)).reshape(8192, H)
    wq = np.ascontiguousarray(np.asarray(L0_attn_Wq, dtype=np.float32))
    wk = np.ascontiguousarray(np.asarray(L0_attn_Wk, dtype=np.float32))
    wv = np.ascontiguousarray(np.asarray(L0_attn_Wv, dtype=np.float32))
    bq = np.ascontiguousarray(np.asarray(L0_attn_bq, dtype=np.float32)).reshape(1, H)
    bk = np.ascontiguousarray(np.asarray(L0_attn_bk, dtype=np.float32)).reshape(1, H)
    bv = np.ascontiguousarray(np.asarray(L0_attn_bv, dtype=np.float32)).reshape(1, H)
    gp = _host_prepack(np.ascontiguousarray(np.asarray(L0_ffn_gate_W_packed, dtype=np.int32)))
    up = _host_prepack(np.ascontiguousarray(np.asarray(L0_ffn_up_W_packed, dtype=np.int32)))
    dp = _host_prepack(np.ascontiguousarray(np.asarray(L0_ffn_down_W_packed, dtype=np.int32)))
    gs = np.ascontiguousarray(np.asarray(L0_ffn_gate_scales, dtype=np.float32).T)
    us = np.ascontiguousarray(np.asarray(L0_ffn_up_scales, dtype=np.float32).T)
    ds = np.ascontiguousarray(np.asarray(L0_ffn_down_scales, dtype=np.float32).T)

    z_x = np.zeros((TC, H), np.float32)
    z_b = np.zeros((1, H), np.float32)

    key = tuple(DBG)
    if key not in _CACHE:
        _CACHE[key] = build(dbg=key)
    nc = _CACHE[key]

    in_maps = []
    for r in range(NCORES):
        odd = r % 2 == 1
        in_maps.append({
            "x": x[r * TC:(r + 1) * TC],
            "xa": x[(r - 1) * TC: r * TC] if odd else z_x,
            "wq": wq, "wk": wk, "wv": wv,
            "bq": bq, "bk": bk, "bv": bv,
            "bak": bk if odd else z_b, "bav": bv if odd else z_b,
            "gatep": gp, "upp": up, "downp": dp,
            "gates": gs, "ups": us, "downs": ds,
        })

    res = run_bass_kernel_spmd(nc, in_maps, core_ids=list(range(NCORES)))
    _LAST.clear()
    _LAST.update({"results": res.results})
    out = np.concatenate([res.results[r]["out"] for r in range(NCORES)], axis=0)
    return out.reshape(4, 2048, H)


if __name__ == "__main__":
    nc = build()
    n = sum(1 for _ in nc.all_instructions())
    print("built ok; instructions:", n)


# version 12
# speedup vs baseline: 1.7063x; 1.1469x over previous
from contextlib import ExitStack

import numpy as np

import concourse.bass as bass
import concourse.mybir as mybir
from concourse.bass_utils import run_bass_kernel_spmd

F32 = mybir.dt.float32
BF16 = mybir.dt.bfloat16
I16 = mybir.dt.int16
AF = mybir.ActivationFunctionType
ALU = mybir.AluOpType
AX = mybir.AxisListType

NCORES = 8
H = 2048
TC = 1024
HEADS = 8
CHUNK = 256
NCH = 4
MLP = 16384
EPS = 1e-5

_BETAS = np.linspace(-5.0, -9.0, 8).astype(np.float32).astype(np.float64)
GAMMA = 1.0 - np.exp2(_BETAS)
LNG = np.log(GAMMA)
CD = GAMMA ** CHUNK


class Ctr:
    __slots__ = ("v",)

    def __init__(self):
        self.v = 0

    def inc(self, d):
        self.v += d
        return self.v


class Prog:
    ENGINES = ("tensor", "vector", "scalar", "gpsimd", "sync")

    def __init__(self):
        self.ops = {e: [] for e in self.ENGINES}
        self.sem = {}
        self.ctr = {}

    def add_sem(self, name, handle):
        self.sem[name] = handle
        self.ctr[name] = Ctr()

    def emit(self, eng, fn):
        self.ops[eng].append(fn)

    def wait(self, eng, sem_name, val):
        if val is None or val <= 0:
            return
        s = self.sem[sem_name]
        self.emit(eng, lambda e, s=s, v=val: e.wait_ge(s, v))
        self.emit(eng, lambda e: e.nop(nofuse=True))

    def fence(self, eng, sem_name):
        self.wait(eng, sem_name, self.ctr[sem_name].v)

    def dma(self, eng, sem_name, out, in_, **kw):
        s = self.sem[sem_name]
        tok = self.ctr[sem_name].inc(16)
        self.emit(eng, lambda e, s=s, o=out, i=in_, kw=kw:
                  e.dma_start(out=o, in_=i, **kw).then_inc(s, 16))
        return tok

    def tag(self, eng, sem_name):
        s = self.sem[sem_name]
        tok = self.ctr[sem_name].inc(1)
        prev = self.ops[eng].pop()
        self.emit(eng, lambda e, p=prev, s=s: p(e).then_inc(s, 1))
        return tok


DBG = []
_LAST = {}


def build(dbg=()):
    nc = bass.Bass("TRN2", target_bir_lowering=False)

    def din(name, shape, dt):
        return nc.dram_tensor(name, list(shape), dt, kind="ExternalInput").ap()

    x_in = din("x", (TC, H), F32)
    xa_in = din("xa", (TC, H), F32)
    wq_in = din("wq", (H, H), F32)
    wk_in = din("wk", (H, H), F32)
    wv_in = din("wv", (H, H), F32)
    b_ins = {"q": din("bq", (1, H), F32), "k": din("bk", (1, H), F32),
             "v": din("bv", (1, H), F32), "ak": din("bak", (1, H), F32),
             "av": din("bav", (1, H), F32)}
    gp_in = din("gatep", (1024, 8192), I16)
    up_in = din("upp", (1024, 8192), I16)
    dp_in = din("downp", (8192, 1024), I16)
    gs_in = din("gates", (64, MLP), F32)
    us_in = din("ups", (64, MLP), F32)
    ds_in = din("downs", (512, H), F32)

    out_t = nc.dram_tensor("out", [TC, H], F32, kind="ExternalOutput").ap()

    def dint(name, shape, dt):
        return nc.dram_tensor(name, list(shape), dt).ap()

    qT_d = dint("qT_d", (128, 16 * TC), BF16)
    kT_d = dint("kT_d", (128, 16 * TC), BF16)
    kt_d = dint("kt_d", (128, 8 * H), BF16)
    v_d = dint("v_d", (128, 8 * H), BF16)
    akt_d = dint("akt_d", (128, 8 * H), BF16)
    av_d = dint("av_d", (128, 8 * H), BF16)
    s_d = dint("s_d", (128, 8 * H), F32)
    acc_d = dint("acc_d", (128, 8 * H), F32)
    Gg = dint("Gg", (H, MLP), BF16)
    Gu = dint("Gu", (H, MLP), BF16)
    Gd = dint("Gd", (MLP, H), BF16)
    gsh_d = dint("gsh_d", (64, MLP), BF16)
    ush_d = dint("ush_d", (64, MLP), BF16)
    dsh_d = dint("dsh_d", (512, H), BF16)

    with ExitStack() as ctx:
        sbt = lambda n, s, d: ctx.enter_context(nc.sbuf_tensor(n, list(s), d))
        pst = lambda n, s, d: ctx.enter_context(nc.psum_tensor(n, list(s), d))

        xnT = sbt("xnT", (128, 16 * TC), BF16)
        wbig = sbt("wbig", (128, 8 * 1024), BF16)
        scr = sbt("scr", (128, 4096), F32)
        scr2 = sbt("scr2", (128, 2048), F32)
        pk_sb = sbt("pk_sb", (128, 2 * 1024), I16)
        dreg = [sbt(f"dreg{i}", (128, 1024), I16) for i in range(8)]
        slab = sbt("slab_sb", (128, 4 * 1024), BF16)
        srep = sbt("srep_sb", (128, 2 * 1024), BF16)
        qe = sbt("qe", (128, 2 * 1024), BF16)
        Ssb = sbt("Ssb", (128, 16 * 256), F32)
        Sbf = sbt("Sbf", (128, 2 * 256), BF16)
        ksb = sbt("ksb", (128, 2 * 256), BF16)
        attnM = sbt("attnM", (128, 2 * 256), BF16)
        qs_sb = sbt("qs_sb", (128, 2 * 256), BF16)
        gu_sb = [sbt(f"gu{i}", (128, 16 * 512), BF16) for i in range(2)]
        itr = sbt("itr", (128, 16 * 1024), BF16)
        xanT = itr
        cons = sbt("cons", (128, 8), F32)
        csb = sbt("csb", (128, 8 * 256), BF16)
        kvc = sbt("kvc", (128, 16), F32)
        dmsk = sbt("dmsk", (128, 2 * 256), F32)
        ident = sbt("ident", (128, 128), BF16)
        onesb = sbt("onesb", (128, 512), BF16)
        brow = sbt("brow", (128, H), BF16)
        iop = sbt("iop", (128, 2), F32)
        lst = sbt("lst", (128, 16), F32)
        fls = sbt("fls", (128, 2 * 512), F32)

        dtt = scr2[:, 0:512]
        ioc = scr2[:, 512:768]

        pA = [pst(f"pA{i}", (128, 512), F32) for i in range(4)]
        pB = [pst(f"pB{i}", (128, 1024), F32) for i in range(2)]
        p_upd = pB[0][:, 0:512]

        P = Prog()
        for s in ("sLx", "sLp0", "sLp1", "sGs0", "sGs1", "sLg0", "sLg1",
                  "sLw", "sGw0", "sGw1", "sGb", "sAq0", "sAq1", "sAs",
                  "sGa0", "sGa1", "sLo",
                  "sPE", "sDV", "sAC", "sGP"):
            P.add_sem(s, ctx.enter_context(nc.semaphore(s)))

        block = ctx.enter_context(nc.Block())
        E, W, D, T, FE = P.emit, P.wait, P.dma, P.tag, P.fence

        mask_free = [0]
        s_spill_tok = [0]
        attn_free = [0]
        ks_free = [0]
        upd_free = [0]
        gelu_free = [0, 0]
        fls_dve = [0, 0, 0, 0]
        fls_gd = [0, 0]

        for col, val in {0: -4.0, 1: -16.0, 2: 1.0, 3: EPS}.items():
            E("gpsimd", lambda e, c=col, v=val: e.memset(cons[:, c:c + 1], v))
        E("gpsimd", lambda e: e.memset(onesb[0:1, :], 1.0))
        E("gpsimd", lambda e: e.memset(onesb[32:33, :], 1.0))
        E("gpsimd", lambda e: e.memset(onesb[64:65, :], 1.0))
        E("gpsimd", lambda e: e.memset(Ssb[:, :], 0.0))
        E("gpsimd", lambda e: e.iota(ioc, pattern=[[1, 256]], base=0,
                                     channel_multiplier=0,
                                     allow_small_or_imprecise_dtypes=True))
        E("gpsimd", lambda e: e.iota(iop[:, 0:1], pattern=[[0, 1]], base=0,
                                     channel_multiplier=1,
                                     allow_small_or_imprecise_dtypes=True))
        for jc in range(2):
            E("gpsimd", lambda e, jc=jc: e.iota(
                dmsk[:, jc * 256:(jc + 1) * 256], pattern=[[1, 256]],
                base=-128 * jc, channel_multiplier=-1,
                allow_small_or_imprecise_dtypes=True))
            t_io = T("gpsimd", "sGP")
            W("gpsimd", "sGP", t_io)
            E("gpsimd", lambda e, jc=jc: e.affine_select(
                dmsk[:, jc * 256:(jc + 1) * 256], dmsk[:, jc * 256:(jc + 1) * 256],
                pattern=[[1, 256]], compare_op=ALU.is_ge, fill=1.0e9,
                base=-128 * jc, channel_multiplier=-1))
        E("gpsimd", lambda e: e.iota(scr2[:, 768:896], pattern=[[1, 128]], base=0,
                                     channel_multiplier=-1,
                                     allow_small_or_imprecise_dtypes=True))
        tok_gp0 = T("gpsimd", "sGP")

        W("vector", "sGP", tok_gp0)
        E("vector", lambda e: e.tensor_scalar(scr2[:, 896:1024], scr2[:, 768:896],
                                              0.0, None, ALU.is_equal))
        t_ieq = T("vector", "sDV")
        W("vector", "sDV", t_ieq)
        E("vector", lambda e: e.tensor_copy(ident[:, :], scr2[:, 896:1024]))
        tok_ident = T("vector", "sDV")

        W("scalar", "sGP", tok_gp0)
        for h in range(HEADS):
            E("scalar", lambda e, h=h: e.activation(
                csb[:, h * 256:(h + 1) * 256], ioc, AF.Exp, scale=float(LNG[h])))
        for tsl in range(2):
            for h in range(HEADS):
                col = tsl * 8 + h
                E("scalar", lambda e, h=h, col=col: e.activation(
                    kvc[:, col:col + 1], iop[:, 0:1], AF.Exp, scale=-float(LNG[h])))
        tok_exp = T("scalar", "sAC")
        W("vector", "sAC", tok_exp)
        for h in range(HEADS):
            E("vector", lambda e, h=h: e.tensor_scalar(
                csb[:, h * 256:(h + 1) * 256], csb[:, h * 256:(h + 1) * 256],
                float(GAMMA[h]), None, ALU.mult))
            E("vector", lambda e: e.nop())
        for tsl in range(2):
            for h in range(HEADS):
                col = tsl * 8 + h
                mv = float(GAMMA[h] ** (255 - 128 * tsl))
                E("vector", lambda e: e.nop())
                E("vector", lambda e, col=col, mv=mv: e.tensor_scalar(
                    kvc[:, col:col + 1], kvc[:, col:col + 1], mv, None, ALU.mult))
        T("vector", "sDV")

        last_sc_st = 0
        sc_chunks = []
        for src, dst, rows, cols in ((gs_in, gsh_d, 64, MLP),
                                     (us_in, ush_d, 64, MLP),
                                     (ds_in, dsh_d, 512, H)):
            for r0 in range(0, rows, 64):
                rr = min(64, rows - r0)
                for c0 in range(0, cols, 4096):
                    cc = min(4096, cols - c0)
                    sc_chunks.append((src, dst, r0, rr, c0, cc))
        sc_st_tok = [0, 0]
        sc_ld_tok = [0, 0]
        for ci, (src, dst, r0, rr, c0, cc) in enumerate(sc_chunks):
            pp = ci % 2
            rb = pp * 64
            W("sync", "sDV", sc_st_tok[pp])
            t_ld = D("sync", "sLx", scr[rb:rb + rr, 0:cc],
                     src[r0:r0 + rr, c0:c0 + cc])
            W("vector", "sLx", t_ld)
            W("vector", "sAs", sc_ld_tok[pp])
            obuf = (qe[rb:rb + rr, 0:cc] if cc <= 2048 else
                    wbig[0:rr, pp * 4096: pp * 4096 + cc])
            E("vector", lambda e, rr=rr, cc=cc, rb=rb, obuf=obuf: e.tensor_scalar(
                obuf, scr[rb:rb + rr, 0:cc], 0.5, None, ALU.mult))
            t_h = T("vector", "sDV")
            sc_st_tok[pp] = t_h
            E("vector", lambda e: e.nop())
            W("scalar", "sDV", t_h)
            last_sc_st = D("scalar", "sAs", dst[r0:r0 + rr, c0:c0 + cc], obuf)
            sc_ld_tok[pp] = last_sc_st
        tok_scales_st = last_sc_st
        tok_scales_dve = P.ctr["sDV"].v

        def load_bias(name, wait_pe=None):
            if wait_pe:
                W("gpsimd", "sPE", wait_pe)
            return D("gpsimd", "sGb", brow[0:1, 0:H], b_ins[name])

        gsh_aps = {"g": gsh_d, "u": ush_d}
        pk_aps = {"g": gp_in, "u": up_in}
        G_aps = {"g": Gg, "u": Gu}
        stage_tok = {(m, g): [] for m in ("g", "u", "d") for g in range(8)}
        ds_pk_free = [0, 0]
        ds_srep_free = [0, 0]
        slab_st_tok = [0, 0]
        ds_slot = [0]
        FD = 1024

        def emit_chain_pair(planes, pk_ap):
            tok_m_pl = []
            for j, (sh, _, _) in enumerate(planes):
                r = dreg[4 * j]
                if sh:
                    E("vector", lambda e, r=r, sh=sh: e.tensor_scalar(
                        r[:, :], pk_ap, sh, 15,
                        ALU.logical_shift_right, ALU.bitwise_and))
                else:
                    E("vector", lambda e, r=r: e.tensor_scalar(
                        r[:, :], pk_ap, 15, None, ALU.bitwise_and))
            for j in range(len(planes)):
                c0, m = dreg[4 * j], dreg[4 * j + 1]
                E("vector", lambda e, c0=c0, m=m: e.tensor_scalar(
                    m[:, :], c0[:, :], 7, None, ALU.bitwise_and))
                tok_m_pl.append(T("vector", "sDV"))
            tok_ab_pl = []
            for j in range(len(planes)):
                m, a, b = dreg[4 * j + 1], dreg[4 * j + 2], dreg[4 * j + 3]
                W("scalar", "sDV", tok_m_pl[j])
                E("scalar", lambda e, m=m, a=a: e.activation(
                    a[:, :], m[:, :], AF.Identity, scale=2.0, bias=cons[:, 0:1]))
                E("scalar", lambda e, m=m, b=b: e.activation(
                    b[:, :], m[:, :], AF.Identity, scale=4.0, bias=cons[:, 1:2]))
                tok_ab_pl.append(T("scalar", "sAC"))
            t_slab_pl = []
            for j, (_, slab_sl, srep_sl) in enumerate(planes):
                c0, m = dreg[4 * j], dreg[4 * j + 1]
                a, b = dreg[4 * j + 2], dreg[4 * j + 3]
                W("vector", "sAC", tok_ab_pl[j])
                E("vector", lambda e, a=a, b=b: e.tensor_tensor(
                    a[:, :], a[:, :], b[:, :], ALU.max))
                E("vector", lambda e, c0=c0, b=b: e.tensor_scalar(
                    b[:, :], c0[:, :], 3, 1,
                    ALU.logical_shift_right, ALU.bitwise_and))
                E("vector", lambda e, m=m, a=a: e.tensor_tensor(
                    a[:, :], m[:, :], a[:, :], ALU.max))
                E("vector", lambda e, b=b: e.tensor_scalar(
                    b[:, :], b[:, :], -2, 1, ALU.mult, ALU.add))
                E("vector", lambda e: e.nop(nofuse=True))
                E("vector", lambda e, a=a, b=b: e.tensor_tensor(
                    a[:, :], a[:, :], b[:, :], ALU.mult))
                E("vector", lambda e: e.nop(nofuse=True))
                E("vector", lambda e, a=a, o=slab_sl, sr=srep_sl:
                  e.tensor_tensor(o, a[:, :], sr, ALU.mult))
                t_slab_pl.append(T("vector", "sDV"))
            return tok_m_pl[0], t_slab_pl

        def emit_unit_gu(mat, g, c, ch):
            b = g // 4
            frq = 2048 * (g % 4) + 1024 * ch
            n0 = 2048 * g + 1024 * ch
            slot = ds_slot[0]
            ds_slot[0] ^= 1
            W("sync", "sDV", ds_pk_free[slot])
            pk_sl = pk_sb[:, slot * 1024: slot * 1024 + 1024]
            t_pk = D("sync", "sLp%d" % slot, pk_sl,
                     pk_aps[mat][c * 128:(c + 1) * 128, frq:frq + 1024])
            if ds_srep_free[slot] == 0:
                W("gpsimd", "sAs", tok_scales_st)
            W("gpsimd", "sDV", ds_srep_free[slot])
            sr_sl = srep[:, slot * 1024: slot * 1024 + 1024]
            t_sr = D("gpsimd", "sGs%d" % slot, sr_sl,
                     gsh_aps[mat][8 * c:8 * c + 8, n0:n0 + 1024]
                     .unsqueeze(1).broadcast_to([8, 16, 1024]))
            FE("vector", "sLp%d" % slot)
            FE("vector", "sGs%d" % slot)
            W("vector", "sAs", slab_st_tok[slot])
            sb0 = slab[:, slot * 2048: slot * 2048 + 1024]
            sb1 = slab[:, slot * 2048 + 1024: slot * 2048 + 2048]
            planes = [(8 * b, sb0, sr_sl), (8 * b + 4, sb1, sr_sl)]
            tok_pk, t_sl_pl = emit_chain_pair(planes, pk_sl)
            ds_pk_free[slot] = tok_pk
            ds_srep_free[slot] = t_sl_pl[-1]
            Gm = G_aps[mat]
            W("scalar", "sDV", t_sl_pl[0])
            D("scalar", "sAs", Gm[256 * c: 256 * c + 256: 2, n0:n0 + 1024], sb0)
            W("scalar", "sDV", t_sl_pl[1])
            t_st = D("scalar", "sAs",
                     Gm[256 * c + 1: 256 * c + 256: 2, n0:n0 + 1024], sb1)
            slab_st_tok[slot] = t_st
            stage_tok[(mat, g)].append(t_st)

        def emit_unit_down(g, cr, pairi):
            cp = 8 * g + cr
            slot = ds_slot[0]
            ds_slot[0] ^= 1
            W("sync", "sDV", ds_pk_free[slot])
            pk_sl = pk_sb[:, slot * 1024: slot * 1024 + 1024]
            t_pk = D("sync", "sLp%d" % slot, pk_sl,
                     dp_in[cp * 128:(cp + 1) * 128, 0:1024])
            if ds_srep_free[slot] == 0:
                W("gpsimd", "sAs", tok_scales_st)
            W("gpsimd", "sDV", ds_srep_free[slot])
            sr_sl = srep[:, slot * 1024: slot * 1024 + 1024]
            t_sr = D("gpsimd", "sGs%d" % slot, sr_sl,
                     dsh_d[8 * cp:8 * cp + 8, pairi * 1024:(pairi + 1) * 1024]
                     .unsqueeze(1).broadcast_to([8, 16, 1024]))
            FE("vector", "sLp%d" % slot)
            FE("vector", "sGs%d" % slot)
            W("vector", "sAs", slab_st_tok[slot])
            sb0 = slab[:, slot * 2048: slot * 2048 + 1024]
            sb1 = slab[:, slot * 2048 + 1024: slot * 2048 + 2048]
            planes = [(8 * pairi, sb0, sr_sl), (8 * pairi + 4, sb1, sr_sl)]
            tok_pk, t_sl_pl = emit_chain_pair(planes, pk_sl)
            ds_pk_free[slot] = tok_pk
            ds_srep_free[slot] = t_sl_pl[-1]
            t_last = 0
            for nib, sl, t_w in ((0, sb0, t_sl_pl[0]), (1, sb1, t_sl_pl[1])):
                W("scalar", "sDV", t_w)
                t_last = D("scalar", "sAs",
                           Gd[256 * cp + nib: 256 * cp + 256: 2,
                              pairi * 1024: (pairi + 1) * 1024], sl)
            slab_st_tok[slot] = t_last
            stage_tok[("d", g)].append(t_last)

        dec_units = []
        for g in range(8):
            for c in range(8):
                for ch in range(2):
                    dec_units.append(("g", g, c, ch))
                    dec_units.append(("u", g, c, ch))
            for cr in range(8):
                for pairi in range(2):
                    dec_units.append(("d", g, cr, pairi))
        dec_pos = [0]
        dec_on = [False]

        def pump(n):
            if not dec_on[0]:
                return
            for _ in range(n):
                if dec_pos[0] >= len(dec_units):
                    return
                kind, g, c, x2 = dec_units[dec_pos[0]]
                dec_pos[0] += 1
                if kind == "d":
                    emit_unit_down(g, c, x2)
                else:
                    emit_unit_gu(kind, g, c, x2)

        x_sl = scr[:, 0:2048]
        sq_sl = scr[:, 2048:4096]
        xnbf = wbig[:, 0:2048]
        slot_free = tok_scales_dve
        xnbf_pe_free = 0
        dec_on[0] = True
        FE("vector", "sAs")
        W("scalar", "sDV", tok_scales_dve)
        FE("scalar", "sAs")
        tr_evac_tok = {0: 0, 1: 0}
        tpsum = [pB[0][:, 0:64].bitcast(BF16), pB[1][:, 0:64].bitcast(BF16)]

        for kind, tt in [("m", t) for t in range(8)] + [("a", t) for t in range(8)]:
            src = x_in if kind == "m" else xa_in
            W("sync", "sDV", slot_free)
            t_ld = D("sync", "sLx", x_sl, src[tt * 128:(tt + 1) * 128, :])
            FE("vector", "sLx")
            FE("scalar", "sLx")
            E("vector", lambda e: e.tensor_reduce(lst[:, 0:1], x_sl, AX.X, ALU.add))
            t_s0 = T("vector", "sDV")
            E("scalar", lambda e: e.activation(sq_sl, x_sl, AF.Square))
            t_sq = T("scalar", "sAC")
            W("vector", "sAC", t_sq)
            E("vector", lambda e: e.tensor_reduce(lst[:, 1:2], sq_sl, AX.X, ALU.add))
            W("vector", "sDV", t_s0)
            E("vector", lambda e: e.tensor_scalar(lst[:, 2:3], lst[:, 0:1],
                                                  1.0 / H, None, ALU.mult))
            t_ms = T("vector", "sDV")
            W("scalar", "sDV", t_ms)
            E("scalar", lambda e: e.activation(lst[:, 3:4], lst[:, 1:2],
                                               AF.Identity, scale=1.0 / H))
            E("scalar", lambda e: e.activation(lst[:, 4:5], lst[:, 2:3],
                                               AF.Square))
            t_m2 = T("scalar", "sAC")
            W("vector", "sAC", t_m2)
            E("vector", lambda e: e.tensor_tensor(lst[:, 5:6], lst[:, 3:4],
                                                  lst[:, 4:5], ALU.subtract))
            t_var = T("vector", "sDV")
            W("scalar", "sDV", t_var)
            E("scalar", lambda e: e.activation(lst[:, 6:7], lst[:, 5:6],
                                               AF.Sqrt, bias=cons[:, 3:4]))
            t_std = T("scalar", "sAC")
            W("vector", "sAC", t_std)
            E("vector", lambda e: e.reciprocal(lst[:, 7:8], lst[:, 6:7]))
            t_rc = T("vector", "sDV")
            W("vector", "sDV", t_rc)
            W("vector", "sPE", xnbf_pe_free)
            E("vector", lambda e: e.tensor_scalar(xnbf, x_sl, lst[:, 2:3],
                                                  lst[:, 7:8], ALU.subtract,
                                                  ALU.mult))
            t_xn = T("vector", "sDV")
            slot_free = t_xn
            if kind == "m" and tt == 0 and "xnbf0" in dbg:
                dbg_x0 = nc.dram_tensor("dbg_x0", [128, 2048], F32,
                                        kind="ExternalOutput").ap()
                dbg_xb0 = nc.dram_tensor("dbg_xnbf0", [128, 2048], BF16,
                                         kind="ExternalOutput").ap()
                dbg_lst0 = nc.dram_tensor("dbg_lst0", [128, 16], F32,
                                          kind="ExternalOutput").ap()
                W("sync", "sDV", t_xn)
                tt_x0 = D("sync", "sLo", dbg_x0, x_sl)
                tt_xb = D("sync", "sLo", dbg_xb0, xnbf)
                tt_ls = D("sync", "sLo", dbg_lst0, lst[:, 0:16])
                FE("vector", "sLo")
                FE("tensor", "sLo")
            dstT = xnT if kind == "m" else xanT
            W("tensor", "sDV", t_xn)
            W("tensor", "sDV", tok_ident)
            for kc in range(16):
                pp = kc % 2
                W("tensor", "sDV", tr_evac_tok[pp])
                E("tensor", lambda e, kc=kc, pp=pp: e.transpose(
                    tpsum[pp], xnbf[:, kc * 128:(kc + 1) * 128], ident[:, :]))
                t_tr = T("tensor", "sPE")
                W("vector", "sPE", t_tr)
                E("vector", lambda e, dstT=dstT, kc=kc, tt=tt, pp=pp:
                  e.tensor_copy(
                      dstT[:, kc * TC + tt * 128: kc * TC + (tt + 1) * 128],
                      tpsum[pp]))
                tr_evac_tok[pp] = T("vector", "sDV")
            xnbf_pe_free = P.ctr["sPE"].v
            pump(2)
        xnT_ready = max(tr_evac_tok[0], tr_evac_tok[1])
        tok_p1_pe = P.ctr["sPE"].v

        if "xnT_e" in dbg:
            dbg_xe = nc.dram_tensor("dbg_xnT_e", [128, 16 * TC], BF16,
                                    kind="ExternalOutput").ap()
            W("sync", "sDV", xnT_ready)
            D("sync", "sLo", dbg_xe, xnT[:, :])
        FE("gpsimd", "sAs")
        dec_on[0] = True

        pB_free = [xnT_ready, xnT_ready]
        bias_tok = [0]
        FE("vector", "sAs")

        def emit_fm(w_ap, bias_slot, dst_d):
            wslot_free = [P.ctr["sPE"].v, P.ctr["sPE"].v]
            for ocg in range(8):
                ws = ocg % 2
                W("gpsimd", "sPE", wslot_free[ws])
                t_w = D("gpsimd", "sGw%d" % ws, gu_sb[ws][:, 0:16 * 256].rearrange(
                    "p (c n) -> p c n", n=256),
                    w_ap.rearrange("(c p) n -> p c n", p=128)
                    [:, :, ocg * 256:(ocg + 1) * 256])
                pump(1)
                for oc2 in range(2):
                    oc = 2 * ocg + oc2
                    FE("tensor", "sGw%d" % ws)
                    FE("tensor", "sGb")
                    W("tensor", "sDV", pB_free[oc2])
                    for tb in range(2):
                        for kc in range(16):
                            E("tensor", lambda e, oc2=oc2, tb=tb, kc=kc, ws=ws:
                              e.matmul(pB[oc2][:, tb * 512:(tb + 1) * 512],
                                       gu_sb[ws][:, kc * 256 + oc2 * 128:
                                                 kc * 256 + oc2 * 128 + 128],
                                       xnT[:, kc * TC + tb * 512:
                                           kc * TC + (tb + 1) * 512],
                                       start=(kc == 0), stop=False))
                        E("tensor", lambda e, oc2=oc2, tb=tb, oc=oc, bs=bias_slot:
                          e.matmul(pB[oc2][:, tb * 512:(tb + 1) * 512],
                                   brow[0:1, oc * 128: oc * 128 + 128],
                                   onesb[0:1, 0:512], start=False, stop=True))
                    t_pe = T("tensor", "sPE")
                    W("vector", "sPE", t_pe)
                    FE("vector", "sAq%d" % oc2)
                    E("vector", lambda e, oc2=oc2: e.tensor_copy(
                        qe[:, oc2 * 1024:(oc2 + 1) * 1024], pB[oc2][:, :]))
                    t_ev = T("vector", "sDV")
                    pB_free[oc2] = t_ev
                    W("scalar", "sDV", t_ev)
                    D("scalar", "sAq%d" % oc2,
                      dst_d[:, oc * 1024:(oc + 1) * 1024],
                      qe[:, oc2 * 1024:(oc2 + 1) * 1024])
                    wslot_free[ws] = t_pe

        bias_tok[0] = load_bias("q")
        emit_fm(wq_in, 0, qT_d)
        bias_tok[0] = load_bias("k", wait_pe=P.ctr["sPE"].v)
        emit_fm(wk_in, 0, kT_d)

        tm_wfree = [[tok_p1_pe] * 16, [P.ctr["sPE"].v] * 16]
        tm_par = [0]
        D("gpsimd", "sGb", brow[32:33, 0:H], b_ins["ak"])
        bias_tok[0] = D("gpsimd", "sGb", brow[64:65, 0:H], b_ins["v"])

        def emit_tm(w_ap, boff_main, boff_alien, main_dst, alien_dst,
                    av_reload=False):
            for dq in range(4):
                wp = tm_par[0]
                tm_par[0] ^= 1
                wbuf = wbig if wp == 0 else gu_sb[1][:, 0:8192]
                for kc in range(16):
                    W("gpsimd", "sPE", tm_wfree[wp][kc])
                    D("gpsimd", "sGw%d" % wp,
                      wbuf[:, kc * 512:(kc + 1) * 512],
                      w_ap.rearrange("(c p) n -> p c n", p=128)
                      [:, kc, dq * 512:(dq + 1) * 512])
                    if kc % 4 == 3:
                        pump(1)
                if av_reload and dq == 0:
                    W("gpsimd", "sPE", P.ctr["sPE"].v)
                    bias_tok[0] = D("gpsimd", "sGb", brow[32:33, 0:H],
                                    b_ins["av"])
                for srcT, boff, dst_d in ((xnT, boff_main, main_dst),
                                          (xanT, boff_alien, alien_dst)):
                    for tsub in range(8):
                        o2 = tsub % 2
                        reg = pB[o2][:, 0:512]
                        W("tensor", "sDV", pB_free[o2])
                        FE("tensor", "sGb")
                        FE("tensor", "sGw%d" % wp)
                        for kc in range(16):
                            E("tensor", lambda e, reg=reg, kc=kc, tsub=tsub,
                              srcT=srcT, wbuf=wbuf:
                              e.matmul(reg,
                                       srcT[:, kc * TC + tsub * 128:
                                            kc * TC + tsub * 128 + 128],
                                       wbuf[:, kc * 512:(kc + 1) * 512],
                                       start=(kc == 0), stop=False))
                        E("tensor", lambda e, reg=reg, dq=dq, boff=boff:
                          e.matmul(reg, onesb[boff:boff + 1, 0:128],
                                   brow[boff:boff + 1, dq * 512:(dq + 1) * 512],
                                   start=False, stop=True))
                        t_pe = T("tensor", "sPE")
                        W("vector", "sPE", t_pe)
                        FE("vector", "sAq%d" % o2)
                        E("vector", lambda e, o2=o2, reg=reg: e.tensor_copy(
                            qe[:, o2 * 1024: o2 * 1024 + 512], reg))
                        t_ev = T("vector", "sDV")
                        pB_free[o2] = t_ev
                        W("scalar", "sDV", t_ev)
                        D("scalar", "sAq%d" % o2,
                          dst_d[:, tsub * 2048 + dq * 512:
                                tsub * 2048 + (dq + 1) * 512],
                          qe[:, o2 * 1024: o2 * 1024 + 512])
                    pump(2)
                for kc in range(16):
                    tm_wfree[wp][kc] = P.ctr["sPE"].v

        emit_tm(wk_in, 0, 32, kt_d, akt_d)
        emit_tm(wv_in, 64, 32, v_d, av_d, av_reload=True)

        qT_v = qT_d.rearrange("p (c t) -> p c t", t=TC)
        kT_v = kT_d.rearrange("p (c t) -> p c t", t=TC)

        pA_free = [pB_free[0]] * 4
        upd_free[0] = pB_free[0]
        scan_pe_free = P.ctr["sPE"].v
        scan_dve_free = P.ctr["sDV"].v
        last_upd = P.ctr["sDV"].v

        for phase, src_kt, src_v in (("a", akt_d, av_d), ("m", kt_d, v_d)):
            for c in range(NCH):
                FE("sync", "sAq0")
                FE("sync", "sAq1")
                W("sync", "sPE", scan_pe_free)
                W("sync", "sDV", last_upd)
                t_kt = D("sync", "sLx", wbig[:, 0:4096],
                         src_kt[:, (2 * c) * 2048:(2 * c + 2) * 2048])
                t_v = D("sync", "sLx", wbig[:, 4096:8192],
                        src_v[:, (2 * c) * 2048:(2 * c + 2) * 2048])
                t_q = t_k2 = None
                if phase == "m":
                    t_q = D("sync", "sLx", gu_sb[0][:, 0:4096].rearrange(
                        "p (k t) -> p k t", t=256),
                        qT_v[:, :, c * 256:(c + 1) * 256])
                    t_k2 = D("sync", "sLx", gu_sb[0][:, 4096:8192].rearrange(
                        "p (k t) -> p k t", t=256),
                        kT_v[:, :, c * 256:(c + 1) * 256])
                pump(1)
                for h in range(HEADS):
                    hp = h % 2
                    cdh = float(CD[h])
                    if phase == "m":
                        W("vector", "sDV", last_upd)
                        E("vector", lambda e, h=h: e.tensor_copy(
                            Sbf[:, :], Ssb[:, h * 512:(h + 1) * 512]))
                        t_sbf = T("vector", "sDV")
                        W("scalar", "sDV", mask_free[0])
                        for jc in range(2):
                            E("scalar", lambda e, h=h, jc=jc: e.activation(
                                dtt[:, jc * 256:(jc + 1) * 256],
                                dmsk[:, jc * 256:(jc + 1) * 256], AF.Exp,
                                scale=float(LNG[h])))
                        t_dt = T("scalar", "sAC")
                        FE("tensor", "sLx")
                        W("tensor", "sDV", pA_free[hp])
                        for jc in range(2):
                            for dc in range(2):
                                E("tensor", lambda e, h=h, jc=jc, dc=dc, hp=hp:
                                  e.matmul(pA[hp][:, jc * 256:(jc + 1) * 256],
                                           gu_sb[0][:, 4096 + (2 * h + dc) * 256 +
                                                 jc * 128:
                                                 4096 + (2 * h + dc) * 256 +
                                                 jc * 128 + 128],
                                           gu_sb[0][:, (2 * h + dc) * 256:
                                                 (2 * h + dc) * 256 + 256],
                                           start=(dc == 0), stop=(dc == 1)))
                        t_at = T("tensor", "sPE")
                        W("vector", "sPE", t_at)
                        W("vector", "sPE", attn_free[0])
                        W("vector", "sAC", t_dt)
                        for jc in range(2):
                            E("vector", lambda e, jc=jc, hp=hp: e.tensor_tensor(
                                attnM[:, jc * 256:(jc + 1) * 256],
                                pA[hp][:, jc * 256:(jc + 1) * 256],
                                dtt[:, jc * 256:(jc + 1) * 256], ALU.mult))
                            E("vector", lambda e, h=h, jc=jc: e.tensor_tensor(
                                qs_sb[:, jc * 256:(jc + 1) * 256],
                                gu_sb[0][:, (2 * h + jc) * 256:
                                      (2 * h + jc) * 256 + 256],
                                csb[:, h * 256:(h + 1) * 256], ALU.mult))
                        t_mq = T("vector", "sDV")
                        pA_free[hp] = t_mq
                        mask_free[0] = t_mq
                        W("tensor", "sDV", t_mq)
                        W("tensor", "sDV", t_sbf)
                        W("tensor", "sDV", pA_free[2 + hp])
                        for ih in range(2):
                            for jc in range(2):
                                E("tensor", lambda e, ih=ih, jc=jc, h=h, hp=hp:
                                  e.matmul(pA[2 + hp][:, ih * 256:(ih + 1) * 256],
                                           attnM[:, jc * 256 + ih * 128:
                                                 jc * 256 + ih * 128 + 128],
                                           wbig[:, 4096 + jc * 2048 + h * 256:
                                                4096 + jc * 2048 + h * 256 + 256],
                                           start=(jc == 0), stop=False))
                            for dc in range(2):
                                E("tensor", lambda e, ih=ih, dc=dc, hp=hp:
                                  e.matmul(pA[2 + hp][:, ih * 256:(ih + 1) * 256],
                                           qs_sb[:, dc * 256 + ih * 128:
                                                 dc * 256 + ih * 128 + 128],
                                           Sbf[:, dc * 256:(dc + 1) * 256],
                                           start=False, stop=(dc == 1)))
                        t_out = T("tensor", "sPE")
                        attn_free[0] = t_out
                        W("vector", "sPE", t_out)
                        W("vector", "sAs", s_spill_tok[0])
                        for ih in range(2):
                            E("vector", lambda e, ih=ih, h=h, hp=hp:
                              e.tensor_copy(
                                  scr[:, ih * 2048 + h * 256:
                                      ih * 2048 + h * 256 + 256],
                                  pA[2 + hp][:, ih * 256:(ih + 1) * 256]))
                        t_sev = T("vector", "sDV")
                        pA_free[2 + hp] = t_sev
                    FE("vector", "sLx")
                    W("vector", "sPE", ks_free[0])
                    for tsl in range(2):
                        E("vector", lambda e: e.nop())
                        E("vector", lambda e, tsl=tsl, h=h: e.tensor_scalar(
                            ksb[:, tsl * 256:(tsl + 1) * 256],
                            wbig[:, tsl * 2048 + h * 256:
                                 tsl * 2048 + h * 256 + 256],
                            kvc[:, (tsl * 8 + h):(tsl * 8 + h) + 1], None,
                            ALU.mult))
                    t_ks = T("vector", "sDV")
                    W("tensor", "sDV", t_ks)
                    W("tensor", "sDV", upd_free[0])
                    FE("tensor", "sLx")
                    for dh in range(2):
                        for tsl in range(2):
                            E("tensor", lambda e, dh=dh, tsl=tsl, h=h:
                              e.matmul(p_upd[:, dh * 256:(dh + 1) * 256],
                                       ksb[:, tsl * 256 + dh * 128:
                                           tsl * 256 + dh * 128 + 128],
                                       wbig[:, 4096 + tsl * 2048 + h * 256:
                                            4096 + tsl * 2048 + h * 256 + 256],
                                       start=(tsl == 0), stop=(tsl == 1)))
                    t_upd = T("tensor", "sPE")
                    ks_free[0] = t_upd
                    W("vector", "sPE", t_upd)
                    for dh in range(2):
                        E("vector", lambda e: e.nop())
                        E("vector", lambda e, dh=dh, h=h, cdh=cdh:
                          e.scalar_tensor_tensor(
                              Ssb[:, h * 512 + dh * 256:
                                  h * 512 + dh * 256 + 256],
                              Ssb[:, h * 512 + dh * 256:
                                  h * 512 + dh * 256 + 256],
                              cdh, p_upd[:, dh * 256:(dh + 1) * 256],
                              ALU.mult, ALU.add))
                    last_upd = T("vector", "sDV")
                    upd_free[0] = last_upd
                if phase == "m":
                    W("scalar", "sDV", last_upd)
                    s_spill_tok[0] = D("scalar", "sAs",
                                       s_d[:, (2 * c) * 2048:(2 * c + 2) * 2048],
                                       scr[:, 0:4096])
                scan_pe_free = P.ctr["sPE"].v
                pump(2)

        Gg_v = Gg.rearrange("(c p) n -> p c n", p=128)
        Gu_v = Gu.rearrange("(c p) n -> p c n", p=128)
        chain_tok = {}
        gu_ld_free = [P.ctr["sPE"].v, P.ctr["sPE"].v]
        pGU_free = [P.ctr["sDV"].v] * 4
        for i in range(4):
            fls_dve[i] = P.ctr["sDV"].v
        gelu_free[0] = gelu_free[1] = P.ctr["sDV"].v
        wd_free = P.ctr["sPE"].v
        gu_tok = (0, 0)

        for g in range(8):
            assert len(stage_tok[("g", g)]) == 16, (g, len(stage_tok[("g", g)]))
            assert len(stage_tok[("u", g)]) == 16
            tok_gu_staged = max(stage_tok[("g", g)][-1], stage_tok[("u", g)][-1])
            for i in range(16):
                pump(2)
                sl = (i // 2) % 2
                if i % 2 == 0:
                    W("sync", "sAs", tok_gu_staged)
                    W("sync", "sPE", gu_ld_free[sl])
                    ncp0 = (16 * g + i) * 128
                    t_gg = D("sync", "sLg%d" % sl, gu_sb[sl][:, 0:16 * 256].rearrange(
                        "p (c n) -> p c n", n=256), Gg_v[:, :, ncp0:ncp0 + 256])
                    t_gu = D("sync", "sLg%d" % sl,
                             gu_sb[sl][:, 16 * 256:32 * 256].rearrange(
                                 "p (c n) -> p c n", n=256),
                             Gu_v[:, :, ncp0:ncp0 + 256])
                    gu_tok = (t_gg, t_gu)
                io = i % 2
                for th in range(2):
                    FE("tensor", "sLg%d" % sl)
                    W("tensor", "sDV", pGU_free[th])
                    W("tensor", "sDV", pGU_free[2 + th])
                    for kc in range(16):
                        E("tensor", lambda e, sl=sl, kc=kc, io=io, th=th:
                          e.matmul(pA[th][:, 0:512],
                                   gu_sb[sl][:, kc * 256 + io * 128:
                                             kc * 256 + io * 128 + 128],
                                   xnT[:, kc * TC + th * 512:
                                       kc * TC + (th + 1) * 512],
                                   start=(kc == 0), stop=(kc == 15)))
                    t_g = T("tensor", "sPE")
                    for kc in range(16):
                        E("tensor", lambda e, sl=sl, kc=kc, io=io, th=th:
                          e.matmul(pA[2 + th][:, 0:512],
                                   gu_sb[sl][:, 16 * 256 + kc * 256 + io * 128:
                                             16 * 256 + kc * 256 + io * 128 + 128],
                                   xnT[:, kc * TC + th * 512:
                                       kc * TC + (th + 1) * 512],
                                   start=(kc == 0), stop=(kc == 15)))
                    t_u = T("tensor", "sPE")
                    W("scalar", "sPE", t_g)
                    W("scalar", "sDV", gelu_free[th])
                    E("scalar", lambda e, th=th: e.activation(
                        scr2[:, 1024 + th * 512: 1024 + (th + 1) * 512],
                        pA[th][:, 0:512], AF.Gelu_apprx_tanh))
                    t_gelu = T("scalar", "sAC")
                    W("vector", "sAC", t_gelu)
                    W("vector", "sPE", t_u)
                    E("vector", lambda e, i=i, th=th: e.tensor_tensor(
                        itr[:, i * 1024 + th * 512: i * 1024 + (th + 1) * 512],
                        scr2[:, 1024 + th * 512: 1024 + (th + 1) * 512],
                        pA[2 + th][:, 0:512], ALU.mult))
                    t_ch = T("vector", "sDV")
                    chain_tok[(i, th)] = t_ch
                    gelu_free[th] = t_ch
                    pGU_free[th] = t_ch
                    pGU_free[2 + th] = t_ch
                gu_ld_free[sl] = P.ctr["sPE"].v
            assert len(stage_tok[("d", g)]) == 16
            tok_d_staged = stage_tok[("d", g)][-1]
            for q in range(4):
                pump(2)
                W("sync", "sPE", wd_free)
                W("sync", "sAs", tok_d_staged)
                for i in range(16):
                    r0 = (16 * g + i) * 128
                    D("sync", "sLw", wbig[:, i * 512:(i + 1) * 512],
                      Gd[r0:r0 + 128, q * 512:(q + 1) * 512])
                for ts_ in range(8):
                    o4 = ts_ % 4
                    reg = pB[o4 // 2][:, (o4 % 2) * 512:(o4 % 2) * 512 + 512]
                    W("tensor", "sDV", fls_dve[o4])
                    if ts_ == 0:
                        FE("tensor", "sLw")
                    for i in range(16):
                        if q == 0 and ts_ == 0:
                            W("tensor", "sDV", chain_tok[(i, 0)])
                            W("tensor", "sDV", chain_tok[(i, 1)])
                        E("tensor", lambda e, reg=reg, i=i, ts_=ts_:
                          e.matmul(reg,
                                   itr[:, i * 1024 + ts_ * 128:
                                       i * 1024 + ts_ * 128 + 128],
                                   wbig[:, i * 512:(i + 1) * 512],
                                   start=(i == 0), stop=(i == 15)))
                    t_dn = T("tensor", "sPE")
                    W("vector", "sPE", t_dn)
                    o2 = ts_ % 2
                    FE("vector", "sGa%d" % o2)
                    E("vector", lambda e, o2=o2, reg=reg: e.tensor_copy(
                        fls[:, o2 * 512:(o2 + 1) * 512], reg))
                    t_fl = T("vector", "sDV")
                    fls_dve[o4] = t_fl
                    W("gpsimd", "sDV", t_fl)
                    kw = {} if g == 0 else {"accum_op": ALU.add}
                    D("gpsimd", "sGa%d" % o2,
                      acc_d[:, ts_ * 2048 + q * 512:
                            ts_ * 2048 + q * 512 + 512],
                      fls[:, o2 * 512:(o2 + 1) * 512], **kw)
                wd_free = P.ctr["sPE"].v

        W("sync", "sDV", P.ctr["sDV"].v)
        FE("sync", "sGa0")
        FE("sync", "sGa1")
        FE("sync", "sAs")
        cmb_free = 0
        last_store = 0
        for ts_ in range(8):
            W("sync", "sDV", cmb_free)
            FE("sync", "sLo")
            t_x = D("sync", "sLx", scr[:, 0:2048],
                    x_in[ts_ * 128:(ts_ + 1) * 128, :])
            t_s = D("sync", "sLx", scr[:, 2048:4096],
                    s_d[:, ts_ * 2048:(ts_ + 1) * 2048])
            atile = scr2[:, 0:2048]
            t_a = D("sync", "sLx", atile, acc_d[:, ts_ * 2048:(ts_ + 1) * 2048])
            FE("scalar", "sLx")
            W("scalar", "sDV", cmb_free)
            E("scalar", lambda e, a=atile: e.activation(a, a, AF.Silu))
            t_sl = T("scalar", "sAC")
            W("vector", "sAC", t_sl)
            FE("vector", "sLx")
            E("vector", lambda e, a=atile: e.tensor_tensor(
                scr[:, 2048:4096], scr[:, 2048:4096], a, ALU.mult))
            t_sg = T("vector", "sDV")
            W("vector", "sDV", t_sg)
            E("vector", lambda e: e.tensor_tensor(
                scr[:, 0:2048], scr[:, 0:2048], scr[:, 2048:4096], ALU.add))
            t_cb = T("vector", "sDV")
            cmb_free = t_cb
            W("sync", "sDV", t_cb)
            last_store = D("sync", "sLo", out_t[ts_ * 128:(ts_ + 1) * 128, :],
                           scr[:, 0:2048])

        dbg_reg = {"xnT_t": None, "qT_d": qT_d, "kT_d": kT_d, "kt_d": kt_d,
                   "v_d": v_d, "akt_d": akt_d, "av_d": av_d, "s_d": s_d,
                   "acc_d": acc_d, "Gg": Gg, "Gu": Gu, "Gd": Gd,
                   "gsh_d": gsh_d, "dsh_d": dsh_d}
        if dbg:
            FE("sync", "sAs")
            FE("sync", "sGa0")
            FE("sync", "sGa1")
            W("sync", "sDV", P.ctr["sDV"].v)
            for name in dbg:
                if name in ("xnT_e", "xnbf0"):
                    continue
                if name == "xnT_t":
                    o = nc.dram_tensor("dbg_xnT", [128, 16 * TC], BF16,
                                       kind="ExternalOutput").ap()
                    last_store = D("sync", "sLo", o, xnT[:, :])
                    continue
                src = dbg_reg[name]
                o = nc.dram_tensor("dbg_" + name, list(src.shape),
                                   src.dtype, kind="ExternalOutput").ap()
                last_store = D("sync", "sLo", o, src)
            FE("sync", "sLo")

        FE("sync", "sLo")
        FE("gpsimd", "sLo")

        assert dec_pos[0] == len(dec_units), (dec_pos[0], len(dec_units))

        @block.tensor
        def _(e):
            for f in P.ops["tensor"]:
                f(e)

        @block.vector
        def _(e):
            for f in P.ops["vector"]:
                f(e)

        @block.scalar
        def _(e):
            for f in P.ops["scalar"]:
                f(e)

        @block.gpsimd
        def _(e):
            for f in P.ops["gpsimd"]:
                f(e)

        @block.sync
        def _(e):
            for f in P.ops["sync"]:
                f(e)

    return nc


def _host_prepack(w_int32):
    b = np.ascontiguousarray(w_int32).view(np.uint8)
    b = b.reshape(w_int32.shape[0], w_int32.shape[1], 4)[:, :, 0]
    bT = np.ascontiguousarray(b.T)
    n = bT.shape[1]
    il = np.stack([np.arange(n // 2), np.arange(n // 2) + n // 2], 1).reshape(-1)
    bTi = np.ascontiguousarray(bT[:, il])
    return bTi.view(np.int16)


_CACHE = {}


def kernel(L0_attn_x, L0_attn_Wq, L0_attn_bq, L0_attn_Wk, L0_attn_bk,
           L0_attn_Wv, L0_attn_bv, L0_ffn_gate_W_packed, L0_ffn_gate_scales,
           L0_ffn_up_W_packed, L0_ffn_up_scales, L0_ffn_down_W_packed,
           L0_ffn_down_scales):
    x = np.ascontiguousarray(np.asarray(L0_attn_x, dtype=np.float32)).reshape(8192, H)
    wq = np.ascontiguousarray(np.asarray(L0_attn_Wq, dtype=np.float32))
    wk = np.ascontiguousarray(np.asarray(L0_attn_Wk, dtype=np.float32))
    wv = np.ascontiguousarray(np.asarray(L0_attn_Wv, dtype=np.float32))
    bq = np.ascontiguousarray(np.asarray(L0_attn_bq, dtype=np.float32)).reshape(1, H)
    bk = np.ascontiguousarray(np.asarray(L0_attn_bk, dtype=np.float32)).reshape(1, H)
    bv = np.ascontiguousarray(np.asarray(L0_attn_bv, dtype=np.float32)).reshape(1, H)
    gp = _host_prepack(np.ascontiguousarray(np.asarray(L0_ffn_gate_W_packed, dtype=np.int32)))
    up = _host_prepack(np.ascontiguousarray(np.asarray(L0_ffn_up_W_packed, dtype=np.int32)))
    dp = _host_prepack(np.ascontiguousarray(np.asarray(L0_ffn_down_W_packed, dtype=np.int32)))
    gs = np.ascontiguousarray(np.asarray(L0_ffn_gate_scales, dtype=np.float32).T)
    us = np.ascontiguousarray(np.asarray(L0_ffn_up_scales, dtype=np.float32).T)
    ds = np.ascontiguousarray(np.asarray(L0_ffn_down_scales, dtype=np.float32).T)

    z_x = np.zeros((TC, H), np.float32)
    z_b = np.zeros((1, H), np.float32)

    key = tuple(DBG)
    if key not in _CACHE:
        _CACHE[key] = build(dbg=key)
    nc = _CACHE[key]

    in_maps = []
    for r in range(NCORES):
        odd = r % 2 == 1
        in_maps.append({
            "x": x[r * TC:(r + 1) * TC],
            "xa": x[(r - 1) * TC: r * TC] if odd else z_x,
            "wq": wq, "wk": wk, "wv": wv,
            "bq": bq, "bk": bk, "bv": bv,
            "bak": bk if odd else z_b, "bav": bv if odd else z_b,
            "gatep": gp, "upp": up, "downp": dp,
            "gates": gs, "ups": us, "downs": ds,
        })

    res = run_bass_kernel_spmd(nc, in_maps, core_ids=list(range(NCORES)))
    _LAST.clear()
    _LAST.update({"results": res.results})
    out = np.concatenate([res.results[r]["out"] for r in range(NCORES)], axis=0)
    return out.reshape(4, 2048, H)


if __name__ == "__main__":
    nc = build()
    n = sum(1 for _ in nc.all_instructions())
    print("built ok; instructions:", n)
